# version 05
# baseline (speedup 1.0000x reference)
import numpy as np
import ml_dtypes
import concourse.bass as bass
import concourse.mybir as mybir
from concourse.bass import AP
from concourse.bass_utils import run_bass_kernel_spmd

F32 = mybir.dt.float32
BF16 = mybir.dt.bfloat16
AF = mybir.ActivationFunctionType
ALU = mybir.AluOpType
AX = mybir.AxisListType.X

B, T, DM = 2, 8192, 1024
NTILE = T // 128
PTILES = 16
TOK = PTILES * 128
KD = 64
NE = 8
DIN = 2048
CH = 512
DST = 16
DTR = 64
FF = 4096
NEG = -1e30

BF = ml_dtypes.bfloat16


def zig_tiles(j):
    out = []
    for m in range(8):
        out.append(8 * m + j)
        out.append(8 * m + (7 - j))
    return sorted(out)


def att_slots():
    slots = []
    for m in range(8):
        slots.append(("lo", m, 8 * m + 8))
        slots.append(("hi", m, 8 * m + 8))
    return slots


def build_ident(nc, gp, vec, ident, iot, sem):
    gp.iota(iot[:], pattern=[[1, 128]], base=0, channel_multiplier=-1,
            allow_small_or_imprecise_dtypes=True).then_inc(sem, 1)
    vec.wait_ge(sem, 1)
    vec.tensor_scalar(ident[:], iot[:], 0.0, None, ALU.is_equal).then_inc(sem, 1)



def build_prep():
    nc = bass.Bass("TRN2", target_bir_lowering=False)
    d_x = nc.dram_tensor("x", [TOK, DM], F32, kind="ExternalInput").ap()
    d_wq = nc.dram_tensor("wq", [DM, KD], F32, kind="ExternalInput").ap()
    d_wk = nc.dram_tensor("wk", [DM, KD], F32, kind="ExternalInput").ap()
    d_wv1 = nc.dram_tensor("wv1", [DM, KD], F32, kind="ExternalInput").ap()
    d_wv2 = nc.dram_tensor("wv2", [KD, DM], F32, kind="ExternalInput").ap()
    o_xnfm = nc.dram_tensor("xn_fm", [DM, TOK], BF16, kind="ExternalOutput").ap()
    o_q = nc.dram_tensor("q_fm", [KD, TOK], BF16, kind="ExternalOutput").ap()
    o_k = nc.dram_tensor("k_fm", [KD, TOK], BF16, kind="ExternalOutput").ap()
    o_v = nc.dram_tensor("v_tm", [TOK, DM], BF16, kind="ExternalOutput").ap()

    xt = [nc.alloc_sbuf_tensor(f"xt{i}", [128, DM], F32) for i in range(2)]
    sq = nc.alloc_sbuf_tensor("sq", [128, DM], F32)
    sq2 = nc.alloc_sbuf_tensor("sq2", [128, DM], F32)
    cols = nc.alloc_sbuf_tensor("cols", [128, 16], F32)
    xn_tm = [nc.alloc_sbuf_tensor(f"xntm{i}", [128, DM], BF16) for i in range(2)]
    xnfm = nc.alloc_sbuf_tensor("xnfm", [128, 8 * TOK], BF16)
    wq_sb = nc.alloc_sbuf_tensor("wq_sb", [128, 8 * KD], BF16)
    wk_sb = nc.alloc_sbuf_tensor("wk_sb", [128, 8 * KD], BF16)
    wv1_sb = nc.alloc_sbuf_tensor("wv1_sb", [128, 8 * KD], BF16)
    wv2_sb = nc.alloc_sbuf_tensor("wv2_sb", [KD, DM], BF16)
    q_sb = nc.alloc_sbuf_tensor("q_sb", [KD, TOK], BF16)
    k_sb = nc.alloc_sbuf_tensor("k_sb", [KD, TOK], BF16)
    v1_sb = nc.alloc_sbuf_tensor("v1_sb", [KD, TOK], BF16)
    vst = [nc.alloc_sbuf_tensor(f"vst{i}", [128, DM], BF16) for i in range(2)]
    ident = nc.alloc_sbuf_tensor("ident", [128, 128], BF16)
    iot = nc.alloc_sbuf_tensor("iot", [128, 128], F32)

    p_tr = [nc.alloc_psum_tensor(f"p_tr{i}", [128, 128], BF16) for i in range(2)]
    p_q = nc.alloc_psum_tensor("p_q", [KD, 512], F32)
    p_k = nc.alloc_psum_tensor("p_k", [KD, 512], F32)
    p_v1 = nc.alloc_psum_tensor("p_v1", [KD, 512], F32)
    p_v = [nc.alloc_psum_tensor(f"p_vv{i}", [128, 512], F32) for i in range(2)]

    s_xld = [nc.alloc_semaphore(f"s_xld{i}") for i in range(2)]
    s_wld = nc.alloc_semaphore("s_wld")
    s_id = nc.alloc_semaphore("s_id")
    s_v2s = nc.alloc_semaphore("s_v2s")
    s_s2v = nc.alloc_semaphore("s_s2v")
    s_xn = nc.alloc_semaphore("s_xn")
    s_tr = nc.alloc_semaphore("s_tr")
    s_ev = nc.alloc_semaphore("s_ev")
    s_pj = nc.alloc_semaphore("s_pj")
    s_pe = nc.alloc_semaphore("s_pe")
    s_st = nc.alloc_semaphore("s_st")
    s_sq = nc.alloc_semaphore("s_sq")
    s_vst = [nc.alloc_semaphore(f"s_vst{i}") for i in range(2)]
    s_vself = nc.alloc_semaphore("s_vself")
    vctr = [0]

    def vss(vector, inst):
        vctr[0] += 1
        inst.then_inc(s_vself, 1)
        vector.wait_ge(s_vself, vctr[0])

    def evac_tile(scalar, i):
        for c in range(8):
            n = 8 * i + c
            scalar.wait_ge(s_tr, n + 1)
            scalar.copy(xnfm[:, TOK * c + 128 * i: TOK * c + 128 * (i + 1)],
                        p_tr[n % 2][:]).then_inc(s_ev, 1)

    with nc.Block() as block:
        @block.sync
        def _(sync):
            for i in range(PTILES):
                if i >= 2:
                    sync.wait_ge(s_xn, i - 1)
                    sync.wait_ge(s_sq, i - 1)
                sync.dma_start(out=xt[i % 2][:],
                               in_=d_x[128 * i:128 * (i + 1), :]).then_inc(s_xld[i % 2], 16)
            sync.wait_ge(s_ev, 128)
            for c in range(8):
                sync.dma_start(out=o_xnfm[128 * c:128 * (c + 1), :],
                               in_=xnfm[:, TOK * c:TOK * (c + 1)]).then_inc(s_st, 16)
            sync.wait_ge(s_pe, 12)
            sync.dma_start(out=o_q, in_=q_sb[:]).then_inc(s_st, 16)
            sync.dma_start(out=o_k, in_=k_sb[:]).then_inc(s_st, 16)

        @block.gpsimd
        def _(gpsimd):
            for (dst, srcw) in ((wq_sb, d_wq), (wk_sb, d_wk), (wv1_sb, d_wv1)):
                gpsimd.dma_start(
                    out=dst.ap().rearrange("p (c n) -> p c n", n=KD),
                    in_=srcw.rearrange("(c p) n -> p c n", p=128),
                ).then_inc(s_wld, 16)
            gpsimd.dma_start(out=wv2_sb[:], in_=d_wv2).then_inc(s_wld, 16)
            gpsimd.iota(iot[:], pattern=[[1, 128]], base=0, channel_multiplier=-1,
                        allow_small_or_imprecise_dtypes=True).then_inc(s_id, 1)
            for i in range(PTILES):
                gpsimd.wait_ge(s_pe, 12 + i + 1)
                gpsimd.dma_start(out=o_v[128 * i:128 * (i + 1), :],
                                 in_=vst[i % 2][:]).then_inc(s_vst[i % 2], 16)
            gpsimd.wait_ge(s_st, 16 * 10)
            gpsimd.wait_ge(s_vst[0], 16 * (PTILES // 2))
            gpsimd.wait_ge(s_vst[1], 16 * (PTILES // 2))

        @block.vector
        def _(vector):
            vector.wait_ge(s_id, 1)
            vector.tensor_scalar(ident[:], iot[:], 0.0, None, ALU.is_equal).then_inc(s_id, 1)
            for i in range(PTILES):
                t = xt[i % 2]
                b = 8 * (i % 2)
                vector.wait_ge(s_xld[i % 2], 16 * (i // 2 + 1))
                vss(vector, vector.reduce_sum(cols[:, b + 0:b + 1], t[:], AX))
                vector.tensor_tensor(sq2[:], t[:], t[:], ALU.mult)
                vss(vector, vector.reduce_sum(cols[:, b + 1:b + 2], sq2[:], AX))
                vss(vector, vector.tensor_scalar(cols[:, b + 2:b + 3], cols[:, b + 0:b + 1], 1.0 / DM, None, ALU.mult))
                vss(vector, vector.tensor_scalar(cols[:, b + 3:b + 4], cols[:, b + 1:b + 2], 1.0 / DM, None, ALU.mult))
                vss(vector, vector.tensor_tensor(cols[:, b + 4:b + 5], cols[:, b + 2:b + 3], cols[:, b + 2:b + 3], ALU.mult))
                vss(vector, vector.tensor_tensor(cols[:, b + 5:b + 6], cols[:, b + 3:b + 4], cols[:, b + 4:b + 5], ALU.subtract))
                vector.tensor_scalar(cols[:, b + 5:b + 6], cols[:, b + 5:b + 6], 1e-5, None, ALU.add).then_inc(s_v2s, 1)
                vector.wait_ge(s_s2v, i + 1)
                vss(vector, vector.reciprocal(cols[:, b + 7:b + 8], cols[:, b + 6:b + 7]))
                if i >= 2:
                    vector.wait_ge(s_tr, 8 * (i - 1))
                vector.tensor_scalar(xn_tm[i % 2][:], t[:], cols[:, b + 2:b + 3], cols[:, b + 7:b + 8],
                                     ALU.subtract, ALU.mult).then_inc(s_xn, 1)

        @block.scalar
        def _(scalar):
            for i in range(PTILES):
                b = 8 * (i % 2)
                scalar.nop().then_inc(s_sq, 1)
                scalar.wait_ge(s_v2s, i + 1)
                scalar.activation(cols[:, b + 6:b + 7], cols[:, b + 5:b + 6], AF.Sqrt).then_inc(s_s2v, 1)
                if i >= 1:
                    evac_tile(scalar, i - 1)
            evac_tile(scalar, PTILES - 1)
            for jj in range(4):
                for idx, (ps, dst) in enumerate(((p_q, q_sb), (p_k, k_sb), (p_v1, v1_sb))):
                    scalar.wait_ge(s_pj, 3 * jj + idx + 1)
                    scalar.copy(dst.ap()[:, 512 * jj:512 * (jj + 1)], ps[:]).then_inc(s_pe, 1)
            for i in range(PTILES):
                scalar.wait_ge(s_pj, 12 + 2 * (i + 1))
                if i >= 2:
                    scalar.wait_ge(s_vst[i % 2], 16 * ((i - 2) // 2 + 1))
                scalar.copy(vst[i % 2][:, 0:512], p_v[0][:])
                scalar.copy(vst[i % 2][:, 512:1024], p_v[1][:]).then_inc(s_pe, 1)

        @block.tensor
        def _(tensor):
            tensor.wait_ge(s_id, 2)
            for i in range(PTILES):
                tensor.wait_ge(s_xn, i + 1)
                for c in range(8):
                    n = 8 * i + c
                    if n >= 2:
                        tensor.wait_ge(s_ev, n - 1)
                    tensor.transpose(p_tr[n % 2][:], xn_tm[i % 2][:, 128 * c:128 * (c + 1)],
                                     ident[:]).then_inc(s_tr, 1)
            tensor.wait_ge(s_wld, 16 * 4)
            tensor.wait_ge(s_ev, 128)
            for jj in range(4):
                for idx, (wsb, ps) in enumerate(((wq_sb, p_q), (wk_sb, p_k), (wv1_sb, p_v1))):
                    if jj >= 1:
                        tensor.wait_ge(s_pe, 3 * (jj - 1) + idx + 1)
                    for c in range(8):
                        mm = tensor.matmul(
                            ps[:], wsb[:, KD * c:KD * (c + 1)],
                            xnfm[:, TOK * c + 512 * jj:TOK * c + 512 * (jj + 1)],
                            start=(c == 0), stop=(c == 7))
                    mm.then_inc(s_pj, 1)
            for i in range(PTILES):
                tensor.wait_ge(s_pe, 12)
                if i >= 1:
                    tensor.wait_ge(s_pe, 12 + i)
                for u in range(2):
                    tensor.matmul(p_v[u][:], v1_sb[:, 128 * i:128 * (i + 1)],
                                  wv2_sb[:, 512 * u:512 * (u + 1)],
                                  start=True, stop=True).then_inc(s_pj, 1)

    return nc



VRES = 32


def att_slot_list(j):
    out = []
    for m in range(8):
        for t in (8 * m + j, 8 * m + (7 - j)):
            out.append((t, 8 * m + 8))
    out.sort()
    return out


def _att_sched():
    slots = att_slot_list(0)
    sched = []
    gchunk = gpb = gblk = gstream = 0
    stream_gblk = {}
    for si, (tau, kvt) in enumerate(slots):
        m8 = kvt - 8
        nA = (m8 * 128) // 512
        chunks = []
        for ch in range(nA + 2):
            isB = ch >= nA
            blks = []
            for b in range(4):
                v = ch * 4 + b
                sidx = None
                if v >= VRES:
                    sidx = gstream
                    stream_gblk[gstream] = gblk
                    gstream += 1
                blks.append(dict(v=v, gblk=gblk, stream=sidx))
                gblk += 1
            chunks.append(dict(ch=ch, isB=isB, base=ch * 512, gchunk=gchunk,
                               gpb=(gpb if isB else None), blks=blks))
            gchunk += 1
            if isB:
                gpb += 1
        sched.append(dict(si=si, tau=tau, kvt=kvt, m8=m8, nA=nA, chunks=chunks))
    return sched, stream_gblk


def build_l0att():
    nc = bass.Bass("TRN2", target_bir_lowering=False)
    d_q = nc.dram_tensor("q_fm", [KD, TOK], BF16, kind="ExternalInput").ap()
    d_K = nc.dram_tensor("Kfull", [KD, T], BF16, kind="ExternalInput").ap()
    d_V = nc.dram_tensor("Vfull", [T, DM], BF16, kind="ExternalInput").ap()
    d_x = nc.dram_tensor("x", [TOK, DM], F32, kind="ExternalInput").ap()
    d_dpos = nc.dram_tensor("dpos", [1, 16], F32, kind="ExternalInput").ap()
    o_xa = nc.dram_tensor("xatt", [TOK, DM], F32, kind="ExternalOutput").ap()
    h_row = nc.dram_tensor("h_row", [1, 16], F32, kind="Internal").ap()

    q_sb = nc.alloc_sbuf_tensor("q_sb", [KD, TOK], BF16)
    K_sb = nc.alloc_sbuf_tensor("K_sb", [KD, T], BF16)
    Vres = nc.alloc_sbuf_tensor("Vres", [128, VRES * DM], BF16)
    Vst = [nc.alloc_sbuf_tensor(f"Vst{i}", [128, DM], BF16) for i in range(2)]
    k2 = nc.alloc_sbuf_tensor("k2", [KD, 512], BF16)
    onesc = nc.alloc_sbuf_tensor("onesc", [KD, 1], BF16)
    knorm2 = nc.alloc_sbuf_tensor("knorm2", [1, 64], F32)
    kprefix = nc.alloc_sbuf_tensor("kprefix", [1, 16], F32)
    dpos_bc = nc.alloc_sbuf_tensor("dpos_bc", [128, 16], F32)
    kmax_bc = nc.alloc_sbuf_tensor("kmax_bc", [128, 16], F32)
    p_col = nc.alloc_sbuf_tensor("p_col", [128, 1], F32)
    jiota = nc.alloc_sbuf_tensor("jiota", [128, 512], F32)
    ident = nc.alloc_sbuf_tensor("ident", [128, 128], BF16)
    iot = nc.alloc_sbuf_tensor("iot", [128, 128], F32)
    qt2 = nc.alloc_sbuf_tensor("qt2", [128, KD], F32)
    ncols = nc.alloc_sbuf_tensor("ncols", [128, 16], F32)
    sig = nc.alloc_sbuf_tensor("sig", [128, 32], F32)
    dmask = nc.alloc_sbuf_tensor("dmask", [128, 512], F32)
    P_sb = [nc.alloc_sbuf_tensor(f"P_sb{i}", [128, 512], BF16) for i in range(2)]
    PT_sb = [nc.alloc_sbuf_tensor(f"PT_sb{i}", [128, 128], BF16) for i in range(2)]
    o_sb = nc.alloc_sbuf_tensor("o_sb", [128, DM], F32)
    xt = [nc.alloc_sbuf_tensor(f"xt{i}", [128, DM], F32) for i in range(2)]
    xa = [nc.alloc_sbuf_tensor(f"xa{i}", [128, DM], F32) for i in range(2)]

    p_S = [nc.alloc_psum_tensor(f"p_S{i}", [128, 512], F32) for i in range(2)]
    p_PT = [nc.alloc_psum_tensor(f"p_PT{i}", [128, 128], BF16) for i in range(2)]
    p_o = [nc.alloc_psum_tensor(f"p_o{i}", [128, 512], F32) for i in range(2)]
    p_qt = nc.alloc_psum_tensor("p_qt", [128, KD], BF16)
    p_kn = nc.alloc_psum_tensor("p_kn", [1, 512], F32)

    s_ldq = nc.alloc_semaphore("s_ldq")
    s_ldK = nc.alloc_semaphore("s_ldK")
    s_lddp = nc.alloc_semaphore("s_lddp")
    s_ldV = nc.alloc_semaphore("s_ldV")
    s_idg = nc.alloc_semaphore("s_idg")
    s_idv = nc.alloc_semaphore("s_idv")
    s_kn1 = nc.alloc_semaphore("s_kn1")
    s_kn2 = nc.alloc_semaphore("s_kn2")
    s_kn3 = nc.alloc_semaphore("s_kn3")
    s_kn5 = nc.alloc_semaphore("s_kn5")
    s_bc1 = nc.alloc_semaphore("s_bc1")
    s_bc2 = nc.alloc_semaphore("s_bc2")
    s_qt = nc.alloc_semaphore("s_qt")
    s_qs = nc.alloc_semaphore("s_qs")
    s_nc = nc.alloc_semaphore("s_nc")
    s_S = nc.alloc_semaphore("s_S")
    s_msk = nc.alloc_semaphore("s_msk")
    s_P = nc.alloc_semaphore("s_P")
    s_PT = nc.alloc_semaphore("s_PT")
    s_PTe = nc.alloc_semaphore("s_PTe")
    s_pv = nc.alloc_semaphore("s_pv")
    s_sg = nc.alloc_semaphore("s_sg")
    s_oev = nc.alloc_semaphore("s_oev")
    s_vld = [nc.alloc_semaphore(f"s_vld{i}") for i in range(2)]
    s_xld = [nc.alloc_semaphore(f"s_xld{i}") for i in range(2)]
    s_xst = [nc.alloc_semaphore(f"s_xst{i}") for i in range(2)]
    s_vself = nc.alloc_semaphore("s_vself")
    vctr = [0]

    def vss(vector, inst):
        vctr[0] += 1
        inst.then_inc(s_vself, 1)
        vector.wait_ge(s_vself, vctr[0])

    sched, stream_gblk = _att_sched()

    with nc.Block() as block:
        @block.sync
        def _(sync):
            sync.dma_start(out=q_sb[:], in_=d_q).then_inc(s_ldq, 16)
            sync.dma_start(out=K_sb[:], in_=d_K).then_inc(s_ldK, 16)
            sync.dma_start(out=dpos_bc[:], in_=AP(d_dpos.tensor, d_dpos.offset,
                                                  [[0, 128]] + d_dpos.ap[1:])).then_inc(s_lddp, 16)
            for g in range(8):
                sync.dma_start(
                    out=Vres.ap()[:, DM * 4 * g:DM * 4 * (g + 1)].rearrange("p (v n) -> p v n", n=DM),
                    in_=d_V[512 * g:512 * (g + 1), :].rearrange("(v p) n -> p v n", p=128),
                ).then_inc(s_ldV, 16)
            sync.wait_ge(s_kn5, 1)
            sync.dma_start(out=h_row, in_=kprefix[:]).then_inc(s_bc1, 16)
            sync.wait_ge(s_bc1, 16)
            sync.dma_start(out=kmax_bc[:], in_=AP(h_row.tensor, h_row.offset,
                                                  [[0, 128]] + h_row.ap[1:])).then_inc(s_bc2, 16)
            for s in sched:
                si = s["si"]
                if si >= 2:
                    sync.wait_ge(s_oev, si - 1)
                sync.dma_start(out=xt[si % 2][:],
                               in_=d_x[128 * si:128 * (si + 1), :]).then_inc(s_xld[si % 2], 16)
                for chd in s["chunks"]:
                    for bl in chd["blks"]:
                        if bl["stream"] is not None:
                            g = bl["stream"]
                            if g >= 2:
                                sync.wait_ge(s_pv, stream_gblk[g - 2] + 1)
                            sync.dma_start(out=Vst[g % 2][:],
                                           in_=d_V[128 * bl["v"]:128 * (bl["v"] + 1), :]
                                           ).then_inc(s_vld[g % 2], 16)
                if si >= 2:
                    jj = si - 2
                    sync.wait_ge(s_oev, jj + 1)
                    sync.dma_start(out=o_xa[128 * jj:128 * (jj + 1), :],
                                   in_=xa[jj % 2][:]).then_inc(s_xst[jj % 2], 16)
            for jj in (PTILES - 2, PTILES - 1):
                sync.wait_ge(s_oev, jj + 1)
                sync.dma_start(out=o_xa[128 * jj:128 * (jj + 1), :],
                               in_=xa[jj % 2][:]).then_inc(s_xst[jj % 2], 16)

        @block.gpsimd
        def _(gpsimd):
            gpsimd.iota(iot[:], pattern=[[1, 128]], base=0, channel_multiplier=-1,
                        allow_small_or_imprecise_dtypes=True)
            gpsimd.iota(jiota[:], pattern=[[1, 512]], base=0, channel_multiplier=0,
                        allow_small_or_imprecise_dtypes=True)
            gpsimd.iota(p_col[:], pattern=[[0, 1]], base=0, channel_multiplier=1,
                        allow_small_or_imprecise_dtypes=True)
            gpsimd.memset(onesc[:], 1.0).then_inc(s_idg, 4)
            gpsimd.wait_ge(s_xst[0], 16 * (PTILES // 2))
            gpsimd.wait_ge(s_xst[1], 16 * (PTILES // 2))

        @block.scalar
        def _(scalar):
            scalar.wait_ge(s_ldK, 16)
            for ch in range(16):
                if ch >= 1:
                    scalar.wait_ge(s_kn2, ch)
                scalar.activation(k2[:], K_sb[:, 512 * ch:512 * (ch + 1)],
                                  AF.Square).then_inc(s_kn1, 1)
            scalar.wait_ge(s_kn3, 32)
            scalar.activation(kprefix[:], kprefix[:], AF.Sqrt).then_inc(s_kn5, 1)
            for s in sched:
                si = s["si"]
                b = 8 * (si % 2)
                sb = 16 * (si % 2)
                scalar.wait_ge(s_qt, si + 1)
                if si >= 1:
                    scalar.wait_ge(s_nc, si)
                scalar.activation(qt2[:], p_qt[:], AF.Square,
                                  accum_out=ncols[:, b + 0:b + 1])
                scalar.activation(ncols[:, b + 1:b + 2], ncols[:, b + 0:b + 1],
                                  AF.Sqrt).then_inc(s_qs, 1)
                if si >= 2:
                    scalar.wait_ge(s_sg, si - 1)
                for chd in s["chunks"]:
                    g = chd["gchunk"]
                    if chd["isB"]:
                        scalar.wait_ge(s_msk, chd["gpb"] + 1)
                    else:
                        scalar.wait_ge(s_S, g + 1)
                    scalar.wait_ge(s_nc, si + 1)
                    if g >= 2:
                        scalar.wait_ge(s_PT, 4 * (g - 1))
                    scalar.activation(P_sb[g % 2][:], p_S[g % 2][:], AF.Exp,
                                      bias=ncols[:, b + 2:b + 3], scale=0.125,
                                      accum_out=sig[:, sb + chd["ch"]:sb + chd["ch"] + 1]
                                      ).then_inc(s_P, 1)
                    for bl in chd["blks"]:
                        gb = bl["gblk"]
                        scalar.wait_ge(s_PT, gb + 1)
                        scalar.copy(PT_sb[gb % 2][:], p_PT[gb % 2][:]).then_inc(s_PTe, 1)

        @block.vector
        def _(vector):
            vector.wait_ge(s_idg, 4)
            vector.tensor_scalar(ident[:], iot[:], 0.0, None, ALU.is_equal).then_inc(s_idv, 1)
            for ch in range(16):
                vector.wait_ge(s_kn2, ch + 1)
                vss(vector, vector.tensor_reduce(
                    knorm2[:, 4 * ch:4 * (ch + 1)],
                    p_kn.ap().rearrange("p (g s) -> p g s", s=128), AX, ALU.max))
                vector.nop().then_inc(s_kn3, 1)
            for s in sched:
                vss(vector, vector.reduce_max(kprefix[:, s["si"]:s["si"] + 1],
                                              knorm2[:, 0:s["kvt"]], AX))
                vector.nop().then_inc(s_kn3, 1)
            vector.wait_ge(s_bc2, 16)
            vector.wait_ge(s_lddp, 16)
            for s in sched:
                si = s["si"]
                b = 8 * (si % 2)
                sb = 16 * (si % 2)
                vector.wait_ge(s_qs, si + 1)
                vss(vector, vector.tensor_scalar(ncols[:, b + 2:b + 3], ncols[:, b + 1:b + 2],
                                                 kmax_bc[:, si:si + 1], -0.125, ALU.mult, ALU.mult))
                vss(vector, vector.tensor_tensor(ncols[:, b + 3:b + 4], dpos_bc[:, si:si + 1],
                                                 p_col[:], ALU.add))
                vector.nop().then_inc(s_nc, 1)
                for chd in s["chunks"]:
                    if not chd["isB"]:
                        continue
                    g = chd["gchunk"]
                    vss(vector, vector.tensor_scalar(ncols[:, b + 4:b + 5], ncols[:, b + 3:b + 4],
                                                     float(chd["base"]), None, ALU.subtract))
                    vss(vector, vector.tensor_scalar(dmask[:], jiota[:], ncols[:, b + 4:b + 5],
                                                     NEG, ALU.is_gt, ALU.mult))
                    vector.wait_ge(s_S, g + 1)
                    vector.tensor_tensor(p_S[g % 2][:], p_S[g % 2][:], dmask[:],
                                         ALU.add).then_inc(s_msk, 1)
                lastg = s["chunks"][-1]["gchunk"]
                nch = s["nA"] + 2
                vector.wait_ge(s_P, lastg + 1)
                vss(vector, vector.reduce_sum(ncols[:, b + 5:b + 6], sig[:, sb:sb + nch], AX))
                vss(vector, vector.reciprocal(ncols[:, b + 6:b + 7], ncols[:, b + 5:b + 6]))
                vector.nop().then_inc(s_sg, 1)
                vector.wait_ge(s_pv, s["chunks"][-1]["blks"][-1]["gblk"] + 1)
                vector.tensor_scalar(o_sb[:, 0:512], p_o[0][:], ncols[:, b + 6:b + 7], None, ALU.mult)
                vector.tensor_scalar(o_sb[:, 512:1024], p_o[1][:], ncols[:, b + 6:b + 7], None, ALU.mult)
                vector.wait_ge(s_xld[si % 2], 16 * (si // 2 + 1))
                vector.tensor_tensor(xa[si % 2][:], xt[si % 2][:], o_sb[:], ALU.add).then_inc(s_oev, 1)

        @block.tensor
        def _(tensor):
            tensor.wait_ge(s_idg, 4)
            tensor.wait_ge(s_idv, 1)
            tensor.wait_ge(s_ldq, 16)
            tensor.wait_ge(s_ldK, 16)
            tensor.wait_ge(s_ldV, 16 * 8)
            for ch in range(16):
                tensor.wait_ge(s_kn1, ch + 1)
                if ch >= 1:
                    tensor.wait_ge(s_kn3, ch)
                tensor.matmul(p_kn[:], onesc[:], k2[:], start=True, stop=True).then_inc(s_kn2, 1)
            for s in sched:
                si = s["si"]
                if si >= 1:
                    tensor.wait_ge(s_qs, si)
                tensor.transpose(p_qt[:], q_sb[:, 128 * si:128 * (si + 1)],
                                 ident[0:KD, 0:KD]).then_inc(s_qt, 1)
                firstgb = s["chunks"][0]["blks"][0]["gblk"]
                lastgb = s["chunks"][-1]["blks"][-1]["gblk"]
                for chd in s["chunks"]:
                    g = chd["gchunk"]
                    if g >= 2:
                        tensor.wait_ge(s_P, g - 1)
                    tensor.matmul(p_S[g % 2][:], q_sb[:, 128 * si:128 * (si + 1)],
                                  K_sb[:, chd["base"]:chd["base"] + 512],
                                  start=True, stop=True).then_inc(s_S, 1)
                    for bi, bl in enumerate(chd["blks"]):
                        gb = bl["gblk"]
                        tensor.wait_ge(s_P, g + 1)
                        if gb >= 2:
                            tensor.wait_ge(s_PTe, gb - 1)
                        tensor.transpose(p_PT[gb % 2][:],
                                         P_sb[g % 2][:, 128 * bi:128 * (bi + 1)],
                                         ident[:]).then_inc(s_PT, 1)
                        tensor.wait_ge(s_PTe, gb + 1)
                        if bl["stream"] is None:
                            vt = Vres.ap()[:, DM * bl["v"]:DM * (bl["v"] + 1)]
                        else:
                            gs = bl["stream"]
                            tensor.wait_ge(s_vld[gs % 2], 16 * (gs // 2 + 1))
                            vt = Vst[gs % 2].ap()[:, :]
                        if si >= 1 and gb == firstgb:
                            tensor.wait_ge(s_oev, si)
                        tensor.matmul(p_o[0][:], PT_sb[gb % 2][:], vt[:, 0:512],
                                      start=(gb == firstgb), stop=(gb == lastgb),
                                      skip_group_check=True)
                        tensor.matmul(p_o[1][:], PT_sb[gb % 2][:], vt[:, 512:1024],
                                      start=(gb == firstgb), stop=(gb == lastgb),
                                      skip_group_check=True).then_inc(s_pv, 1)

    return nc




def build_l0moe():
    nc = bass.Bass("TRN2", target_bir_lowering=False)
    d_xn = nc.dram_tensor("xn_fm", [DM, TOK], BF16, kind="ExternalInput").ap()
    d_xa = nc.dram_tensor("xatt", [TOK, DM], F32, kind="ExternalInput").ap()
    d_rw = nc.dram_tensor("router_w", [DM, NE], F32, kind="ExternalInput").ap()
    d_rb = nc.dram_tensor("router_b", [1, NE], F32, kind="ExternalInput").ap()
    d_wg = nc.dram_tensor("we_gate", [NE, DM, DM], F32, kind="ExternalInput").ap()
    d_wu = nc.dram_tensor("we_up", [NE, DM, DM], F32, kind="ExternalInput").ap()
    d_wd = nc.dram_tensor("we_down", [NE, DM, DM], F32, kind="ExternalInput").ap()
    o_x1 = nc.dram_tensor("x1", [TOK, DM], F32, kind="ExternalOutput").ap()
    h_y = nc.dram_tensor("h_y", [TOK, DM], F32, kind="Internal").ap()

    xnfm = nc.alloc_sbuf_tensor("xnfm", [128, 8 * TOK], BF16)
    rw_sb = nc.alloc_sbuf_tensor("rw_sb", [128, 8 * NE], BF16)
    rb_bc = nc.alloc_sbuf_tensor("rb_bc", [128, NE], F32)
    wg_sb = [nc.alloc_sbuf_tensor(f"wg_sb{i}", [128, 8 * DM], BF16) for i in range(2)]
    wu_sb = [nc.alloc_sbuf_tensor(f"wu_sb{i}", [128, 8 * DM], BF16) for i in range(2)]
    wd_sb = [nc.alloc_sbuf_tensor(f"wd_sb{i}", [128, 8 * DM], BF16) for i in range(2)]
    act_sb = nc.alloc_sbuf_tensor("act_sb", [128, 8 * TOK], BF16)
    g_tmp = [nc.alloc_sbuf_tensor(f"g_tmp{i}", [128, 512], BF16) for i in range(2)]
    mrow = nc.alloc_sbuf_tensor("mrow", [128, 64], F32)
    rcols = nc.alloc_sbuf_tensor("rcols", [128, 16], F32)
    Wall = nc.alloc_sbuf_tensor("Wall", [128, 8 * PTILES], F32)
    ybuf = [nc.alloc_sbuf_tensor(f"ybuf{i}", [128, DM], F32) for i in range(2)]
    xat = [nc.alloc_sbuf_tensor(f"xat{i}", [128, DM], F32) for i in range(2)]
    yt = [nc.alloc_sbuf_tensor(f"yt{i}", [128, DM], F32) for i in range(2)]
    x1b = [nc.alloc_sbuf_tensor(f"x1b{i}", [128, DM], F32) for i in range(2)]

    p_g = [nc.alloc_psum_tensor(f"p_g{i}", [128, 512], F32) for i in range(2)]
    p_u = [nc.alloc_psum_tensor(f"p_u{i}", [128, 512], F32) for i in range(2)]
    p_y = [nc.alloc_psum_tensor(f"p_y{i}", [128, 512], F32) for i in range(4)]
    p_r = p_g[0]

    s_ldxn = nc.alloc_semaphore("s_ldxn")
    s_ldr = nc.alloc_semaphore("s_ldr")
    s_ldrb = nc.alloc_semaphore("s_ldrb")
    s_r0 = nc.alloc_semaphore("s_r0")
    s_lg = nc.alloc_semaphore("s_lg")
    s_r1 = nc.alloc_semaphore("s_r1")
    s_r2 = nc.alloc_semaphore("s_r2")
    s_w = nc.alloc_semaphore("s_w")
    s_wld = [nc.alloc_semaphore(f"s_wld{i}") for i in range(2)]
    s_g = nc.alloc_semaphore("s_g")
    s_u = nc.alloc_semaphore("s_u")
    s_sl = nc.alloc_semaphore("s_sl")
    s_act = nc.alloc_semaphore("s_act")
    s_dn = nc.alloc_semaphore("s_dn")
    s_yw = nc.alloc_semaphore("s_yw")
    s_yacc = [nc.alloc_semaphore(f"s_yacc{i}") for i in range(PTILES)]
    s_xal = [nc.alloc_semaphore(f"s_xal{i}") for i in range(2)]
    s_yld = [nc.alloc_semaphore(f"s_yld{i}") for i in range(2)]
    s_fadd = nc.alloc_semaphore("s_fadd")
    s_x1st = [nc.alloc_semaphore(f"s_x1st{i}") for i in range(2)]
    s_vself = nc.alloc_semaphore("s_vself")
    vctr = [0]

    def vss(vector, inst):
        vctr[0] += 1
        inst.then_inc(s_vself, 1)
        vector.wait_ge(s_vself, vctr[0])

    NU = 32

    with nc.Block() as block:
        @block.sync
        def _(sync):
            for c in range(8):
                sync.dma_start(out=xnfm[:, TOK * c:TOK * (c + 1)],
                               in_=d_xn[128 * c:128 * (c + 1), :]).then_inc(s_ldxn, 16)
            sync.dma_start(out=rb_bc[:], in_=AP(d_rb.tensor, d_rb.offset,
                                                [[0, 128]] + d_rb.ap[1:])).then_inc(s_ldrb, 16)
            for i in range(PTILES + 2):
                if i < PTILES:
                    if i >= 2:
                        sync.wait_ge(s_fadd, i - 1)
                    sync.dma_start(out=xat[i % 2][:],
                                   in_=d_xa[128 * i:128 * (i + 1), :]).then_inc(s_xal[i % 2], 16)
                    for e in range(NE):
                        sync.wait_ge(s_yacc[i], 16 * NE)
                        break
                    sync.dma_start(out=yt[i % 2][:],
                                   in_=h_y[128 * i:128 * (i + 1), :]).then_inc(s_yld[i % 2], 16)
                if i >= 2:
                    jj = i - 2
                    sync.wait_ge(s_fadd, jj + 1)
                    sync.dma_start(out=o_x1[128 * jj:128 * (jj + 1), :],
                                   in_=x1b[jj % 2][:]).then_inc(s_x1st[jj % 2], 16)

        @block.gpsimd
        def _(gpsimd):
            gpsimd.dma_start(out=rw_sb.ap().rearrange("p (c n) -> p c n", n=NE),
                             in_=d_rw.rearrange("(c p) n -> p c n", p=128)).then_inc(s_ldr, 16)

            def wload(e):
                par = e % 2
                if e >= 2:
                    gpsimd.wait_ge(s_u, NU * (e - 1))
                    gpsimd.wait_ge(s_dn, PTILES * (e - 1))
                for (dst, srcw) in ((wg_sb[par], d_wg), (wu_sb[par], d_wu), (wd_sb[par], d_wd)):
                    gpsimd.dma_start(
                        out=dst.ap().rearrange("p (c n) -> p c n", n=DM),
                        in_=srcw[e].rearrange("(c p) n -> p c n", p=128),
                    ).then_inc(s_wld[par], 16)
            wload(0)
            wload(1)
            for e in range(NE):
                for tt in range(PTILES):
                    gpsimd.wait_ge(s_yw, PTILES * e + tt + 1)
                    if e >= 1:
                        gpsimd.wait_ge(s_yacc[tt], 16 * e)
                    gpsimd.dma_start(out=h_y[128 * tt:128 * (tt + 1), :],
                                     in_=ybuf[tt % 2][:],
                                     accum_op=(ALU.bypass if e == 0 else ALU.add)
                                     ).then_inc(s_yacc[tt], 16)
                if e + 2 < NE:
                    wload(e + 2)
            gpsimd.wait_ge(s_x1st[0], 16 * (PTILES // 2))
            gpsimd.wait_ge(s_x1st[1], 16 * (PTILES // 2))

        @block.scalar
        def _(scalar):
            for tt in range(PTILES):
                b = 8 * (tt % 2)
                scalar.wait_ge(s_r1, tt + 1)
                scalar.activation(rcols[:, b + 1:b + 2], rcols[:, b + 0:b + 1],
                                  AF.Sigmoid).then_inc(s_r2, 1)
            for e in range(NE):
                for u in range(NU):
                    gu = NU * e + u
                    scalar.wait_ge(s_g, gu + 1)
                    if gu >= 2:
                        scalar.wait_ge(s_sl, gu - 1)
                        scalar.wait_ge(s_act, gu - 1)
                    scalar.activation(g_tmp[gu % 2][:], p_g[gu % 2][:],
                                      AF.Silu).then_inc(s_sl, 1)

        @block.vector
        def _(vector):
            vector.wait_ge(s_ldrb, 16)
            for tt in range(PTILES):
                b = 8 * (tt % 2)
                mb = 32 * (tt % 2)
                vector.wait_ge(s_r0, tt + 1)
                vss(vector, vector.tensor_tensor(mrow[:, mb + 0:mb + 8], p_r.ap()[:, 0:NE], rb_bc[:], ALU.add))
                vector.nop().then_inc(s_lg, 1)
                lg = mrow.ap()[:, mb + 0:mb + 8]
                m1 = rcols.ap()[:, b + 2:b + 3]
                vss(vector, vector.reduce_max(m1, lg, AX))
                vss(vector, vector.tensor_scalar(mrow[:, mb + 8:mb + 16], lg, m1, None, ALU.is_equal))
                vss(vector, vector.scalar_tensor_tensor(mrow[:, mb + 16:mb + 24],
                                                        mrow[:, mb + 8:mb + 16], NEG, lg,
                                                        ALU.mult, ALU.add))
                m2 = rcols.ap()[:, b + 3:b + 4]
                vss(vector, vector.reduce_max(m2, mrow[:, mb + 16:mb + 24], AX))
                vss(vector, vector.tensor_tensor(rcols[:, b + 0:b + 1], m1, m2, ALU.subtract))
                vector.nop().then_inc(s_r1, 1)
                vector.wait_ge(s_r2, tt + 1)
                vss(vector, vector.tensor_scalar(rcols[:, b + 4:b + 5], rcols[:, b + 1:b + 2],
                                                 -1.0, 1.0, ALU.mult, ALU.add))
                vss(vector, vector.tensor_scalar(mrow[:, mb + 24:mb + 32],
                                                 mrow[:, mb + 16:mb + 24], m2, None, ALU.is_equal))
                vss(vector, vector.tensor_scalar(mrow[:, mb + 8:mb + 16],
                                                 mrow[:, mb + 8:mb + 16],
                                                 rcols[:, b + 1:b + 2], None, ALU.mult))
                vss(vector, vector.scalar_tensor_tensor(Wall[:, 8 * tt:8 * (tt + 1)],
                                                        mrow[:, mb + 24:mb + 32],
                                                        rcols[:, b + 4:b + 5],
                                                        mrow[:, mb + 8:mb + 16],
                                                        ALU.mult, ALU.add))
                vector.nop().then_inc(s_w, 1)
            for e in range(NE):
                for u in range(NU):
                    gu = NU * e + u
                    hc, tc = divmod(u, 4)
                    vector.wait_ge(s_sl, gu + 1)
                    vector.wait_ge(s_u, gu + 1)
                    if e >= 1 and u == 0:
                        vector.wait_ge(s_dn, PTILES * e)
                    vector.tensor_tensor(act_sb[:, TOK * hc + 512 * tc:TOK * hc + 512 * (tc + 1)],
                                         g_tmp[gu % 2][:], p_u[gu % 2][:],
                                         ALU.mult).then_inc(s_act, 1)
                for tt in range(PTILES):
                    gd = PTILES * e + tt
                    vector.wait_ge(s_dn, gd + 1)
                    vector.wait_ge(s_w, PTILES)
                    if gd >= 2:
                        vector.wait_ge(s_yacc[(tt - 2) % PTILES], 16 * (e + (1 if tt >= 2 else 0)))
                    wcol = Wall.ap()[:, 8 * tt + e:8 * tt + e + 1]
                    gp = 2 * (gd % 2)
                    vector.tensor_scalar(ybuf[tt % 2][:, 0:512], p_y[gp][:], wcol, None, ALU.mult)
                    vector.tensor_scalar(ybuf[tt % 2][:, 512:1024], p_y[gp + 1][:], wcol, None,
                                         ALU.mult).then_inc(s_yw, 1)
            for i in range(PTILES):
                vector.wait_ge(s_xal[i % 2], 16 * (i // 2 + 1))
                vector.wait_ge(s_yld[i % 2], 16 * (i // 2 + 1))
                if i >= 2:
                    vector.wait_ge(s_x1st[i % 2], 16 * ((i - 2) // 2 + 1))
                vector.tensor_tensor(x1b[i % 2][:], xat[i % 2][:], yt[i % 2][:],
                                     ALU.add).then_inc(s_fadd, 1)

        @block.tensor
        def _(tensor):
            tensor.wait_ge(s_ldxn, 16 * 8)
            tensor.wait_ge(s_ldr, 16)
            for tt in range(PTILES):
                if tt >= 1:
                    tensor.wait_ge(s_lg, tt)
                for c in range(8):
                    mm = tensor.matmul(p_r.ap()[:, 0:NE],
                                       xnfm[:, TOK * c + 128 * tt:TOK * c + 128 * (tt + 1)],
                                       rw_sb[:, NE * c:NE * (c + 1)],
                                       start=(c == 0), stop=(c == 7))
                mm.then_inc(s_r0, 1)
            for e in range(NE):
                par = e % 2
                tensor.wait_ge(s_wld[par], 48 * (e // 2 + 1))
                for u in range(NU):
                    gu = NU * e + u
                    hc, tc = divmod(u, 4)
                    if gu == 0:
                        tensor.wait_ge(s_lg, PTILES)
                    if gu >= 2:
                        tensor.wait_ge(s_sl, gu - 1)
                        tensor.wait_ge(s_act, gu - 1)
                    for c in range(8):
                        mm = tensor.matmul(
                            p_g[gu % 2][:], wg_sb[par][:, DM * c + 128 * hc:DM * c + 128 * (hc + 1)],
                            xnfm[:, TOK * c + 512 * tc:TOK * c + 512 * (tc + 1)],
                            start=(c == 0), stop=(c == 7))
                    mm.then_inc(s_g, 1)
                    for c in range(8):
                        mm = tensor.matmul(
                            p_u[gu % 2][:], wu_sb[par][:, DM * c + 128 * hc:DM * c + 128 * (hc + 1)],
                            xnfm[:, TOK * c + 512 * tc:TOK * c + 512 * (tc + 1)],
                            start=(c == 0), stop=(c == 7))
                    mm.then_inc(s_u, 1)
                for tt in range(PTILES):
                    gd = PTILES * e + tt
                    gp = 2 * (gd % 2)
                    tensor.wait_ge(s_act, NU * (e + 1))
                    if gd >= 2:
                        tensor.wait_ge(s_yw, gd - 1)
                    for hc in range(8):
                        srcA = act_sb.ap()[:, TOK * hc + 128 * tt:TOK * hc + 128 * (tt + 1)]
                        mm0 = tensor.matmul(p_y[gp][:], srcA,
                                            wd_sb[par][:, DM * hc + 0:DM * hc + 512],
                                            start=(hc == 0), stop=(hc == 7),
                                            skip_group_check=True)
                        mm1 = tensor.matmul(p_y[gp + 1][:], srcA,
                                            wd_sb[par][:, DM * hc + 512:DM * hc + 1024],
                                            start=(hc == 0), stop=(hc == 7),
                                            skip_group_check=True)
                    mm1.then_inc(s_dn, 1)

    return nc



def build_l1a():
    nc = bass.Bass("TRN2", target_bir_lowering=False)
    d_x = nc.dram_tensor("x1", [TOK, DM], F32, kind="ExternalInput").ap()
    o_xnfm = nc.dram_tensor("xn1_fm", [DM, TOK], BF16, kind="ExternalOutput").ap()

    xt = [nc.alloc_sbuf_tensor(f"xt{i}", [128, DM], F32) for i in range(2)]
    sq2 = nc.alloc_sbuf_tensor("sq2", [128, DM], F32)
    cols = nc.alloc_sbuf_tensor("cols", [128, 16], F32)
    xn_tm = [nc.alloc_sbuf_tensor(f"xntm{i}", [128, DM], BF16) for i in range(2)]
    xnfm = nc.alloc_sbuf_tensor("xnfm", [128, 8 * TOK], BF16)
    ident = nc.alloc_sbuf_tensor("ident", [128, 128], BF16)
    iot = nc.alloc_sbuf_tensor("iot", [128, 128], F32)
    p_tr = [nc.alloc_psum_tensor(f"p_tr{i}", [128, 128], BF16) for i in range(2)]

    s_xld = [nc.alloc_semaphore(f"s_xld{i}") for i in range(2)]
    s_idg = nc.alloc_semaphore("s_idg")
    s_idv = nc.alloc_semaphore("s_idv")
    s_v2s = nc.alloc_semaphore("s_v2s")
    s_s2v = nc.alloc_semaphore("s_s2v")
    s_xn = nc.alloc_semaphore("s_xn")
    s_tr = nc.alloc_semaphore("s_tr")
    s_ev = nc.alloc_semaphore("s_ev")
    s_st = nc.alloc_semaphore("s_st")
    s_vself = nc.alloc_semaphore("s_vself")
    vctr = [0]

    def vss(vector, inst):
        vctr[0] += 1
        inst.then_inc(s_vself, 1)
        vector.wait_ge(s_vself, vctr[0])

    with nc.Block() as block:
        @block.sync
        def _(sync):
            for i in range(PTILES):
                if i >= 2:
                    sync.wait_ge(s_xn, i - 1)
                sync.dma_start(out=xt[i % 2][:],
                               in_=d_x[128 * i:128 * (i + 1), :]).then_inc(s_xld[i % 2], 16)
            sync.wait_ge(s_ev, 128)
            for c in range(8):
                sync.dma_start(out=o_xnfm[128 * c:128 * (c + 1), :],
                               in_=xnfm[:, TOK * c:TOK * (c + 1)]).then_inc(s_st, 16)

        @block.gpsimd
        def _(gpsimd):
            gpsimd.iota(iot[:], pattern=[[1, 128]], base=0, channel_multiplier=-1,
                        allow_small_or_imprecise_dtypes=True).then_inc(s_idg, 1)
            gpsimd.wait_ge(s_st, 16 * 8)

        @block.vector
        def _(vector):
            vector.wait_ge(s_idg, 1)
            vector.tensor_scalar(ident[:], iot[:], 0.0, None, ALU.is_equal).then_inc(s_idv, 1)
            for i in range(PTILES):
                t = xt[i % 2]
                b = 8 * (i % 2)
                vector.wait_ge(s_xld[i % 2], 16 * (i // 2 + 1))
                vss(vector, vector.reduce_sum(cols[:, b + 0:b + 1], t[:], AX))
                vector.tensor_tensor(sq2[:], t[:], t[:], ALU.mult)
                vss(vector, vector.reduce_sum(cols[:, b + 1:b + 2], sq2[:], AX))
                vss(vector, vector.tensor_scalar(cols[:, b + 2:b + 3], cols[:, b + 0:b + 1], 1.0 / DM, None, ALU.mult))
                vss(vector, vector.tensor_scalar(cols[:, b + 3:b + 4], cols[:, b + 1:b + 2], 1.0 / DM, None, ALU.mult))
                vss(vector, vector.tensor_tensor(cols[:, b + 4:b + 5], cols[:, b + 2:b + 3], cols[:, b + 2:b + 3], ALU.mult))
                vss(vector, vector.tensor_tensor(cols[:, b + 5:b + 6], cols[:, b + 3:b + 4], cols[:, b + 4:b + 5], ALU.subtract))
                vector.tensor_scalar(cols[:, b + 5:b + 6], cols[:, b + 5:b + 6], 1e-5, None, ALU.add).then_inc(s_v2s, 1)
                vector.wait_ge(s_s2v, i + 1)
                vss(vector, vector.reciprocal(cols[:, b + 7:b + 8], cols[:, b + 6:b + 7]))
                if i >= 2:
                    vector.wait_ge(s_tr, 8 * (i - 1))
                vector.tensor_scalar(xn_tm[i % 2][:], t[:], cols[:, b + 2:b + 3], cols[:, b + 7:b + 8],
                                     ALU.subtract, ALU.mult).then_inc(s_xn, 1)

        @block.scalar
        def _(scalar):
            for i in range(PTILES):
                b = 8 * (i % 2)
                scalar.wait_ge(s_v2s, i + 1)
                scalar.activation(cols[:, b + 6:b + 7], cols[:, b + 5:b + 6], AF.Sqrt).then_inc(s_s2v, 1)
                if i >= 1:
                    for c in range(8):
                        n = 8 * (i - 1) + c
                        scalar.wait_ge(s_tr, n + 1)
                        scalar.copy(xnfm[:, TOK * c + 128 * (i - 1): TOK * c + 128 * i],
                                    p_tr[n % 2][:]).then_inc(s_ev, 1)
            i = PTILES
            for c in range(8):
                n = 8 * (i - 1) + c
                scalar.wait_ge(s_tr, n + 1)
                scalar.copy(xnfm[:, TOK * c + 128 * (i - 1): TOK * c + 128 * i],
                            p_tr[n % 2][:]).then_inc(s_ev, 1)

        @block.tensor
        def _(tensor):
            tensor.wait_ge(s_idg, 1)
            tensor.wait_ge(s_idv, 1)
            for i in range(PTILES):
                tensor.wait_ge(s_xn, i + 1)
                for c in range(8):
                    n = 8 * i + c
                    if n >= 2:
                        tensor.wait_ge(s_ev, n - 1)
                    tensor.transpose(p_tr[n % 2][:], xn_tm[i % 2][:, 128 * c:128 * (c + 1)],
                                     ident[:]).then_inc(s_tr, 1)

    return nc




def build_m1():
    nc = bass.Bass("TRN2", target_bir_lowering=False)
    d_xn = nc.dram_tensor("xn1f", [DM, T], BF16, kind="ExternalInput").ap()
    d_win = nc.dram_tensor("w_in_T", [DM, 4 * CH], F32, kind="ExternalInput").ap()
    d_cw = nc.dram_tensor("conv_w", [2 * CH, 4], F32, kind="ExternalInput").ap()
    d_cb = nc.dram_tensor("conv_b", [2 * CH, 1], F32, kind="ExternalInput").ap()
    d_xp = nc.dram_tensor("xproj_T", [2 * CH, 96], F32, kind="ExternalInput").ap()
    o_xc = nc.dram_tensor("xc", [2 * CH, T], F32, kind="ExternalOutput").ap()
    o_sz = nc.dram_tensor("sz", [2 * CH, T], BF16, kind="ExternalOutput").ap()
    o_xdb = nc.dram_tensor("xdb", [2, 96, T], F32, kind="ExternalOutput").ap()

    CK = 1024
    NCHK = T // CK

    xch = [nc.alloc_sbuf_tensor(f"xch{i}", [128, 8 * CK], BF16) for i in range(2)]
    w_in_sb = nc.alloc_sbuf_tensor("w_in_sb", [128, 8 * 4 * CH], BF16)
    conv_sb = nc.alloc_sbuf_tensor("conv_sb", [128, 8 * 4], F32)
    convb_sb = nc.alloc_sbuf_tensor("convb_sb", [128, 8], F32)
    xproj_sb = nc.alloc_sbuf_tensor("xproj_sb", [128, 8 * 96], BF16)
    xi_buf = [nc.alloc_sbuf_tensor(f"xi_buf{i}", [128, CK + 3], F32) for i in range(4)]
    xc_pre = nc.alloc_sbuf_tensor("xc_pre", [128, CK], F32)
    xc_out = [nc.alloc_sbuf_tensor(f"xc_out{i}", [128, CK], F32) for i in range(4)]
    xcb = [nc.alloc_sbuf_tensor(f"xcb{i}", [128, 4 * CK], BF16) for i in range(2)]
    z_out = [nc.alloc_sbuf_tensor(f"z_out{i}", [128, CK], BF16) for i in range(4)]
    xdb_buf = [nc.alloc_sbuf_tensor(f"xdb_buf{i}", [96, CK], F32) for i in range(2)]

    p_ip = [nc.alloc_psum_tensor(f"p_ip{i}", [128, 512], F32) for i in range(2)]
    p_xdb = [nc.alloc_psum_tensor(f"p_xdb{i}", [96, 512], F32) for i in range(2)]

    s_wld = nc.alloc_semaphore("s_wld")
    s_xch = [nc.alloc_semaphore(f"s_xch{i}") for i in range(2)]
    s_mm = nc.alloc_semaphore("s_mm")
    s_ev = nc.alloc_semaphore("s_ev")
    s_cv = nc.alloc_semaphore("s_cv")
    s_sl = nc.alloc_semaphore("s_sl")
    s_b16 = nc.alloc_semaphore("s_b16")
    s_xmm = nc.alloc_semaphore("s_xmm")
    s_xev = nc.alloc_semaphore("s_xev")
    s_xcst = [nc.alloc_semaphore(f"s_xcst{i}") for i in range(4)]
    s_zst = [nc.alloc_semaphore(f"s_zst{i}") for i in range(4)]
    s_xdbst = [nc.alloc_semaphore(f"s_xdbst{i}") for i in range(2)]
    s_chend = nc.alloc_semaphore("s_chend")

    def chunks():
        out = []
        for P in range(2):
            for cc in range(NCHK):
                nat = cc if P == 0 else NCHK - 1 - cc
                out.append((8 * P + cc, P, nat, cc))
        return out

    def cv_wait(gci, ct):
        P, cc = divmod(gci, 8)
        if cc == 0:
            return 36 * P + ct + 1
        return 36 * P + 4 + 4 * (cc - 1) + ct + 1

    with nc.Block() as block:
        @block.sync
        def _(sync):
            chl = chunks()

            def emit_stores(gci, P, nat, cc):
                for ct in range(4):
                    u = 4 * gci + ct
                    sync.wait_ge(s_sl, u + 1)
                    sync.dma_start(out=o_xc[128 * (4 * P + ct):128 * (4 * P + ct + 1),
                                            CK * cc:CK * (cc + 1)],
                                   in_=xc_out[ct][:]).then_inc(s_xcst[ct], 16)
                for zt in range(4):
                    sync.wait_ge(s_ev, 16 * gci + 2 * zt + 10)
                    sync.dma_start(out=o_sz[128 * (4 * P + zt):128 * (4 * P + zt + 1),
                                            CK * cc:CK * (cc + 1)],
                                   in_=z_out[zt][:]).then_inc(s_zst[zt], 16)
                par = gci % 2
                sync.wait_ge(s_xev, gci + 1)
                sync.dma_start(out=o_xdb[P, :, CK * cc:CK * (cc + 1)],
                               in_=xdb_buf[par][:]).then_inc(s_xdbst[par], 16)

            for (gci, P, nat, cc) in chl:
                par = gci % 2
                if gci >= 1:
                    emit_stores(*chl[gci - 1])
                if gci >= 2:
                    sync.wait_ge(s_chend, gci - 1)
                sync.dma_start(out=xch[par].ap().rearrange("p (c n) -> p c n", n=CK),
                               in_=d_xn[:, CK * nat:CK * (nat + 1)].rearrange(
                                   "(c p) n -> p c n", p=128)).then_inc(s_xch[par], 16)
            emit_stores(*chl[-1])

        @block.gpsimd
        def _(gpsimd):
            gpsimd.dma_start(out=w_in_sb.ap().rearrange("p (c n) -> p c n", n=4 * CH),
                             in_=d_win.rearrange("(c p) n -> p c n", p=128)).then_inc(s_wld, 16)
            gpsimd.dma_start(out=conv_sb.ap().rearrange("p (c n) -> p c n", n=4),
                             in_=d_cw.rearrange("(c p) n -> p c n", p=128)).then_inc(s_wld, 16)
            gpsimd.dma_start(out=convb_sb.ap().rearrange("p (c n) -> p c n", n=1),
                             in_=d_cb.rearrange("(c p) n -> p c n", p=128),
                             allow_slow_non_contiguous=True).then_inc(s_wld, 16)
            gpsimd.dma_start(out=xproj_sb.ap().rearrange("p (c n) -> p c n", n=96),
                             in_=d_xp.rearrange("(c p) n -> p c n", p=128)).then_inc(s_wld, 16)
            for q in range(4):
                gpsimd.wait_ge(s_xcst[q], 16 * 16)
                gpsimd.wait_ge(s_zst[q], 16 * 16)
            gpsimd.wait_ge(s_xdbst[0], 16 * 8)
            gpsimd.wait_ge(s_xdbst[1], 16 * 8)

        @block.tensor
        def _(tensor):
            tensor.wait_ge(s_wld, 64)
            for (gci, P, nat, cc) in chunks():
                par = gci % 2
                tensor.wait_ge(s_xch[par], 16 * (gci // 2 + 1))
                for ob in range(8):
                    ochan = 1024 * P + 128 * ob
                    for u in range(2):
                        gu = 16 * gci + 2 * ob + u
                        if gu >= 2:
                            tensor.wait_ge(s_ev, gu - 1)
                        for c in range(8):
                            mm = tensor.matmul(
                                p_ip[gu % 2][:],
                                w_in_sb[:, 4 * CH * c + ochan:4 * CH * c + ochan + 128],
                                xch[par][:, CK * c + 512 * u:CK * c + 512 * (u + 1)],
                                start=(c == 0), stop=(c == 7))
                        mm.then_inc(s_mm, 1)
                for u in range(2):
                    gx = 2 * gci + u
                    if gx >= 2:
                        tensor.wait_ge(s_xev, gci)
                    tensor.wait_ge(s_b16, 4 * (gci + 1))
                    for ct in range(4):
                        mm = tensor.matmul(
                            p_xdb[u][:],
                            xproj_sb[:, 96 * (4 * P + ct):96 * (4 * P + ct + 1)],
                            xcb[gci % 2][:, CK * ct + 512 * u:CK * ct + 512 * (u + 1)],
                            start=(ct == 0), stop=(ct == 3))
                    mm.then_inc(s_xmm, 1)
                tensor.nop().then_inc(s_chend, 1)

        @block.scalar
        def _(scalar):
            for (gci, P, nat, cc) in chunks():
                for ob in range(8):
                    for u in range(2):
                        gu = 16 * gci + 2 * ob + u
                        scalar.wait_ge(s_mm, gu + 1)
                        if ob < 4:
                            ct = ob
                            scalar.wait_ge(s_cv, cv_wait(gci, ct))
                            dst_u = u if P == 0 else 1 - u
                            src = p_ip[gu % 2].ap()[:, :]
                            if P == 1:
                                src = src[:, ::-1]
                            scalar.copy(xi_buf[ct][:, 3 + 512 * dst_u:3 + 512 * (dst_u + 1)],
                                        src).then_inc(s_ev, 1)
                        else:
                            zt = ob - 4
                            if gci >= 1:
                                scalar.wait_ge(s_zst[zt], 16 * gci)
                            dst_u = u if P == 0 else 1 - u
                            src = p_ip[gu % 2].ap()[:, :]
                            if P == 1:
                                src = src[:, ::-1]
                            scalar.activation(z_out[zt][:, 512 * dst_u:512 * (dst_u + 1)],
                                              src, AF.Silu).then_inc(s_ev, 1)
                for ct in range(4):
                    uu = 4 * gci + ct
                    scalar.wait_ge(s_cv, 36 * (gci // 8) + 4 + 4 * (gci % 8) + ct + 1)
                    if gci >= 1:
                        scalar.wait_ge(s_xcst[ct], 16 * gci)
                        scalar.wait_ge(s_b16, 4 * gci)
                    scalar.activation(xc_out[ct][:], xc_pre[:], AF.Silu,
                                      bias=convb_sb[:, (4 * P + ct):(4 * P + ct) + 1]
                                      ).then_inc(s_sl, 1)
                par = gci % 2
                scalar.wait_ge(s_xmm, 2 * (gci + 1))
                if gci >= 2:
                    scalar.wait_ge(s_xdbst[par], 16 * ((gci - 2) // 2 + 1))
                scalar.copy(xdb_buf[par][:, 0:512], p_xdb[0][:])
                scalar.copy(xdb_buf[par][:, 512:1024], p_xdb[1][:]).then_inc(s_xev, 1)

        @block.vector
        def _(vector):
            for P in range(2):
                for ct in range(4):
                    vector.memset(xi_buf[ct][:, 0:3], 0.0).then_inc(s_cv, 1)
                for cc in range(NCHK):
                    gci = 8 * P + cc
                    for ct in range(4):
                        gu_hi = 16 * gci + 2 * ct + 1
                        vector.wait_ge(s_ev, gu_hi + 1)
                        w0 = conv_sb.ap()[:, 4 * (4 * P + ct) + 0:4 * (4 * P + ct) + 1]
                        vector.wait_ge(s_sl, 4 * gci + ct)
                        vector.tensor_scalar(xc_pre[:], xi_buf[ct][:, 0:CK], w0, None, ALU.mult)
                        for k in range(1, 4):
                            wk = conv_sb.ap()[:, 4 * (4 * P + ct) + k:4 * (4 * P + ct) + k + 1]
                            vector.scalar_tensor_tensor(xc_pre[:], xi_buf[ct][:, k:k + CK], wk,
                                                        xc_pre[:], ALU.mult, ALU.add)
                        vector.tensor_copy(xi_buf[ct][:, 0:3], xi_buf[ct][:, CK:CK + 3])
                        vector.nop().then_inc(s_cv, 1)
                    for ct in range(4):
                        uu = 4 * gci + ct
                        vector.wait_ge(s_sl, uu + 1)
                        if gci >= 2:
                            vector.wait_ge(s_xmm, 2 * (gci - 1))
                        vector.tensor_copy(xcb[gci % 2][:, CK * ct:CK * (ct + 1)],
                                           xc_out[ct][:]).then_inc(s_b16, 1)

    return nc




def build_m2():
    nc = bass.Bass("TRN2", target_bir_lowering=False)
    d_xc = nc.dram_tensor("xc", [2 * CH, T], F32, kind="ExternalInput").ap()
    d_sz = nc.dram_tensor("sz", [2 * CH, T], BF16, kind="ExternalInput").ap()
    d_xa = nc.dram_tensor("xdb_a", [2, 96, T], F32, kind="ExternalInput").ap()
    d_xb = [nc.dram_tensor(f"xdb_p{i}", [2, 96, T], F32, kind="ExternalInput").ap()
            for i in range(3)]
    d_dtw = nc.dram_tensor("dt_wT", [DTR, 2 * CH], F32, kind="ExternalInput").ap()
    d_dtb = nc.dram_tensor("dt_b", [2 * CH, 1], F32, kind="ExternalInput").ap()
    d_alog = nc.dram_tensor("A_log", [2 * CH, DST], F32, kind="ExternalInput").ap()
    d_dp = nc.dram_tensor("Dp", [2 * CH, 1], F32, kind="ExternalInput").ap()
    o_yg = nc.dram_tensor("yg", [2 * CH, T], BF16, kind="ExternalOutput").ap()
    h_xdb = nc.dram_tensor("h_xdb", [2, 96, T], F32, kind="Internal").ap()
    h_bc = nc.dram_tensor("h_bc", [2, 32, T], BF16, kind="Internal").ap()

    TC = 1024
    NCK = T // TC

    xdb1 = [nc.alloc_sbuf_tensor(f"xdb1_{i}", [96, 2048], F32) for i in range(2)]
    bc16 = [nc.alloc_sbuf_tensor(f"bc16_{i}", [32, 2048], BF16) for i in range(2)]
    xdb2 = [nc.alloc_sbuf_tensor(f"xdb2_{i}", [96, 2048], F32) for i in range(2)]
    dtw_sb = nc.alloc_sbuf_tensor("dtw_sb", [DTR, 2 * CH], F32)
    dtb_sb = nc.alloc_sbuf_tensor("dtb_sb", [128, 8], F32)
    dp_sb = nc.alloc_sbuf_tensor("dp_sb", [128, 8], F32)
    alog_sb = nc.alloc_sbuf_tensor("alog_sb", [128, 8 * DST], F32)
    A_sb = nc.alloc_sbuf_tensor("A_sb", [128, 8 * DST], F32)
    dtr = [nc.alloc_sbuf_tensor(f"dtr{i}", [DTR, TC], F32) for i in range(2)]
    u_sb = [nc.alloc_sbuf_tensor(f"u_sb{i}", [128, TC], F32) for i in range(4)]
    sz_sb = [nc.alloc_sbuf_tensor(f"sz_sb{i}", [128, TC], BF16) for i in range(4)]
    dl_sb = [nc.alloc_sbuf_tensor(f"dl_sb{i}", [128, TC], F32) for i in range(4)]
    w_sb = [nc.alloc_sbuf_tensor(f"w_sb{i}", [128, TC], F32) for i in range(4)]
    y_sb = [nc.alloc_sbuf_tensor(f"y_sb{i}", [128, TC], F32) for i in range(4)]
    Bb = [nc.alloc_sbuf_tensor(f"Bb{i}", [128, 4 * TC], BF16) for i in range(2)]
    Cb = [nc.alloc_sbuf_tensor(f"Cb{i}", [128, 4 * TC], BF16) for i in range(2)]
    dA = [nc.alloc_sbuf_tensor(f"dA{i}", [128, TC], F32) for i in range(3)]
    dBu = [nc.alloc_sbuf_tensor(f"dBu{i}", [128, TC], F32) for i in range(2)]
    h_b = [nc.alloc_sbuf_tensor(f"h_b{i}", [128, TC], F32) for i in range(2)]
    hC = [nc.alloc_sbuf_tensor(f"hC{i}", [128, TC], F32) for i in range(3)]
    state = [nc.alloc_sbuf_tensor(f"state{i}", [128, DST], F32) for i in range(4)]
    st4 = nc.alloc_sbuf_tensor("st4", [128, 8], F32)
    ygb = [nc.alloc_sbuf_tensor(f"ygb{i}", [128, TC], BF16) for i in range(4)]
    sf1 = nc.alloc_sbuf_tensor("sf1", [128, 512], F32)

    p_dt = [nc.alloc_psum_tensor(f"p_dt{i}", [128, 512], F32) for i in range(2)]

    s_ldw = nc.alloc_semaphore("s_ldw")
    s_x1 = [nc.alloc_semaphore(f"s_x1_{i}") for i in range(2)]
    s_x2 = [nc.alloc_semaphore(f"s_x2_{i}") for i in range(2)]
    s_add = nc.alloc_semaphore("s_add")
    s_b16c = nc.alloc_semaphore("s_b16c")
    s_bcst = [nc.alloc_semaphore(f"s_bcst{i}") for i in range(2)]
    s_xst = [nc.alloc_semaphore(f"s_xst{i}") for i in range(2)]
    s_uld = [nc.alloc_semaphore(f"s_uld{i}") for i in range(4)]
    s_zld = [nc.alloc_semaphore(f"s_zld{i}") for i in range(4)]
    s_dtld = [nc.alloc_semaphore(f"s_dtld{i}") for i in range(2)]
    s_bld = [nc.alloc_semaphore(f"s_bld{i}") for i in range(2)]
    s_cld = [nc.alloc_semaphore(f"s_cld{i}") for i in range(2)]
    s_dmm = nc.alloc_semaphore("s_dmm")
    s_sp = nc.alloc_semaphore("s_sp")
    s_dl = nc.alloc_semaphore("s_dl")
    s_w = nc.alloc_semaphore("s_w")
    s_da = nc.alloc_semaphore("s_da")
    s_hc = nc.alloc_semaphore("s_hc")
    s_y = nc.alloc_semaphore("s_y")
    s_zf = nc.alloc_semaphore("s_zf")
    s_gat = nc.alloc_semaphore("s_gat")
    s_ygst = [nc.alloc_semaphore(f"s_ygst{i}") for i in range(4)]
    s_A = nc.alloc_semaphore("s_A")
    s_vself = nc.alloc_semaphore("s_vself")
    vctr = [0]

    def vss(vector, inst):
        vctr[0] += 1
        inst.then_inc(s_vself, 1)
        vector.wait_ge(s_vself, vctr[0])

    def uidx_of(d, c):
        return NCK * d + c

    NUNIT = 2 * NCK

    with nc.Block() as block:
        @block.sync
        def _(sync):
            for d in range(2):
                for q in range(4):
                    k = 4 * d + q
                    par = k % 2
                    if k >= 2:
                        sync.wait_ge(s_xst[k % 2], 16 * ((k - 2) // 2 + 1))
                    sync.dma_start(out=xdb1[par][:], in_=d_xa[d, :, 2048 * q:2048 * (q + 1)]
                                   ).then_inc(s_x1[par], 16)
                    for m in range(3):
                        if 3 * k + m >= 1:
                            sync.wait_ge(s_add, 3 * k + m)
                        sync.dma_start(out=xdb2[(3 * k + m) % 2][:],
                                       in_=d_xb[m][d, :, 2048 * q:2048 * (q + 1)]
                                       ).then_inc(s_x2[(3 * k + m) % 2], 16)
                    sync.wait_ge(s_add, 3 * (k + 1))
                    sync.dma_start(out=h_xdb[d, :, 2048 * q:2048 * (q + 1)],
                                   in_=xdb1[par][:]).then_inc(s_xst[par], 16)
                    sync.wait_ge(s_b16c, k + 1)
                    sync.dma_start(out=h_bc[d, :, 2048 * q:2048 * (q + 1)],
                                   in_=bc16[par][:]).then_inc(s_bcst[par], 16)
            for d in range(2):
                for c in range(NCK):
                    ui = uidx_of(d, c)
                    if ui >= 1:
                        pd, pc = divmod(ui - 1, NCK)
                        for ct in range(4):
                            pu = 4 * (ui - 1) + ct
                            sync.wait_ge(s_gat, pu + 1)
                            sync.dma_start(out=o_yg[512 * pd + 128 * ct:512 * pd + 128 * (ct + 1),
                                                    TC * pc:TC * (pc + 1)],
                                           in_=ygb[ct][:]).then_inc(s_ygst[ct], 16)
                    par = ui % 2
                    sync.wait_ge(s_xst[0], 16 * (2 * (d + 1)))
                    sync.wait_ge(s_xst[1], 16 * (2 * (d + 1)))
                    if ui >= 2:
                        sync.wait_ge(s_dmm, 2 * (ui - 1))
                    sync.dma_start(out=dtr[par][:], in_=h_xdb[d, 0:DTR, TC * c:TC * (c + 1)]
                                   ).then_inc(s_dtld[par], 16)
                    for ct in range(4):
                        if ui >= 1:
                            sync.wait_ge(s_gat, 4 * (ui - 1) + ct + 1)
                        sync.dma_start(out=u_sb[ct][:],
                                       in_=d_xc[512 * d + 128 * ct:512 * d + 128 * (ct + 1),
                                                TC * c:TC * (c + 1)]).then_inc(s_uld[ct], 16)
                        sync.dma_start(out=sz_sb[ct][:],
                                       in_=d_sz[512 * d + 128 * ct:512 * d + 128 * (ct + 1),
                                                TC * c:TC * (c + 1)]).then_inc(s_zld[ct], 16)
                    for sg in range(4):
                        gg = 4 * ui + sg
                        par2 = gg % 2
                        if gg >= 2:
                            sync.wait_ge(s_hc, 16 * (gg - 1))
                        sync.dma_start(out=Bb[par2][:].rearrange("p (s n) -> p s n", n=TC),
                                       in_=AP(h_bc.tensor,
                                              h_bc.offset + (T * 32 * d + T * 4 * sg + TC * c),
                                              [[0, 128], [T, 4], [1, TC]])).then_inc(s_bld[par2], 16)
                        sync.dma_start(out=Cb[par2][:].rearrange("p (s n) -> p s n", n=TC),
                                       in_=AP(h_bc.tensor,
                                              h_bc.offset + (T * 32 * d + T * (DST + 4 * sg) + TC * c),
                                              [[0, 128], [T, 4], [1, TC]])).then_inc(s_cld[par2], 16)
            for ct in range(4):
                pu = 4 * (NUNIT - 1) + ct
                sync.wait_ge(s_gat, pu + 1)
                sync.dma_start(out=o_yg[512 + 128 * ct:512 + 128 * (ct + 1),
                                        TC * (NCK - 1):TC * NCK],
                               in_=ygb[ct][:]).then_inc(s_ygst[ct], 16)

        @block.gpsimd
        def _(gpsimd):
            gpsimd.dma_start(out=dtw_sb[:], in_=d_dtw).then_inc(s_ldw, 16)
            gpsimd.dma_start(out=dtb_sb.ap().rearrange("p (c n) -> p c n", n=1),
                             in_=d_dtb.rearrange("(c p) n -> p c n", p=128),
                             allow_slow_non_contiguous=True).then_inc(s_ldw, 16)
            gpsimd.dma_start(out=dp_sb.ap().rearrange("p (c n) -> p c n", n=1),
                             in_=d_dp.rearrange("(c p) n -> p c n", p=128),
                             allow_slow_non_contiguous=True).then_inc(s_ldw, 16)
            gpsimd.dma_start(out=alog_sb.ap().rearrange("p (c n) -> p c n", n=DST),
                             in_=d_alog.rearrange("(c p) n -> p c n", p=128)).then_inc(s_ldw, 16)
            for d in range(2):
                for c in range(NCK):
                    ui = uidx_of(d, c)
                    for s in range(DST):
                        gb = 16 * ui + s
                        for ct in range(4):
                            gsc = 4 * gb + ct
                            gpsimd.wait_ge(s_hc, gsc + 1)
                            if s == 0:
                                if ui >= 1:
                                    gpsimd.wait_ge(s_gat, 4 * (ui - 1) + ct + 1)
                                gpsimd.tensor_copy(y_sb[ct][:], hC[gsc % 3][:]).then_inc(s_y, 1)
                            else:
                                gpsimd.tensor_tensor(y_sb[ct][:], y_sb[ct][:], hC[gsc % 3][:],
                                                     ALU.add).then_inc(s_y, 1)
            for q in range(4):
                gpsimd.wait_ge(s_ygst[q], 16 * NUNIT)
            gpsimd.wait_ge(s_bcst[0], 64)
            gpsimd.wait_ge(s_bcst[1], 64)

        @block.scalar
        def _(scalar):
            scalar.wait_ge(s_ldw, 64)
            scalar.activation(A_sb[:], alog_sb[:], AF.Exp).then_inc(s_A, 1)
            scalar.wait_ge(s_A, 2)
            for d in range(2):
                for c in range(NCK):
                    ui = uidx_of(d, c)
                    for ct in range(4):
                        for u2 in range(2):
                            gdt = 8 * ui + 2 * ct + u2
                            scalar.wait_ge(s_dmm, gdt + 1)
                            bias = dtb_sb.ap()[:, 4 * d + ct:4 * d + ct + 1]
                            scalar.activation(sf1[:], p_dt[gdt % 2][:], AF.Abs, bias=bias)
                            scalar.activation(sf1[:], sf1[:], AF.Exp, scale=-1.0)
                            scalar.activation(sf1[:], sf1[:], AF.Ln, bias=1.0)
                            scalar.activation(dl_sb[ct][:, 512 * u2:512 * (u2 + 1)],
                                              p_dt[gdt % 2][:], AF.Relu,
                                              bias=bias).then_inc(s_sp, 1)
                            scalar.wait_ge(s_dl, gdt + 1)
                    for s in range(DST):
                        gb = 16 * ui + s
                        for ct in range(4):
                            gsc = 4 * gb + ct
                            scalar.wait_ge(s_dl, 8 * ui + 2 * ct + 2)
                            if gsc >= 3:
                                scalar.wait_ge(s_hc, gsc - 2)
                            scalar.activation(dA[gsc % 3][:], dl_sb[ct][:], AF.Exp,
                                              scale=A_sb[:, DST * (4 * d + ct) + s:
                                                         DST * (4 * d + ct) + s + 1]
                                              ).then_inc(s_da, 1)

        @block.vector
        def _(vector):
            vector.wait_ge(s_A, 1)
            vector.tensor_scalar(A_sb[:], A_sb[:], -1.0, None, ALU.mult).then_inc(s_A, 1)
            for k in range(8):
                par = k % 2
                vector.wait_ge(s_x1[par], 16 * (k // 2 + 1))
                for m in range(3):
                    g2 = 3 * k + m
                    vector.wait_ge(s_x2[g2 % 2], 16 * (g2 // 2 + 1))
                    vector.tensor_tensor(xdb1[par][:], xdb1[par][:], xdb2[g2 % 2][:],
                                         ALU.add).then_inc(s_add, 1)
                if k >= 2:
                    vector.wait_ge(s_bcst[par], 16 * ((k - 2) // 2 + 1))
                vector.tensor_copy(bc16[par][:], xdb1[par][64:96, :]).then_inc(s_b16c, 1)
            for d in range(2):
                for c in range(NCK):
                    ui = uidx_of(d, c)
                    for ct in range(4):
                        for u2 in range(2):
                            gdt = 8 * ui + 2 * ct + u2
                            vector.wait_ge(s_sp, gdt + 1)
                            vector.tensor_tensor(dl_sb[ct][:, 512 * u2:512 * (u2 + 1)],
                                                 dl_sb[ct][:, 512 * u2:512 * (u2 + 1)],
                                                 sf1[:], ALU.add).then_inc(s_dl, 1)
                        vector.wait_ge(s_uld[ct], 16 * (ui + 1))
                        vector.tensor_tensor(w_sb[ct][:], dl_sb[ct][:], u_sb[ct][:],
                                             ALU.mult).then_inc(s_w, 1)
                    if c == 0:
                        for ct in range(4):
                            vector.memset(state[ct][:], 0.0)
                    for s in range(DST):
                        gb = 16 * ui + s
                        for ct in range(4):
                            gsc = 4 * gb + ct
                            gg = 4 * ui + s // 4
                            so = s % 4
                            vector.wait_ge(s_bld[gg % 2], 16 * (gg // 2 + 1))
                            vector.tensor_tensor(dBu[gsc % 2][:], w_sb[ct][:],
                                                 Bb[gg % 2][:, TC * so:TC * (so + 1)],
                                                 ALU.mult)
                            vector.wait_ge(s_da, gsc + 1)
                            vector.tensor_tensor_scan(h_b[gsc % 2][:], dA[gsc % 3][:],
                                                      dBu[gsc % 2][:],
                                                      state[ct][:, s:s + 1], ALU.mult, ALU.add)
                            vss(vector, vector.tensor_copy(state[ct][:, s:s + 1],
                                                           h_b[gsc % 2][:, TC - 1:TC]))
                            vector.wait_ge(s_cld[gg % 2], 16 * (gg // 2 + 1))
                            if gsc >= 3:
                                vector.wait_ge(s_y, gsc - 2)
                            vector.tensor_tensor(hC[gsc % 3][:], h_b[gsc % 2][:],
                                                 Cb[gg % 2][:, TC * so:TC * (so + 1)],
                                                 ALU.mult).then_inc(s_hc, 1)
                    for ct in range(4):
                        uu = 4 * ui + ct
                        vector.wait_ge(s_y, 64 * ui + 60 + ct + 1)
                        vector.wait_ge(s_zld[ct], 16 * (ui + 1))
                        dcol = dp_sb.ap()[:, 4 * d + ct:4 * d + ct + 1]
                        vector.scalar_tensor_tensor(y_sb[ct][:], u_sb[ct][:], dcol, y_sb[ct][:],
                                                    ALU.mult, ALU.add)
                        if ui >= 1:
                            vector.wait_ge(s_ygst[ct], 16 * ui)
                        vector.tensor_tensor(ygb[ct][:], y_sb[ct][:], sz_sb[ct][:],
                                             ALU.mult).then_inc(s_gat, 1)

        @block.tensor
        def _(tensor):
            tensor.wait_ge(s_ldw, 16)
            for d in range(2):
                for c in range(NCK):
                    ui = uidx_of(d, c)
                    par = ui % 2
                    tensor.wait_ge(s_dtld[par], 16 * (ui // 2 + 1))
                    for ct in range(4):
                        for u2 in range(2):
                            gdt = 8 * ui + 2 * ct + u2
                            if gdt >= 2:
                                tensor.wait_ge(s_sp, gdt - 1)
                            tensor.matmul(p_dt[gdt % 2][:],
                                          dtw_sb[:, 512 * d + 128 * ct:512 * d + 128 * (ct + 1)],
                                          dtr[par][:, 512 * u2:512 * (u2 + 1)],
                                          start=True, stop=True).then_inc(s_dmm, 1)

    return nc




def build_l1ffn():
    nc = bass.Bass("TRN2", target_bir_lowering=False)
    d_wup = nc.dram_tensor("w_upT", [DM, FF], F32, kind="ExternalInput").ap()
    d_wdn = nc.dram_tensor("w_downT", [FF, DM], F32, kind="ExternalInput").ap()
    d_xn = nc.dram_tensor("xn1_fm", [DM, TOK], BF16, kind="ExternalInput").ap()
    o_f = nc.dram_tensor("ffn", [TOK, DM], F32, kind="ExternalOutput").ap()
    h_hbm = nc.dram_tensor("h_hbm", [FF, TOK], BF16, kind="Internal").ap()
    h_row = nc.dram_tensor("hrow", [1, TOK], F32, kind="Internal").ap()
    h_s1 = nc.dram_tensor("h_s1", [1, 2], F32, kind="Internal").ap()

    arena = nc.alloc_sbuf_tensor("arena", [128, 32768], BF16)
    wld = [nc.alloc_sbuf_tensor(f"wld{i}", [128, 2048], F32) for i in range(2)]
    tern = nc.alloc_sbuf_tensor("tern", [128, 2048], BF16)
    acc8 = nc.alloc_sbuf_tensor("acc8", [128, 40], F32)
    onef = nc.alloc_sbuf_tensor("onef", [128, 1], F32)
    onesb = nc.alloc_sbuf_tensor("onesb", [128, 1], BF16)
    tau_bc = nc.alloc_sbuf_tensor("tau_bc", [128, 2], F32)
    xn_sb = nc.alloc_sbuf_tensor("xn_sb", [128, 8 * TOK], BF16)
    sqb = nc.alloc_sbuf_tensor("sqb", [128, 2048], BF16)
    rrow = nc.alloc_sbuf_tensor("rrow", [1, TOK], F32)
    rbc = nc.alloc_sbuf_tensor("rbc", [128, TOK], BF16)
    h_t = [nc.alloc_sbuf_tensor(f"h_t{i}", [128, 512], BF16) for i in range(2)]
    hcol = nc.alloc_sbuf_tensor("hcol", [128, 32 * 128], BF16)
    outb = [nc.alloc_sbuf_tensor(f"outb{i}", [128, DM], F32) for i in range(2)]

    pA = [nc.alloc_psum_tensor(f"pA{i}", [128, 512], F32) for i in range(2)]
    pB = [nc.alloc_psum_tensor(f"pB{i}", [128, 512], F32) for i in range(2)]
    pS = [nc.alloc_psum_tensor(f"pS{i}", [1, 512], F32) for i in range(4)]

    s_wl = [nc.alloc_semaphore(f"s_wl{i}") for i in range(2)]
    s_abs = nc.alloc_semaphore("s_abs")
    s_ones = nc.alloc_semaphore("s_ones")
    s_smm = nc.alloc_semaphore("s_smm")
    s_sev = nc.alloc_semaphore("s_sev")
    s_bc1 = nc.alloc_semaphore("s_bc1")
    s_bc2 = nc.alloc_semaphore("s_bc2")
    s_tern = nc.alloc_semaphore("s_tern")
    s_ldxn = nc.alloc_semaphore("s_ldxn")
    s_sq = nc.alloc_semaphore("s_sq")
    s_qmm = nc.alloc_semaphore("s_qmm")
    s_r1d = nc.alloc_semaphore("s_r1d")
    s_rrow = nc.alloc_semaphore("s_rrow")
    s_rb1 = nc.alloc_semaphore("s_rb1")
    s_rb2 = nc.alloc_semaphore("s_rb2")
    s_xno = nc.alloc_semaphore("s_xno")
    s_up = nc.alloc_semaphore("s_up")
    s_gel = nc.alloc_semaphore("s_gel")
    s_h2 = nc.alloc_semaphore("s_h2")
    s_h2m = nc.alloc_semaphore("s_h2m")
    s_hst = [nc.alloc_semaphore(f"s_hst{i}") for i in range(2)]
    s_hld = nc.alloc_semaphore("s_hld")
    s_hsc = nc.alloc_semaphore("s_hsc")
    s_dmm = nc.alloc_semaphore("s_dmm")
    s_fout = nc.alloc_semaphore("s_fout")
    s_ost = [nc.alloc_semaphore(f"s_ost{i}") for i in range(2)]
    s_vself = nc.alloc_semaphore("s_vself")
    s_sself = nc.alloc_semaphore("s_sself")
    vctr = [0]
    sctr = [0]

    def vss(vector, inst):
        vctr[0] += 1
        inst.then_inc(s_vself, 1)
        vector.wait_ge(s_vself, vctr[0])

    def sss(scalar, inst):
        sctr[0] += 1
        inst.then_inc(s_sself, 1)
        scalar.wait_ge(s_sself, sctr[0])

    def wchunks():
        out = []
        for i8 in range(8):
            for hh in range(2):
                out.append(("up", i8, hh))
        for i16 in range(16):
            out.append(("dn", i16, 0))
        return out

    WCH = wchunks()

    def wload(sync, k, tag, a, b):
        par = k % 2
        if tag == "up":
            sync.dma_start(out=wld[par][:],
                           in_=d_wup[128 * a:128 * (a + 1), 2048 * b:2048 * (b + 1)]
                           ).then_inc(s_wl[par], 16)
        else:
            sync.dma_start(out=wld[par].ap().rearrange("p (c n) -> p c n", n=DM),
                           in_=d_wdn[256 * a:256 * (a + 1), :].rearrange(
                               "(c p) n -> p c n", p=128)).then_inc(s_wl[par], 16)

    with nc.Block() as block:
        @block.sync
        def _(sync):
            for k, (tag, a, b) in enumerate(WCH):
                if k >= 2:
                    sync.wait_ge(s_abs, k - 1)
                wload(sync, k, tag, a, b)
            for c in range(8):
                sync.dma_start(out=xn_sb[:, TOK * c:TOK * (c + 1)],
                               in_=d_xn[128 * c:128 * (c + 1), :]).then_inc(s_ldxn, 16)
            sync.wait_ge(s_sev, 2)
            sync.dma_start(out=h_s1, in_=acc8[0:1, 34:36]).then_inc(s_bc1, 16)
            sync.wait_ge(s_bc1, 16)
            sync.dma_start(out=tau_bc[:], in_=AP(h_s1.tensor, h_s1.offset,
                                                 [[0, 128], [1, 2]])).then_inc(s_bc2, 16)
            for k, (tag, a, b) in enumerate(WCH[:16]):
                sync.wait_ge(s_tern, max(0, k - 1))
                wload(sync, 32 + k, tag, a, b)
            sync.wait_ge(s_rrow, 101)
            sync.dma_start(out=h_row, in_=rrow[:]).then_inc(s_rb1, 16)
            for k in range(128):
                par = k % 2
                sync.wait_ge(s_gel, k + 1)
                fc, tc = divmod(k, 4)
                sync.dma_start(out=h_hbm[128 * fc:128 * (fc + 1), 512 * tc:512 * (tc + 1)],
                               in_=h_t[par][:]).then_inc(s_hst[par], 16)
            for k, (tag, a, b) in enumerate(WCH):
                if k >= 16:
                    sync.wait_ge(s_tern, k - 1)
                    wload(sync, 32 + k, tag, a, b)
            sync.wait_ge(s_rrow, 202)
            sync.dma_start(out=h_row, in_=rrow[:]).then_inc(s_rb1, 16)
            sync.wait_ge(s_hst[0], 16 * 64)
            sync.wait_ge(s_hst[1], 16 * 64)
            for tt in range(PTILES + 2):
                if tt < PTILES:
                    if tt >= 1:
                        sync.wait_ge(s_dmm, tt)
                    sync.dma_start(out=hcol.ap().rearrange("p (c n) -> p c n", n=128),
                                   in_=h_hbm[:, 128 * tt:128 * (tt + 1)].rearrange(
                                       "(c p) n -> p c n", p=128)).then_inc(s_hld, 16)
                if tt >= 2:
                    jj = tt - 2
                    sync.wait_ge(s_fout, jj + 1)
                    sync.dma_start(out=o_f[128 * jj:128 * (jj + 1), :],
                                   in_=outb[jj % 2][:]).then_inc(s_ost[jj % 2], 16)

        @block.gpsimd
        def _(gpsimd):
            gpsimd.memset(onef[:], 1.0)
            gpsimd.memset(onesb[:], 1.0).then_inc(s_ones, 1)
            gpsimd.wait_ge(s_rb1, 16)
            gpsimd.dma_start(out=rbc[:], in_=AP(h_row.tensor, h_row.offset,
                                                [[0, 128], [1, TOK]])).then_inc(s_rb2, 16)
            gpsimd.wait_ge(s_rb1, 32)
            gpsimd.wait_ge(s_xno, 8)
            gpsimd.dma_start(out=rbc[:], in_=AP(h_row.tensor, h_row.offset,
                                                [[0, 128], [1, TOK]])).then_inc(s_rb2, 16)
            gpsimd.wait_ge(s_hst[0], 16 * 64)
            gpsimd.wait_ge(s_hst[1], 16 * 64)
            gpsimd.wait_ge(s_ost[0], 16 * (PTILES // 2))
            gpsimd.wait_ge(s_ost[1], 16 * (PTILES // 2))

        @block.vector
        def _(vector):
            for k in range(32):
                vector.wait_ge(s_wl[k % 2], 16 * (k // 2 + 1))
                vss(vector, vector.tensor_reduce(acc8[:, k:k + 1], wld[k % 2][:], AX,
                                                 ALU.add, apply_absolute_value=True))
                vector.nop().then_inc(s_abs, 1)
            vss(vector, vector.reduce_sum(acc8[:, 32:33], acc8[:, 0:16], AX))
            vss(vector, vector.reduce_sum(acc8[:, 33:34], acc8[:, 16:32], AX))
            vector.nop().then_inc(s_abs, 1)
            def tern_one(k, tag, a, b):
                vector.wait_ge(s_wl[k % 2], 16 * (16 + k // 2 + 1))
                vector.wait_ge(s_bc2, 16)
                if tag == "dn" and a == 0 and b == 0:
                    vector.wait_ge(s_up, 128)
                tcol = tau_bc.ap()[:, 0:1] if tag == "up" else tau_bc.ap()[:, 1:2]
                if tag == "up":
                    dst = arena.ap()[:, 4096 * a + 2048 * b:4096 * a + 2048 * (b + 1)]
                else:
                    dst = arena.ap()[:, 2048 * a:2048 * (a + 1)]
                vss(vector, vector.tensor_scalar(tern[:], wld[k % 2][:], tcol, None, ALU.is_gt))
                vss(vector, vector.tensor_scalar(wld[k % 2][:], wld[k % 2][:], -1.0, None, ALU.mult))
                vss(vector, vector.tensor_scalar(dst, wld[k % 2][:], tcol, None, ALU.is_gt))
                vss(vector, vector.tensor_tensor(dst, tern[:], dst, ALU.subtract))
                vector.nop().then_inc(s_tern, 1)
            for k, (tag, a, b) in enumerate(WCH[:16]):
                tern_one(k, tag, a, b)
            vector.wait_ge(s_rrow, 100)
            vss(vector, vector.tensor_scalar(rrow[:], rrow[:], 1e-8, None, ALU.add))
            vss(vector, vector.reciprocal(rrow[:], rrow[:]))
            vector.nop().then_inc(s_rrow, 1)
            vector.wait_ge(s_rb2, 16)
            vector.wait_ge(s_qmm, 32)
            for c in range(8):
                vector.tensor_tensor(xn_sb[:, TOK * c:TOK * (c + 1)],
                                     xn_sb[:, TOK * c:TOK * (c + 1)], rbc[:],
                                     ALU.mult).then_inc(s_xno, 1)
            for k, (tag, a, b) in enumerate(WCH):
                if k >= 16:
                    tern_one(k, tag, a, b)
            vector.wait_ge(s_rrow, 201)
            vss(vector, vector.tensor_scalar(rrow[:], rrow[:], 1e-8, None, ALU.add))
            vss(vector, vector.reciprocal(rrow[:], rrow[:]))
            vector.nop().then_inc(s_rrow, 1)
            for tt in range(PTILES):
                vector.wait_ge(s_rb2, 32)
                vector.wait_ge(s_hld, 16 * (tt + 1))
                rs = AP(rbc.ap().tensor, rbc.ap().offset + 128 * tt,
                        [rbc.ap().ap[0], [0, 32], [1, 128]])
                h3 = hcol.ap().rearrange("p (c n) -> p c n", n=128)
                vector.tensor_tensor(h3, h3, rs, ALU.mult).then_inc(s_hsc, 1)
                vector.wait_ge(s_dmm, tt + 1)
                par = tt % 2
                if tt >= 2:
                    vector.wait_ge(s_ost[par], 16 * ((tt - 2) // 2 + 1))
                vector.tensor_copy(outb[par][:, 0:512], pA[tt % 2][:])
                vector.tensor_copy(outb[par][:, 512:1024], pB[tt % 2][:]).then_inc(s_fout, 1)

        @block.scalar
        def _(scalar):
            scalar.wait_ge(s_smm, 2)
            sss(scalar, scalar.copy(acc8[0:1, 34:35], pS[0].ap()[0:1, 0:1]))
            sss(scalar, scalar.copy(acc8[0:1, 35:36], pS[1].ap()[0:1, 0:1]))
            sss(scalar, scalar.mul(acc8[0:1, 34:35], acc8[0:1, 34:35], 0.5 / (DM * FF)))
            scalar.mul(acc8[0:1, 35:36], acc8[0:1, 35:36], 0.5 / (DM * FF)).then_inc(s_sev, 2)
            scalar.wait_ge(s_ldxn, 16 * 8)
            for c in range(8):
                if c >= 1:
                    scalar.wait_ge(s_qmm, 4 * c)
                scalar.activation(sqb[:], xn_sb[:, TOK * c:TOK * (c + 1)],
                                  AF.Square).then_inc(s_sq, 1)
            scalar.wait_ge(s_qmm, 32)
            for q in range(4):
                sss(scalar, scalar.activation(rrow[:, 512 * q:512 * (q + 1)],
                                              pS[q].ap()[0:1, :], AF.Sqrt, scale=1.0 / DM))
            scalar.nop().then_inc(s_rrow, 100)
            scalar.nop().then_inc(s_r1d, 1)
            for k in range(128):
                par = k % 2
                scalar.wait_ge(s_up, k + 1)
                if k >= 2:
                    scalar.wait_ge(s_hst[par], 16 * ((k - 2) // 2 + 1))
                scalar.activation(h_t[par][:], pA[k % 2][:], AF.Gelu).then_inc(s_gel, 1)
                if k >= 2:
                    scalar.wait_ge(s_h2m, k - 1)
                scalar.activation(sqb[:, 512 * par:512 * (par + 1)], h_t[par][:],
                                  AF.Square).then_inc(s_h2, 1)
            scalar.wait_ge(s_h2m, 128)
            scalar.wait_ge(s_rb1, 16)
            for q in range(4):
                sss(scalar, scalar.activation(rrow[:, 512 * q:512 * (q + 1)],
                                              pS[q].ap()[0:1, :], AF.Sqrt, scale=1.0 / FF))
            scalar.nop().then_inc(s_rrow, 100)

        @block.tensor
        def _(tensor):
            tensor.wait_ge(s_abs, 33)
            tensor.wait_ge(s_ones, 1)
            tensor.matmul(pS[0].ap()[0:1, 0:1], onef[:], acc8[:, 32:33], start=True, stop=True)
            tensor.matmul(pS[1].ap()[0:1, 0:1], onef[:], acc8[:, 33:34],
                          start=True, stop=True).then_inc(s_smm, 2)
            for c in range(8):
                tensor.wait_ge(s_sq, c + 1)
                if c == 0:
                    tensor.wait_ge(s_sev, 2)
                for q in range(4):
                    mmq = tensor.matmul(pS[q].ap()[0:1, :], onesb[:], sqb[:, 512 * q:512 * (q + 1)],
                                        start=(c == 0), stop=(c == 7), skip_group_check=True)
                mmq.then_inc(s_qmm, 4)
            for k in range(128):
                fc, tc = divmod(k, 4)
                if fc == 0 and tc == 0:
                    tensor.wait_ge(s_tern, 16)
                    tensor.wait_ge(s_xno, 8)
                if k >= 2:
                    tensor.wait_ge(s_gel, k - 1)
                for c in range(8):
                    mm = tensor.matmul(pA[k % 2][:],
                                       arena[:, 4096 * c + 128 * fc:4096 * c + 128 * (fc + 1)],
                                       xn_sb[:, TOK * c + 512 * tc:TOK * c + 512 * (tc + 1)],
                                       start=(c == 0), stop=(c == 7))
                mm.then_inc(s_up, 1)
                tensor.wait_ge(s_h2, k + 1)
                if k < 4:
                    tensor.wait_ge(s_r1d, 1)
                tensor.matmul(pS[k % 4].ap()[0:1, :], onesb[:],
                              sqb[:, 512 * (k % 2):512 * (k % 2 + 1)],
                              start=(k // 4 == 0), stop=(k // 4 == 31),
                              skip_group_check=True).then_inc(s_h2m, 1)
            tensor.wait_ge(s_tern, 32)
            for tt in range(PTILES):
                tensor.wait_ge(s_hsc, tt + 1)
                if tt >= 2:
                    tensor.wait_ge(s_fout, tt - 1)
                for fc in range(32):
                    lhs = hcol.ap()[:, 128 * fc:128 * (fc + 1)]
                    tensor.matmul(pA[tt % 2][:], lhs, arena[:, 1024 * fc + 0:1024 * fc + 512],
                                  start=(fc == 0), stop=(fc == 31), skip_group_check=True)
                    mmd = tensor.matmul(pB[tt % 2][:], lhs, arena[:, 1024 * fc + 512:1024 * (fc + 1)],
                                        start=(fc == 0), stop=(fc == 31), skip_group_check=True)
                mmd.then_inc(s_dmm, 1)

    return nc




def build_l1fin():
    nc = bass.Bass("TRN2", target_bir_lowering=False)
    d_yf = nc.dram_tensor("yf", [DIN, TOK], BF16, kind="ExternalInput").ap()
    d_yb = nc.dram_tensor("yb", [DIN, TOK], BF16, kind="ExternalInput").ap()
    d_owf = nc.dram_tensor("outwT_f", [DIN, DM], F32, kind="ExternalInput").ap()
    d_owb = nc.dram_tensor("outwT_b", [DIN, DM], F32, kind="ExternalInput").ap()
    d_x1 = nc.dram_tensor("x1", [TOK, DM], F32, kind="ExternalInput").ap()
    d_ffn = nc.dram_tensor("ffn", [TOK, DM], F32, kind="ExternalInput").ap()
    o_out = nc.dram_tensor("out", [TOK, DM], F32, kind="ExternalOutput").ap()

    owsb = [nc.alloc_sbuf_tensor(f"owsb{i}", [128, 16 * DM], BF16) for i in range(2)]
    yfc = [nc.alloc_sbuf_tensor(f"yfc{i}", [128, 16 * 128], BF16) for i in range(2)]
    ybc = [nc.alloc_sbuf_tensor(f"ybc{i}", [128, 16 * 128], BF16) for i in range(2)]
    x1t = [nc.alloc_sbuf_tensor(f"x1t{i}", [128, DM], F32) for i in range(2)]
    fft = [nc.alloc_sbuf_tensor(f"fft{i}", [128, DM], F32) for i in range(2)]
    t1b = nc.alloc_sbuf_tensor("t1b", [128, DM], F32)
    outb = [nc.alloc_sbuf_tensor(f"outb{i}", [128, DM], F32) for i in range(2)]

    pA = [nc.alloc_psum_tensor(f"pA{i}", [128, 512], F32) for i in range(2)]
    pB = [nc.alloc_psum_tensor(f"pB{i}", [128, 512], F32) for i in range(2)]

    s_owl = nc.alloc_semaphore("s_owl")
    s_yfl = [nc.alloc_semaphore(f"s_yfl{i}") for i in range(2)]
    s_ybl = [nc.alloc_semaphore(f"s_ybl{i}") for i in range(2)]
    s_x1l = [nc.alloc_semaphore(f"s_x1l{i}") for i in range(2)]
    s_ffl = [nc.alloc_semaphore(f"s_ffl{i}") for i in range(2)]
    s_mm = nc.alloc_semaphore("s_mm")
    s_fo = nc.alloc_semaphore("s_fo")
    s_ost = [nc.alloc_semaphore(f"s_ost{i}") for i in range(2)]
    s_vself = nc.alloc_semaphore("s_vself")
    vctr = [0]

    def vss(vector, inst):
        vctr[0] += 1
        inst.then_inc(s_vself, 1)
        vector.wait_ge(s_vself, vctr[0])

    with nc.Block() as block:
        @block.sync
        def _(sync):
            for tt in range(PTILES + 2):
                if tt < PTILES:
                    par = tt % 2
                    if tt >= 2:
                        sync.wait_ge(s_mm, tt - 1)
                    sync.dma_start(out=yfc[par].ap().rearrange("p (c n) -> p c n", n=128),
                                   in_=d_yf[:, 128 * tt:128 * (tt + 1)].rearrange(
                                       "(c p) n -> p c n", p=128)).then_inc(s_yfl[par], 16)
                    sync.dma_start(out=ybc[par].ap().rearrange("p (c n) -> p c n", n=128),
                                   in_=d_yb[:, 128 * tt:128 * (tt + 1)].rearrange(
                                       "(c p) n -> p c n", p=128)).then_inc(s_ybl[par], 16)
                    if tt >= 2:
                        sync.wait_ge(s_fo, tt - 1)
                    sync.dma_start(out=x1t[par][:], in_=d_x1[128 * tt:128 * (tt + 1), :]
                                   ).then_inc(s_x1l[par], 16)
                    sync.dma_start(out=fft[par][:], in_=d_ffn[128 * tt:128 * (tt + 1), :]
                                   ).then_inc(s_ffl[par], 16)
                if tt >= 2:
                    jj = tt - 2
                    sync.wait_ge(s_fo, jj + 1)
                    sync.dma_start(out=o_out[128 * jj:128 * (jj + 1), :],
                                   in_=outb[jj % 2][:]).then_inc(s_ost[jj % 2], 16)

        @block.gpsimd
        def _(gpsimd):
            gpsimd.dma_start(out=owsb[0].ap().rearrange("p (c n) -> p c n", n=DM),
                             in_=d_owf.rearrange("(c p) n -> p c n", p=128)).then_inc(s_owl, 16)
            gpsimd.dma_start(out=owsb[1].ap().rearrange("p (c n) -> p c n", n=DM),
                             in_=d_owb.rearrange("(c p) n -> p c n", p=128)).then_inc(s_owl, 16)
            gpsimd.wait_ge(s_ost[0], 16 * (PTILES // 2))
            gpsimd.wait_ge(s_ost[1], 16 * (PTILES // 2))

        @block.tensor
        def _(tensor):
            tensor.wait_ge(s_owl, 32)
            for tt in range(PTILES):
                par = tt % 2
                tensor.wait_ge(s_yfl[par], 16 * (tt // 2 + 1))
                tensor.wait_ge(s_ybl[par], 16 * (tt // 2 + 1))
                if tt >= 2:
                    tensor.wait_ge(s_fo, tt - 1)
                for dd in range(2):
                    ysrc = yfc[par] if dd == 0 else ybc[par]
                    for c in range(16):
                        lhs = ysrc.ap()[:, 128 * c:128 * (c + 1)]
                        first = (dd == 0 and c == 0)
                        last = (dd == 1 and c == 15)
                        tensor.matmul(pA[par][:], lhs, owsb[dd][:, DM * c + 0:DM * c + 512],
                                      start=first, stop=last, skip_group_check=True)
                        mml = tensor.matmul(pB[par][:], lhs, owsb[dd][:, DM * c + 512:DM * (c + 1)],
                                            start=first, stop=last, skip_group_check=True)
                mml.then_inc(s_mm, 1)

        @block.vector
        def _(vector):
            for tt in range(PTILES):
                par = tt % 2
                vector.wait_ge(s_mm, tt + 1)
                vector.wait_ge(s_x1l[par], 16 * (tt // 2 + 1))
                vector.wait_ge(s_ffl[par], 16 * (tt // 2 + 1))
                if tt >= 2:
                    vector.wait_ge(s_ost[par], 16 * ((tt - 2) // 2 + 1))
                vss(vector, vector.tensor_tensor(t1b[:], x1t[par][:], fft[par][:], ALU.add))
                vss(vector, vector.tensor_tensor(outb[par][:, 0:512], t1b[:, 0:512], pA[par][:], ALU.add))
                vss(vector, vector.tensor_tensor(outb[par][:, 512:1024], t1b[:, 512:1024],
                                                 pB[par][:], ALU.add))
                vector.nop().then_inc(s_fo, 1)

        @block.scalar
        def _(scalar):
            scalar.nop()

    return nc



def _rows(j):
    return np.concatenate([np.arange(128 * t, 128 * (t + 1)) for t in zig_tiles(j)])


def kernel(**inputs):
    C = np.ascontiguousarray
    x = C(inputs["L0_attn_x"])
    rows_of = [_rows(j) for j in range(4)]

    nc = build_prep()
    maps = []
    for cid in range(8):
        g, j = divmod(cid, 4)
        maps.append({
            "x": C(x[g][rows_of[j]]),
            "wq": C(inputs["L0_attn_w_q"]), "wk": C(inputs["L0_attn_w_k"]),
            "wv1": C(inputs["L0_attn_w_v1"]), "wv2": C(inputs["L0_attn_w_v2"]),
        })
    r1 = run_bass_kernel_spmd(nc, maps, core_ids=list(range(8))).results

    Kf = [np.zeros((KD, T), dtype=BF) for _ in range(2)]
    Vf = [np.zeros((T, DM), dtype=BF) for _ in range(2)]
    for cid in range(8):
        g, j = divmod(cid, 4)
        Kf[g][:, rows_of[j]] = np.asarray(r1[cid]["k_fm"])
        Vf[g][rows_of[j]] = np.asarray(r1[cid]["v_tm"])

    nc = build_l0att()
    maps = []
    for cid in range(8):
        g, j = divmod(cid, 4)
        maps.append({
            "q_fm": np.asarray(r1[cid]["q_fm"]),
            "Kfull": C(Kf[g]), "Vfull": C(Vf[g]),
            "x": C(x[g][rows_of[j]]),
            "dpos": np.array([[128.0 * t for t in zig_tiles(j)]], np.float32),
        })
    r2 = run_bass_kernel_spmd(nc, maps, core_ids=list(range(8))).results

    nc = build_l0moe()
    maps = []
    for cid in range(8):
        g, j = divmod(cid, 4)
        maps.append({
            "xn_fm": np.asarray(r1[cid]["xn_fm"]),
            "xatt": np.asarray(r2[cid]["xatt"]),
            "router_w": C(inputs["L0_ffn_router_w"]),
            "router_b": C(np.asarray(inputs["L0_ffn_router_b"])[None, :]),
            "we_gate": C(inputs["L0_ffn_we_gate"]),
            "we_up": C(inputs["L0_ffn_we_up"]),
            "we_down": C(inputs["L0_ffn_we_down"]),
        })
    r3 = run_bass_kernel_spmd(nc, maps, core_ids=list(range(8))).results

    nc = build_l1a()
    maps = [{"x1": np.asarray(r3[cid]["x1"])} for cid in range(8)]
    r4 = run_bass_kernel_spmd(nc, maps, core_ids=list(range(8))).results

    xn1f = [np.zeros((DM, T), dtype=BF) for _ in range(2)]
    for cid in range(8):
        g, j = divmod(cid, 4)
        xn1f[g][:, rows_of[j]] = np.asarray(r4[cid]["xn1_fm"])

    fw = np.asarray(inputs["L1_attn_f_in_proj_w"])
    bw = np.asarray(inputs["L1_attn_b_in_proj_w"])
    nc = build_m1()
    maps = []
    for cid in range(8):
        g, j = divmod(cid, 4)
        sl = slice(512 * j, 512 * (j + 1))
        w_in_T = np.concatenate([
            fw[512 * j:512 * (j + 1)].T, fw[2048 + 512 * j:2048 + 512 * (j + 1)].T,
            bw[512 * j:512 * (j + 1)].T, bw[2048 + 512 * j:2048 + 512 * (j + 1)].T], axis=1)
        maps.append({
            "xn1f": C(xn1f[g]),
            "w_in_T": C(w_in_T.astype(np.float32)),
            "conv_w": C(np.concatenate([np.asarray(inputs["L1_attn_f_conv_w"])[sl],
                                        np.asarray(inputs["L1_attn_b_conv_w"])[sl]]).astype(np.float32)),
            "conv_b": C(np.concatenate([np.asarray(inputs["L1_attn_f_conv_b"])[sl],
                                        np.asarray(inputs["L1_attn_b_conv_b"])[sl]])[:, None].astype(np.float32)),
            "xproj_T": C(np.concatenate([np.asarray(inputs["L1_attn_f_x_proj_w"]).T[sl],
                                         np.asarray(inputs["L1_attn_b_x_proj_w"]).T[sl]]).astype(np.float32)),
        })
    r5 = run_bass_kernel_spmd(nc, maps, core_ids=list(range(8))).results

    nc = build_m2()
    maps = []
    for cid in range(8):
        g, j = divmod(cid, 4)
        sl = slice(512 * j, 512 * (j + 1))
        partners = [4 * g + jj for jj in range(4) if jj != j]
        m = {
            "xc": np.asarray(r5[cid]["xc"]), "sz": np.asarray(r5[cid]["sz"]),
            "xdb_a": np.asarray(r5[cid]["xdb"]),
            "dt_wT": C(np.concatenate([np.asarray(inputs["L1_attn_f_dt_proj_w"]).T[:, sl],
                                       np.asarray(inputs["L1_attn_b_dt_proj_w"]).T[:, sl]],
                                      axis=1).astype(np.float32)),
            "dt_b": C(np.concatenate([np.asarray(inputs["L1_attn_f_dt_proj_b"])[sl],
                                      np.asarray(inputs["L1_attn_b_dt_proj_b"])[sl]])[:, None].astype(np.float32)),
            "A_log": C(np.concatenate([np.asarray(inputs["L1_attn_f_A_log"])[sl],
                                       np.asarray(inputs["L1_attn_b_A_log"])[sl]]).astype(np.float32)),
            "Dp": C(np.concatenate([np.asarray(inputs["L1_attn_f_D"])[sl],
                                    np.asarray(inputs["L1_attn_b_D"])[sl]])[:, None].astype(np.float32)),
        }
        for mi, pc in enumerate(partners):
            m[f"xdb_p{mi}"] = np.asarray(r5[pc]["xdb"])
        maps.append(m)
    r6 = run_bass_kernel_spmd(nc, maps, core_ids=list(range(8))).results

    Yf = [np.zeros((DIN, T), dtype=BF) for _ in range(2)]
    Yb = [np.zeros((DIN, T), dtype=BF) for _ in range(2)]
    for cid in range(8):
        g, j = divmod(cid, 4)
        yg = np.asarray(r6[cid]["yg"])
        Yf[g][512 * j:512 * (j + 1)] = yg[0:512]
        Yb[g][512 * j:512 * (j + 1)] = yg[512:1024][:, ::-1]

    nc = build_l1ffn()
    maps = []
    for cid in range(8):
        g, j = divmod(cid, 4)
        maps.append({
            "w_upT": C(np.asarray(inputs["L1_ffn_w_up"]).T.astype(np.float32)),
            "w_downT": C(np.asarray(inputs["L1_ffn_w_down"]).T.astype(np.float32)),
            "xn1_fm": np.asarray(r4[cid]["xn1_fm"]),
        })
    r7 = run_bass_kernel_spmd(nc, maps, core_ids=list(range(8))).results

    nc = build_l1fin()
    maps = []
    for cid in range(8):
        g, j = divmod(cid, 4)
        maps.append({
            "yf": C(Yf[g][:, rows_of[j]]),
            "yb": C(Yb[g][:, rows_of[j]]),
            "outwT_f": C(np.asarray(inputs["L1_attn_f_out_proj_w"]).T.astype(np.float32)),
            "outwT_b": C(np.asarray(inputs["L1_attn_b_out_proj_w"]).T.astype(np.float32)),
            "x1": np.asarray(r3[cid]["x1"]),
            "ffn": np.asarray(r7[cid]["ffn"]),
        })
    r8 = run_bass_kernel_spmd(nc, maps, core_ids=list(range(8))).results

    out = np.zeros((2, T, DM), np.float32)
    for cid in range(8):
        g, j = divmod(cid, 4)
        out[g][rows_of[j]] = np.asarray(r8[cid]["out"])
    return out


# version 07 (final)
# speedup vs baseline: 1.0142x; 1.0142x over previous
import numpy as np
import ml_dtypes
import concourse.bass as bass
import concourse.mybir as mybir
from concourse.bass import AP
from concourse.bass_utils import run_bass_kernel_spmd

F32 = mybir.dt.float32
BF16 = mybir.dt.bfloat16
AF = mybir.ActivationFunctionType
ALU = mybir.AluOpType
AX = mybir.AxisListType.X

B, T, DM = 2, 8192, 1024
NTILE = T // 128
PTILES = 16
TOK = PTILES * 128
KD = 64
NE = 8
DIN = 2048
CH = 512
DST = 16
DTR = 64
FF = 4096
NEG = -1e30

BF = ml_dtypes.bfloat16


def zig_tiles(j):
    out = []
    for m in range(8):
        out.append(8 * m + j)
        out.append(8 * m + (7 - j))
    return sorted(out)


def att_slots():
    slots = []
    for m in range(8):
        slots.append(("lo", m, 8 * m + 8))
        slots.append(("hi", m, 8 * m + 8))
    return slots


def build_ident(nc, gp, vec, ident, iot, sem):
    gp.iota(iot[:], pattern=[[1, 128]], base=0, channel_multiplier=-1,
            allow_small_or_imprecise_dtypes=True).then_inc(sem, 1)
    vec.wait_ge(sem, 1)
    vec.tensor_scalar(ident[:], iot[:], 0.0, None, ALU.is_equal).then_inc(sem, 1)



def build_prep():
    nc = bass.Bass("TRN2", target_bir_lowering=False)
    d_x = nc.dram_tensor("x", [TOK, DM], F32, kind="ExternalInput").ap()
    d_wq = nc.dram_tensor("wq", [DM, KD], F32, kind="ExternalInput").ap()
    d_wk = nc.dram_tensor("wk", [DM, KD], F32, kind="ExternalInput").ap()
    d_wv1 = nc.dram_tensor("wv1", [DM, KD], F32, kind="ExternalInput").ap()
    d_wv2 = nc.dram_tensor("wv2", [KD, DM], F32, kind="ExternalInput").ap()
    o_xnfm = nc.dram_tensor("xn_fm", [DM, TOK], BF16, kind="ExternalOutput").ap()
    o_q = nc.dram_tensor("q_fm", [KD, TOK], BF16, kind="ExternalOutput").ap()
    o_k = nc.dram_tensor("k_fm", [KD, TOK], BF16, kind="ExternalOutput").ap()
    o_v = nc.dram_tensor("v_tm", [TOK, DM], BF16, kind="ExternalOutput").ap()

    xt = [nc.alloc_sbuf_tensor(f"xt{i}", [128, DM], F32) for i in range(2)]
    sq = nc.alloc_sbuf_tensor("sq", [128, DM], F32)
    sq2 = nc.alloc_sbuf_tensor("sq2", [128, DM], F32)
    cols = nc.alloc_sbuf_tensor("cols", [128, 16], F32)
    xn_tm = [nc.alloc_sbuf_tensor(f"xntm{i}", [128, DM], BF16) for i in range(2)]
    xnfm = nc.alloc_sbuf_tensor("xnfm", [128, 8 * TOK], BF16)
    wq_sb = nc.alloc_sbuf_tensor("wq_sb", [128, 8 * KD], BF16)
    wk_sb = nc.alloc_sbuf_tensor("wk_sb", [128, 8 * KD], BF16)
    wv1_sb = nc.alloc_sbuf_tensor("wv1_sb", [128, 8 * KD], BF16)
    wv2_sb = nc.alloc_sbuf_tensor("wv2_sb", [KD, DM], BF16)
    q_sb = nc.alloc_sbuf_tensor("q_sb", [KD, TOK], BF16)
    k_sb = nc.alloc_sbuf_tensor("k_sb", [KD, TOK], BF16)
    v1_sb = nc.alloc_sbuf_tensor("v1_sb", [KD, TOK], BF16)
    vst = [nc.alloc_sbuf_tensor(f"vst{i}", [128, DM], BF16) for i in range(2)]
    ident = nc.alloc_sbuf_tensor("ident", [128, 128], BF16)
    iot = nc.alloc_sbuf_tensor("iot", [128, 128], F32)

    p_tr = [nc.alloc_psum_tensor(f"p_tr{i}", [128, 128], BF16) for i in range(2)]
    p_q = nc.alloc_psum_tensor("p_q", [KD, 512], F32)
    p_k = nc.alloc_psum_tensor("p_k", [KD, 512], F32)
    p_v1 = nc.alloc_psum_tensor("p_v1", [KD, 512], F32)
    p_v = [nc.alloc_psum_tensor(f"p_vv{i}", [128, 512], F32) for i in range(2)]

    s_xld = [nc.alloc_semaphore(f"s_xld{i}") for i in range(2)]
    s_wld = nc.alloc_semaphore("s_wld")
    s_id = nc.alloc_semaphore("s_id")
    s_v2s = nc.alloc_semaphore("s_v2s")
    s_s2v = nc.alloc_semaphore("s_s2v")
    s_xn = nc.alloc_semaphore("s_xn")
    s_tr = nc.alloc_semaphore("s_tr")
    s_ev = nc.alloc_semaphore("s_ev")
    s_pj = nc.alloc_semaphore("s_pj")
    s_pe = nc.alloc_semaphore("s_pe")
    s_st = nc.alloc_semaphore("s_st")
    s_sq = nc.alloc_semaphore("s_sq")
    s_vst = [nc.alloc_semaphore(f"s_vst{i}") for i in range(2)]
    s_vself = nc.alloc_semaphore("s_vself")
    vctr = [0]

    def vss(vector, inst):
        vctr[0] += 1
        inst.then_inc(s_vself, 1)
        vector.wait_ge(s_vself, vctr[0])

    def evac_tile(scalar, i):
        for c in range(8):
            n = 8 * i + c
            scalar.wait_ge(s_tr, n + 1)
            scalar.copy(xnfm[:, TOK * c + 128 * i: TOK * c + 128 * (i + 1)],
                        p_tr[n % 2][:]).then_inc(s_ev, 1)

    with nc.Block() as block:
        @block.sync
        def _(sync):
            for i in range(PTILES):
                if i >= 2:
                    sync.wait_ge(s_xn, i - 1)
                    sync.wait_ge(s_sq, i - 1)
                sync.dma_start(out=xt[i % 2][:],
                               in_=d_x[128 * i:128 * (i + 1), :]).then_inc(s_xld[i % 2], 16)
            sync.wait_ge(s_ev, 128)
            for c in range(8):
                sync.dma_start(out=o_xnfm[128 * c:128 * (c + 1), :],
                               in_=xnfm[:, TOK * c:TOK * (c + 1)]).then_inc(s_st, 16)
            sync.wait_ge(s_pe, 12)
            sync.dma_start(out=o_q, in_=q_sb[:]).then_inc(s_st, 16)
            sync.dma_start(out=o_k, in_=k_sb[:]).then_inc(s_st, 16)

        @block.gpsimd
        def _(gpsimd):
            for (dst, srcw) in ((wq_sb, d_wq), (wk_sb, d_wk), (wv1_sb, d_wv1)):
                gpsimd.dma_start(
                    out=dst.ap().rearrange("p (c n) -> p c n", n=KD),
                    in_=srcw.rearrange("(c p) n -> p c n", p=128),
                ).then_inc(s_wld, 16)
            gpsimd.dma_start(out=wv2_sb[:], in_=d_wv2).then_inc(s_wld, 16)
            gpsimd.iota(iot[:], pattern=[[1, 128]], base=0, channel_multiplier=-1,
                        allow_small_or_imprecise_dtypes=True).then_inc(s_id, 1)
            for i in range(PTILES):
                gpsimd.wait_ge(s_pe, 12 + i + 1)
                gpsimd.dma_start(out=o_v[128 * i:128 * (i + 1), :],
                                 in_=vst[i % 2][:]).then_inc(s_vst[i % 2], 16)
            gpsimd.wait_ge(s_st, 16 * 10)
            gpsimd.wait_ge(s_vst[0], 16 * (PTILES // 2))
            gpsimd.wait_ge(s_vst[1], 16 * (PTILES // 2))

        @block.vector
        def _(vector):
            vector.wait_ge(s_id, 1)
            vector.tensor_scalar(ident[:], iot[:], 0.0, None, ALU.is_equal).then_inc(s_id, 1)
            for i in range(PTILES):
                t = xt[i % 2]
                b = 8 * (i % 2)
                vector.wait_ge(s_xld[i % 2], 16 * (i // 2 + 1))
                vss(vector, vector.reduce_sum(cols[:, b + 0:b + 1], t[:], AX))
                vector.tensor_tensor(sq2[:], t[:], t[:], ALU.mult)
                vss(vector, vector.reduce_sum(cols[:, b + 1:b + 2], sq2[:], AX))
                vss(vector, vector.tensor_scalar(cols[:, b + 2:b + 3], cols[:, b + 0:b + 1], 1.0 / DM, None, ALU.mult))
                vss(vector, vector.tensor_scalar(cols[:, b + 3:b + 4], cols[:, b + 1:b + 2], 1.0 / DM, None, ALU.mult))
                vss(vector, vector.tensor_tensor(cols[:, b + 4:b + 5], cols[:, b + 2:b + 3], cols[:, b + 2:b + 3], ALU.mult))
                vss(vector, vector.tensor_tensor(cols[:, b + 5:b + 6], cols[:, b + 3:b + 4], cols[:, b + 4:b + 5], ALU.subtract))
                vector.tensor_scalar(cols[:, b + 5:b + 6], cols[:, b + 5:b + 6], 1e-5, None, ALU.add).then_inc(s_v2s, 1)
                vector.wait_ge(s_s2v, i + 1)
                vss(vector, vector.reciprocal(cols[:, b + 7:b + 8], cols[:, b + 6:b + 7]))
                if i >= 2:
                    vector.wait_ge(s_tr, 8 * (i - 1))
                vector.tensor_scalar(xn_tm[i % 2][:], t[:], cols[:, b + 2:b + 3], cols[:, b + 7:b + 8],
                                     ALU.subtract, ALU.mult).then_inc(s_xn, 1)

        @block.scalar
        def _(scalar):
            for i in range(PTILES):
                b = 8 * (i % 2)
                scalar.nop().then_inc(s_sq, 1)
                scalar.wait_ge(s_v2s, i + 1)
                scalar.activation(cols[:, b + 6:b + 7], cols[:, b + 5:b + 6], AF.Sqrt).then_inc(s_s2v, 1)
                if i >= 1:
                    evac_tile(scalar, i - 1)
            evac_tile(scalar, PTILES - 1)
            for jj in range(4):
                for idx, (ps, dst) in enumerate(((p_q, q_sb), (p_k, k_sb), (p_v1, v1_sb))):
                    scalar.wait_ge(s_pj, 3 * jj + idx + 1)
                    scalar.copy(dst.ap()[:, 512 * jj:512 * (jj + 1)], ps[:]).then_inc(s_pe, 1)
            for i in range(PTILES):
                scalar.wait_ge(s_pj, 12 + 2 * (i + 1))
                if i >= 2:
                    scalar.wait_ge(s_vst[i % 2], 16 * ((i - 2) // 2 + 1))
                scalar.copy(vst[i % 2][:, 0:512], p_v[0][:])
                scalar.copy(vst[i % 2][:, 512:1024], p_v[1][:]).then_inc(s_pe, 1)

        @block.tensor
        def _(tensor):
            tensor.wait_ge(s_id, 2)
            for i in range(PTILES):
                tensor.wait_ge(s_xn, i + 1)
                for c in range(8):
                    n = 8 * i + c
                    if n >= 2:
                        tensor.wait_ge(s_ev, n - 1)
                    tensor.transpose(p_tr[n % 2][:], xn_tm[i % 2][:, 128 * c:128 * (c + 1)],
                                     ident[:]).then_inc(s_tr, 1)
            tensor.wait_ge(s_wld, 16 * 4)
            tensor.wait_ge(s_ev, 128)
            for jj in range(4):
                for idx, (wsb, ps) in enumerate(((wq_sb, p_q), (wk_sb, p_k), (wv1_sb, p_v1))):
                    if jj >= 1:
                        tensor.wait_ge(s_pe, 3 * (jj - 1) + idx + 1)
                    for c in range(8):
                        mm = tensor.matmul(
                            ps[:], wsb[:, KD * c:KD * (c + 1)],
                            xnfm[:, TOK * c + 512 * jj:TOK * c + 512 * (jj + 1)],
                            start=(c == 0), stop=(c == 7))
                    mm.then_inc(s_pj, 1)
            for i in range(PTILES):
                tensor.wait_ge(s_pe, 12)
                if i >= 1:
                    tensor.wait_ge(s_pe, 12 + i)
                for u in range(2):
                    tensor.matmul(p_v[u][:], v1_sb[:, 128 * i:128 * (i + 1)],
                                  wv2_sb[:, 512 * u:512 * (u + 1)],
                                  start=True, stop=True).then_inc(s_pj, 1)

    return nc



VRES = 32


def att_slot_list(j):
    out = []
    for m in range(8):
        for t in (8 * m + j, 8 * m + (7 - j)):
            out.append((t, 8 * m + 8))
    out.sort()
    return out


def _att_sched():
    slots = att_slot_list(0)
    sched = []
    gchunk = gpb = gblk = gstream = 0
    stream_gblk = {}
    for si, (tau, kvt) in enumerate(slots):
        m8 = kvt - 8
        nA = (m8 * 128) // 512
        chunks = []
        for ch in range(nA + 2):
            isB = ch >= nA
            blks = []
            for b in range(4):
                v = ch * 4 + b
                sidx = None
                if v >= VRES:
                    sidx = gstream
                    stream_gblk[gstream] = gblk
                    gstream += 1
                blks.append(dict(v=v, gblk=gblk, stream=sidx))
                gblk += 1
            chunks.append(dict(ch=ch, isB=isB, base=ch * 512, gchunk=gchunk,
                               gpb=(gpb if isB else None), blks=blks))
            gchunk += 1
            if isB:
                gpb += 1
        sched.append(dict(si=si, tau=tau, kvt=kvt, m8=m8, nA=nA, chunks=chunks))
    return sched, stream_gblk


def build_l0att():
    nc = bass.Bass("TRN2", target_bir_lowering=False)
    d_q = nc.dram_tensor("q_fm", [KD, TOK], BF16, kind="ExternalInput").ap()
    d_K = nc.dram_tensor("Kfull", [KD, T], BF16, kind="ExternalInput").ap()
    d_V = nc.dram_tensor("Vfull", [T, DM], BF16, kind="ExternalInput").ap()
    d_x = nc.dram_tensor("x", [TOK, DM], F32, kind="ExternalInput").ap()
    d_dpos = nc.dram_tensor("dpos", [1, 16], F32, kind="ExternalInput").ap()
    o_xa = nc.dram_tensor("xatt", [TOK, DM], F32, kind="ExternalOutput").ap()
    h_row = nc.dram_tensor("h_row", [1, 16], F32, kind="Internal").ap()

    q_sb = nc.alloc_sbuf_tensor("q_sb", [KD, TOK], BF16)
    K_sb = nc.alloc_sbuf_tensor("K_sb", [KD, T], BF16)
    Vres = nc.alloc_sbuf_tensor("Vres", [128, VRES * DM], BF16)
    Vst = [nc.alloc_sbuf_tensor(f"Vst{i}", [128, DM], BF16) for i in range(2)]
    k2 = nc.alloc_sbuf_tensor("k2", [KD, 512], BF16)
    onesc = nc.alloc_sbuf_tensor("onesc", [KD, 1], BF16)
    knorm2 = nc.alloc_sbuf_tensor("knorm2", [1, 64], F32)
    kprefix = nc.alloc_sbuf_tensor("kprefix", [1, 16], F32)
    dpos_bc = nc.alloc_sbuf_tensor("dpos_bc", [128, 16], F32)
    kmax_bc = nc.alloc_sbuf_tensor("kmax_bc", [128, 16], F32)
    p_col = nc.alloc_sbuf_tensor("p_col", [128, 1], F32)
    jiota = nc.alloc_sbuf_tensor("jiota", [128, 512], F32)
    ident = nc.alloc_sbuf_tensor("ident", [128, 128], BF16)
    iot = nc.alloc_sbuf_tensor("iot", [128, 128], F32)
    qt2 = nc.alloc_sbuf_tensor("qt2", [128, KD], F32)
    ncols = nc.alloc_sbuf_tensor("ncols", [128, 16], F32)
    sig = nc.alloc_sbuf_tensor("sig", [128, 32], F32)
    dmask = nc.alloc_sbuf_tensor("dmask", [128, 512], F32)
    P_sb = [nc.alloc_sbuf_tensor(f"P_sb{i}", [128, 512], BF16) for i in range(2)]
    PT_sb = [nc.alloc_sbuf_tensor(f"PT_sb{i}", [128, 128], BF16) for i in range(2)]
    o_sb = nc.alloc_sbuf_tensor("o_sb", [128, DM], F32)
    xt = [nc.alloc_sbuf_tensor(f"xt{i}", [128, DM], F32) for i in range(2)]
    xa = [nc.alloc_sbuf_tensor(f"xa{i}", [128, DM], F32) for i in range(2)]

    p_S = [nc.alloc_psum_tensor(f"p_S{i}", [128, 512], F32) for i in range(2)]
    p_PT = [nc.alloc_psum_tensor(f"p_PT{i}", [128, 128], BF16) for i in range(2)]
    p_o = [nc.alloc_psum_tensor(f"p_o{i}", [128, 512], F32) for i in range(2)]
    p_qt = nc.alloc_psum_tensor("p_qt", [128, KD], BF16)
    p_kn = nc.alloc_psum_tensor("p_kn", [1, 512], F32)

    s_ldq = nc.alloc_semaphore("s_ldq")
    s_ldK = nc.alloc_semaphore("s_ldK")
    s_lddp = nc.alloc_semaphore("s_lddp")
    s_ldV = nc.alloc_semaphore("s_ldV")
    s_idg = nc.alloc_semaphore("s_idg")
    s_idv = nc.alloc_semaphore("s_idv")
    s_kn1 = nc.alloc_semaphore("s_kn1")
    s_kn2 = nc.alloc_semaphore("s_kn2")
    s_kn3 = nc.alloc_semaphore("s_kn3")
    s_kn5 = nc.alloc_semaphore("s_kn5")
    s_bc1 = nc.alloc_semaphore("s_bc1")
    s_bc2 = nc.alloc_semaphore("s_bc2")
    s_qt = nc.alloc_semaphore("s_qt")
    s_qs = nc.alloc_semaphore("s_qs")
    s_nc = nc.alloc_semaphore("s_nc")
    s_S = nc.alloc_semaphore("s_S")
    s_msk = nc.alloc_semaphore("s_msk")
    s_P = nc.alloc_semaphore("s_P")
    s_PT = nc.alloc_semaphore("s_PT")
    s_PTe = nc.alloc_semaphore("s_PTe")
    s_pv = nc.alloc_semaphore("s_pv")
    s_sg = nc.alloc_semaphore("s_sg")
    s_oev = nc.alloc_semaphore("s_oev")
    s_vld = [nc.alloc_semaphore(f"s_vld{i}") for i in range(2)]
    s_xld = [nc.alloc_semaphore(f"s_xld{i}") for i in range(2)]
    s_xst = [nc.alloc_semaphore(f"s_xst{i}") for i in range(2)]
    s_vself = nc.alloc_semaphore("s_vself")
    vctr = [0]

    def vss(vector, inst):
        vctr[0] += 1
        inst.then_inc(s_vself, 1)
        vector.wait_ge(s_vself, vctr[0])

    sched, stream_gblk = _att_sched()

    with nc.Block() as block:
        @block.sync
        def _(sync):
            sync.dma_start(out=q_sb[:], in_=d_q).then_inc(s_ldq, 16)
            sync.dma_start(out=K_sb[:], in_=d_K).then_inc(s_ldK, 16)
            sync.dma_start(out=dpos_bc[:], in_=AP(d_dpos.tensor, d_dpos.offset,
                                                  [[0, 128]] + d_dpos.ap[1:])).then_inc(s_lddp, 16)
            for g in range(8):
                sync.dma_start(
                    out=Vres.ap()[:, DM * 4 * g:DM * 4 * (g + 1)].rearrange("p (v n) -> p v n", n=DM),
                    in_=d_V[512 * g:512 * (g + 1), :].rearrange("(v p) n -> p v n", p=128),
                ).then_inc(s_ldV, 16)
            sync.wait_ge(s_kn5, 1)
            sync.dma_start(out=h_row, in_=kprefix[:]).then_inc(s_bc1, 16)
            sync.wait_ge(s_bc1, 16)
            sync.dma_start(out=kmax_bc[:], in_=AP(h_row.tensor, h_row.offset,
                                                  [[0, 128]] + h_row.ap[1:])).then_inc(s_bc2, 16)
            for s in sched:
                si = s["si"]
                if si >= 2:
                    sync.wait_ge(s_oev, si - 1)
                sync.dma_start(out=xt[si % 2][:],
                               in_=d_x[128 * si:128 * (si + 1), :]).then_inc(s_xld[si % 2], 16)
                for chd in s["chunks"]:
                    for bl in chd["blks"]:
                        if bl["stream"] is not None:
                            g = bl["stream"]
                            if g >= 2:
                                sync.wait_ge(s_pv, stream_gblk[g - 2] + 1)
                            sync.dma_start(out=Vst[g % 2][:],
                                           in_=d_V[128 * bl["v"]:128 * (bl["v"] + 1), :]
                                           ).then_inc(s_vld[g % 2], 16)
                if si >= 2:
                    jj = si - 2
                    sync.wait_ge(s_oev, jj + 1)
                    sync.dma_start(out=o_xa[128 * jj:128 * (jj + 1), :],
                                   in_=xa[jj % 2][:]).then_inc(s_xst[jj % 2], 16)
            for jj in (PTILES - 2, PTILES - 1):
                sync.wait_ge(s_oev, jj + 1)
                sync.dma_start(out=o_xa[128 * jj:128 * (jj + 1), :],
                               in_=xa[jj % 2][:]).then_inc(s_xst[jj % 2], 16)

        @block.gpsimd
        def _(gpsimd):
            gpsimd.iota(iot[:], pattern=[[1, 128]], base=0, channel_multiplier=-1,
                        allow_small_or_imprecise_dtypes=True)
            gpsimd.iota(jiota[:], pattern=[[1, 512]], base=0, channel_multiplier=0,
                        allow_small_or_imprecise_dtypes=True)
            gpsimd.iota(p_col[:], pattern=[[0, 1]], base=0, channel_multiplier=1,
                        allow_small_or_imprecise_dtypes=True)
            gpsimd.memset(onesc[:], 1.0).then_inc(s_idg, 4)
            gpsimd.wait_ge(s_xst[0], 16 * (PTILES // 2))
            gpsimd.wait_ge(s_xst[1], 16 * (PTILES // 2))

        @block.scalar
        def _(scalar):
            scalar.wait_ge(s_ldK, 16)
            for ch in range(16):
                if ch >= 1:
                    scalar.wait_ge(s_kn2, ch)
                scalar.activation(k2[:], K_sb[:, 512 * ch:512 * (ch + 1)],
                                  AF.Square).then_inc(s_kn1, 1)
            scalar.wait_ge(s_kn3, 32)
            scalar.activation(kprefix[:], kprefix[:], AF.Sqrt).then_inc(s_kn5, 1)
            for s in sched:
                si = s["si"]
                b = 8 * (si % 2)
                sb = 16 * (si % 2)
                scalar.wait_ge(s_qt, si + 1)
                if si >= 1:
                    scalar.wait_ge(s_nc, si)
                scalar.activation(qt2[:], p_qt[:], AF.Square,
                                  accum_out=ncols[:, b + 0:b + 1])
                scalar.activation(ncols[:, b + 1:b + 2], ncols[:, b + 0:b + 1],
                                  AF.Sqrt).then_inc(s_qs, 1)
                if si >= 2:
                    scalar.wait_ge(s_sg, si - 1)
                for chd in s["chunks"]:
                    g = chd["gchunk"]
                    if chd["isB"]:
                        scalar.wait_ge(s_msk, chd["gpb"] + 1)
                    else:
                        scalar.wait_ge(s_S, g + 1)
                    scalar.wait_ge(s_nc, si + 1)
                    if g >= 2:
                        scalar.wait_ge(s_PT, 4 * (g - 1))
                    scalar.activation(P_sb[g % 2][:], p_S[g % 2][:], AF.Exp,
                                      bias=ncols[:, b + 2:b + 3], scale=0.125,
                                      accum_out=sig[:, sb + chd["ch"]:sb + chd["ch"] + 1]
                                      ).then_inc(s_P, 1)
                    for bl in chd["blks"]:
                        gb = bl["gblk"]
                        scalar.wait_ge(s_PT, gb + 1)
                        scalar.copy(PT_sb[gb % 2][:], p_PT[gb % 2][:]).then_inc(s_PTe, 1)

        @block.vector
        def _(vector):
            vector.wait_ge(s_idg, 4)
            vector.tensor_scalar(ident[:], iot[:], 0.0, None, ALU.is_equal).then_inc(s_idv, 1)
            for ch in range(16):
                vector.wait_ge(s_kn2, ch + 1)
                vss(vector, vector.tensor_reduce(
                    knorm2[:, 4 * ch:4 * (ch + 1)],
                    p_kn.ap().rearrange("p (g s) -> p g s", s=128), AX, ALU.max))
                vector.nop().then_inc(s_kn3, 1)
            for s in sched:
                vss(vector, vector.reduce_max(kprefix[:, s["si"]:s["si"] + 1],
                                              knorm2[:, 0:s["kvt"]], AX))
                vector.nop().then_inc(s_kn3, 1)
            vector.wait_ge(s_bc2, 16)
            vector.wait_ge(s_lddp, 16)
            for s in sched:
                si = s["si"]
                b = 8 * (si % 2)
                sb = 16 * (si % 2)
                vector.wait_ge(s_qs, si + 1)
                vss(vector, vector.tensor_scalar(ncols[:, b + 2:b + 3], ncols[:, b + 1:b + 2],
                                                 kmax_bc[:, si:si + 1], -0.125, ALU.mult, ALU.mult))
                vss(vector, vector.tensor_tensor(ncols[:, b + 3:b + 4], dpos_bc[:, si:si + 1],
                                                 p_col[:], ALU.add))
                vector.nop().then_inc(s_nc, 1)
                for chd in s["chunks"]:
                    if not chd["isB"]:
                        continue
                    g = chd["gchunk"]
                    vss(vector, vector.tensor_scalar(ncols[:, b + 4:b + 5], ncols[:, b + 3:b + 4],
                                                     float(chd["base"]), None, ALU.subtract))
                    vss(vector, vector.tensor_scalar(dmask[:], jiota[:], ncols[:, b + 4:b + 5],
                                                     NEG, ALU.is_gt, ALU.mult))
                    vector.wait_ge(s_S, g + 1)
                    vector.tensor_tensor(p_S[g % 2][:], p_S[g % 2][:], dmask[:],
                                         ALU.add).then_inc(s_msk, 1)
                lastg = s["chunks"][-1]["gchunk"]
                nch = s["nA"] + 2
                vector.wait_ge(s_P, lastg + 1)
                vss(vector, vector.reduce_sum(ncols[:, b + 5:b + 6], sig[:, sb:sb + nch], AX))
                vss(vector, vector.reciprocal(ncols[:, b + 6:b + 7], ncols[:, b + 5:b + 6]))
                vector.nop().then_inc(s_sg, 1)
                vector.wait_ge(s_pv, s["chunks"][-1]["blks"][-1]["gblk"] + 1)
                vector.tensor_scalar(o_sb[:, 0:512], p_o[0][:], ncols[:, b + 6:b + 7], None, ALU.mult)
                vector.tensor_scalar(o_sb[:, 512:1024], p_o[1][:], ncols[:, b + 6:b + 7], None, ALU.mult)
                vector.wait_ge(s_xld[si % 2], 16 * (si // 2 + 1))
                vector.tensor_tensor(xa[si % 2][:], xt[si % 2][:], o_sb[:], ALU.add).then_inc(s_oev, 1)

        @block.tensor
        def _(tensor):
            tensor.wait_ge(s_idg, 4)
            tensor.wait_ge(s_idv, 1)
            tensor.wait_ge(s_ldq, 16)
            tensor.wait_ge(s_ldK, 16)
            tensor.wait_ge(s_ldV, 16 * 8)
            for ch in range(16):
                tensor.wait_ge(s_kn1, ch + 1)
                if ch >= 1:
                    tensor.wait_ge(s_kn3, ch)
                tensor.matmul(p_kn[:], onesc[:], k2[:], start=True, stop=True).then_inc(s_kn2, 1)
            for s in sched:
                si = s["si"]
                if si >= 1:
                    tensor.wait_ge(s_qs, si)
                tensor.transpose(p_qt[:], q_sb[:, 128 * si:128 * (si + 1)],
                                 ident[0:KD, 0:KD]).then_inc(s_qt, 1)
                firstgb = s["chunks"][0]["blks"][0]["gblk"]
                lastgb = s["chunks"][-1]["blks"][-1]["gblk"]
                for chd in s["chunks"]:
                    g = chd["gchunk"]
                    if g >= 2:
                        tensor.wait_ge(s_P, g - 1)
                    tensor.matmul(p_S[g % 2][:], q_sb[:, 128 * si:128 * (si + 1)],
                                  K_sb[:, chd["base"]:chd["base"] + 512],
                                  start=True, stop=True).then_inc(s_S, 1)
                    for bi, bl in enumerate(chd["blks"]):
                        gb = bl["gblk"]
                        tensor.wait_ge(s_P, g + 1)
                        if gb >= 2:
                            tensor.wait_ge(s_PTe, gb - 1)
                        tensor.transpose(p_PT[gb % 2][:],
                                         P_sb[g % 2][:, 128 * bi:128 * (bi + 1)],
                                         ident[:]).then_inc(s_PT, 1)
                        tensor.wait_ge(s_PTe, gb + 1)
                        if bl["stream"] is None:
                            vt = Vres.ap()[:, DM * bl["v"]:DM * (bl["v"] + 1)]
                        else:
                            gs = bl["stream"]
                            tensor.wait_ge(s_vld[gs % 2], 16 * (gs // 2 + 1))
                            vt = Vst[gs % 2].ap()[:, :]
                        if si >= 1 and gb == firstgb:
                            tensor.wait_ge(s_oev, si)
                        tensor.matmul(p_o[0][:], PT_sb[gb % 2][:], vt[:, 0:512],
                                      start=(gb == firstgb), stop=(gb == lastgb),
                                      skip_group_check=True)
                        tensor.matmul(p_o[1][:], PT_sb[gb % 2][:], vt[:, 512:1024],
                                      start=(gb == firstgb), stop=(gb == lastgb),
                                      skip_group_check=True).then_inc(s_pv, 1)

    return nc




def build_l0moe():
    nc = bass.Bass("TRN2", target_bir_lowering=False)
    d_xn = nc.dram_tensor("xn_fm", [DM, TOK], BF16, kind="ExternalInput").ap()
    d_xa = nc.dram_tensor("xatt", [TOK, DM], F32, kind="ExternalInput").ap()
    d_rw = nc.dram_tensor("router_w", [DM, NE], F32, kind="ExternalInput").ap()
    d_rb = nc.dram_tensor("router_b", [1, NE], F32, kind="ExternalInput").ap()
    d_wg = nc.dram_tensor("we_gate", [NE, DM, DM], F32, kind="ExternalInput").ap()
    d_wu = nc.dram_tensor("we_up", [NE, DM, DM], F32, kind="ExternalInput").ap()
    d_wd = nc.dram_tensor("we_down", [NE, DM, DM], F32, kind="ExternalInput").ap()
    o_x1 = nc.dram_tensor("x1", [TOK, DM], F32, kind="ExternalOutput").ap()
    h_y = nc.dram_tensor("h_y", [TOK, DM], F32, kind="Internal").ap()

    xnfm = nc.alloc_sbuf_tensor("xnfm", [128, 8 * TOK], BF16)
    rw_sb = nc.alloc_sbuf_tensor("rw_sb", [128, 8 * NE], BF16)
    rb_bc = nc.alloc_sbuf_tensor("rb_bc", [128, NE], F32)
    wg_sb = [nc.alloc_sbuf_tensor(f"wg_sb{i}", [128, 8 * DM], BF16) for i in range(2)]
    wu_sb = [nc.alloc_sbuf_tensor(f"wu_sb{i}", [128, 8 * DM], BF16) for i in range(2)]
    wd_sb = [nc.alloc_sbuf_tensor(f"wd_sb{i}", [128, 8 * DM], BF16) for i in range(2)]
    act_sb = nc.alloc_sbuf_tensor("act_sb", [128, 8 * TOK], BF16)
    g_tmp = [nc.alloc_sbuf_tensor(f"g_tmp{i}", [128, 512], BF16) for i in range(2)]
    mrow = nc.alloc_sbuf_tensor("mrow", [128, 64], F32)
    rcols = nc.alloc_sbuf_tensor("rcols", [128, 16], F32)
    Wall = nc.alloc_sbuf_tensor("Wall", [128, 8 * PTILES], F32)
    ybuf = [nc.alloc_sbuf_tensor(f"ybuf{i}", [128, DM], F32) for i in range(2)]
    xat = [nc.alloc_sbuf_tensor(f"xat{i}", [128, DM], F32) for i in range(2)]
    yt = [nc.alloc_sbuf_tensor(f"yt{i}", [128, DM], F32) for i in range(2)]
    x1b = [nc.alloc_sbuf_tensor(f"x1b{i}", [128, DM], F32) for i in range(2)]

    p_g = [nc.alloc_psum_tensor(f"p_g{i}", [128, 512], F32) for i in range(2)]
    p_u = [nc.alloc_psum_tensor(f"p_u{i}", [128, 512], F32) for i in range(2)]
    p_y = [nc.alloc_psum_tensor(f"p_y{i}", [128, 512], F32) for i in range(4)]
    p_r = p_g[0]

    s_ldxn = nc.alloc_semaphore("s_ldxn")
    s_ldr = nc.alloc_semaphore("s_ldr")
    s_ldrb = nc.alloc_semaphore("s_ldrb")
    s_r0 = nc.alloc_semaphore("s_r0")
    s_lg = nc.alloc_semaphore("s_lg")
    s_r1 = nc.alloc_semaphore("s_r1")
    s_r2 = nc.alloc_semaphore("s_r2")
    s_w = nc.alloc_semaphore("s_w")
    s_wld = [nc.alloc_semaphore(f"s_wld{i}") for i in range(2)]
    s_g = nc.alloc_semaphore("s_g")
    s_u = nc.alloc_semaphore("s_u")
    s_sl = nc.alloc_semaphore("s_sl")
    s_act = nc.alloc_semaphore("s_act")
    s_dn = nc.alloc_semaphore("s_dn")
    s_yw = nc.alloc_semaphore("s_yw")
    s_yacc = [nc.alloc_semaphore(f"s_yacc{i}") for i in range(PTILES)]
    s_xal = [nc.alloc_semaphore(f"s_xal{i}") for i in range(2)]
    s_yld = [nc.alloc_semaphore(f"s_yld{i}") for i in range(2)]
    s_fadd = nc.alloc_semaphore("s_fadd")
    s_x1st = [nc.alloc_semaphore(f"s_x1st{i}") for i in range(2)]
    s_vself = nc.alloc_semaphore("s_vself")
    vctr = [0]

    def vss(vector, inst):
        vctr[0] += 1
        inst.then_inc(s_vself, 1)
        vector.wait_ge(s_vself, vctr[0])

    NU = 32

    with nc.Block() as block:
        @block.sync
        def _(sync):
            for c in range(8):
                sync.dma_start(out=xnfm[:, TOK * c:TOK * (c + 1)],
                               in_=d_xn[128 * c:128 * (c + 1), :]).then_inc(s_ldxn, 16)
            sync.dma_start(out=rb_bc[:], in_=AP(d_rb.tensor, d_rb.offset,
                                                [[0, 128]] + d_rb.ap[1:])).then_inc(s_ldrb, 16)
            for i in range(PTILES + 2):
                if i < PTILES:
                    if i >= 2:
                        sync.wait_ge(s_fadd, i - 1)
                    sync.dma_start(out=xat[i % 2][:],
                                   in_=d_xa[128 * i:128 * (i + 1), :]).then_inc(s_xal[i % 2], 16)
                    for e in range(NE):
                        sync.wait_ge(s_yacc[i], 16 * NE)
                        break
                    sync.dma_start(out=yt[i % 2][:],
                                   in_=h_y[128 * i:128 * (i + 1), :]).then_inc(s_yld[i % 2], 16)
                if i >= 2:
                    jj = i - 2
                    sync.wait_ge(s_fadd, jj + 1)
                    sync.dma_start(out=o_x1[128 * jj:128 * (jj + 1), :],
                                   in_=x1b[jj % 2][:]).then_inc(s_x1st[jj % 2], 16)

        @block.gpsimd
        def _(gpsimd):
            gpsimd.dma_start(out=rw_sb.ap().rearrange("p (c n) -> p c n", n=NE),
                             in_=d_rw.rearrange("(c p) n -> p c n", p=128)).then_inc(s_ldr, 16)

            def wload(e):
                par = e % 2
                if e >= 2:
                    gpsimd.wait_ge(s_u, NU * (e - 1))
                    gpsimd.wait_ge(s_dn, PTILES * (e - 1))
                for (dst, srcw) in ((wg_sb[par], d_wg), (wu_sb[par], d_wu), (wd_sb[par], d_wd)):
                    gpsimd.dma_start(
                        out=dst.ap().rearrange("p (c n) -> p c n", n=DM),
                        in_=srcw[e].rearrange("(c p) n -> p c n", p=128),
                    ).then_inc(s_wld[par], 16)
            wload(0)
            wload(1)
            for e in range(NE):
                for tt in range(PTILES):
                    gpsimd.wait_ge(s_yw, PTILES * e + tt + 1)
                    if e >= 1:
                        gpsimd.wait_ge(s_yacc[tt], 16 * e)
                    gpsimd.dma_start(out=h_y[128 * tt:128 * (tt + 1), :],
                                     in_=ybuf[tt % 2][:],
                                     accum_op=(ALU.bypass if e == 0 else ALU.add)
                                     ).then_inc(s_yacc[tt], 16)
                if e + 2 < NE:
                    wload(e + 2)
            gpsimd.wait_ge(s_x1st[0], 16 * (PTILES // 2))
            gpsimd.wait_ge(s_x1st[1], 16 * (PTILES // 2))

        @block.scalar
        def _(scalar):
            for tt in range(PTILES):
                b = 8 * (tt % 2)
                scalar.wait_ge(s_r1, tt + 1)
                scalar.activation(rcols[:, b + 1:b + 2], rcols[:, b + 0:b + 1],
                                  AF.Sigmoid).then_inc(s_r2, 1)
            for e in range(NE):
                for u in range(NU):
                    gu = NU * e + u
                    scalar.wait_ge(s_g, gu + 1)
                    if gu >= 2:
                        scalar.wait_ge(s_sl, gu - 1)
                        scalar.wait_ge(s_act, gu - 1)
                    scalar.activation(g_tmp[gu % 2][:], p_g[gu % 2][:],
                                      AF.Silu).then_inc(s_sl, 1)

        @block.vector
        def _(vector):
            vector.wait_ge(s_ldrb, 16)
            for tt in range(PTILES):
                b = 8 * (tt % 2)
                mb = 32 * (tt % 2)
                vector.wait_ge(s_r0, tt + 1)
                vss(vector, vector.tensor_tensor(mrow[:, mb + 0:mb + 8], p_r.ap()[:, 0:NE], rb_bc[:], ALU.add))
                vector.nop().then_inc(s_lg, 1)
                lg = mrow.ap()[:, mb + 0:mb + 8]
                m1 = rcols.ap()[:, b + 2:b + 3]
                vss(vector, vector.reduce_max(m1, lg, AX))
                vss(vector, vector.tensor_scalar(mrow[:, mb + 8:mb + 16], lg, m1, None, ALU.is_equal))
                vss(vector, vector.scalar_tensor_tensor(mrow[:, mb + 16:mb + 24],
                                                        mrow[:, mb + 8:mb + 16], NEG, lg,
                                                        ALU.mult, ALU.add))
                m2 = rcols.ap()[:, b + 3:b + 4]
                vss(vector, vector.reduce_max(m2, mrow[:, mb + 16:mb + 24], AX))
                vss(vector, vector.tensor_tensor(rcols[:, b + 0:b + 1], m1, m2, ALU.subtract))
                vector.nop().then_inc(s_r1, 1)
                vector.wait_ge(s_r2, tt + 1)
                vss(vector, vector.tensor_scalar(rcols[:, b + 4:b + 5], rcols[:, b + 1:b + 2],
                                                 -1.0, 1.0, ALU.mult, ALU.add))
                vss(vector, vector.tensor_scalar(mrow[:, mb + 24:mb + 32],
                                                 mrow[:, mb + 16:mb + 24], m2, None, ALU.is_equal))
                vss(vector, vector.tensor_scalar(mrow[:, mb + 8:mb + 16],
                                                 mrow[:, mb + 8:mb + 16],
                                                 rcols[:, b + 1:b + 2], None, ALU.mult))
                vss(vector, vector.scalar_tensor_tensor(Wall[:, 8 * tt:8 * (tt + 1)],
                                                        mrow[:, mb + 24:mb + 32],
                                                        rcols[:, b + 4:b + 5],
                                                        mrow[:, mb + 8:mb + 16],
                                                        ALU.mult, ALU.add))
                vector.nop().then_inc(s_w, 1)
            for e in range(NE):
                for u in range(NU):
                    gu = NU * e + u
                    hc, tc = divmod(u, 4)
                    vector.wait_ge(s_sl, gu + 1)
                    vector.wait_ge(s_u, gu + 1)
                    if e >= 1 and u == 0:
                        vector.wait_ge(s_dn, PTILES * e)
                    vector.tensor_tensor(act_sb[:, TOK * hc + 512 * tc:TOK * hc + 512 * (tc + 1)],
                                         g_tmp[gu % 2][:], p_u[gu % 2][:],
                                         ALU.mult).then_inc(s_act, 1)
                for tt in range(PTILES):
                    gd = PTILES * e + tt
                    vector.wait_ge(s_dn, gd + 1)
                    vector.wait_ge(s_w, PTILES)
                    if gd >= 2:
                        vector.wait_ge(s_yacc[(tt - 2) % PTILES], 16 * (e + (1 if tt >= 2 else 0)))
                    wcol = Wall.ap()[:, 8 * tt + e:8 * tt + e + 1]
                    gp = 2 * (gd % 2)
                    vector.tensor_scalar(ybuf[tt % 2][:, 0:512], p_y[gp][:], wcol, None, ALU.mult)
                    vector.tensor_scalar(ybuf[tt % 2][:, 512:1024], p_y[gp + 1][:], wcol, None,
                                         ALU.mult).then_inc(s_yw, 1)
            for i in range(PTILES):
                vector.wait_ge(s_xal[i % 2], 16 * (i // 2 + 1))
                vector.wait_ge(s_yld[i % 2], 16 * (i // 2 + 1))
                if i >= 2:
                    vector.wait_ge(s_x1st[i % 2], 16 * ((i - 2) // 2 + 1))
                vector.tensor_tensor(x1b[i % 2][:], xat[i % 2][:], yt[i % 2][:],
                                     ALU.add).then_inc(s_fadd, 1)

        @block.tensor
        def _(tensor):
            tensor.wait_ge(s_ldxn, 16 * 8)
            tensor.wait_ge(s_ldr, 16)
            for tt in range(PTILES):
                if tt >= 1:
                    tensor.wait_ge(s_lg, tt)
                for c in range(8):
                    mm = tensor.matmul(p_r.ap()[:, 0:NE],
                                       xnfm[:, TOK * c + 128 * tt:TOK * c + 128 * (tt + 1)],
                                       rw_sb[:, NE * c:NE * (c + 1)],
                                       start=(c == 0), stop=(c == 7))
                mm.then_inc(s_r0, 1)
            for e in range(NE):
                par = e % 2
                tensor.wait_ge(s_wld[par], 48 * (e // 2 + 1))
                for u in range(NU):
                    gu = NU * e + u
                    hc, tc = divmod(u, 4)
                    if gu == 0:
                        tensor.wait_ge(s_lg, PTILES)
                    if gu >= 2:
                        tensor.wait_ge(s_sl, gu - 1)
                        tensor.wait_ge(s_act, gu - 1)
                    for c in range(8):
                        mm = tensor.matmul(
                            p_g[gu % 2][:], wg_sb[par][:, DM * c + 128 * hc:DM * c + 128 * (hc + 1)],
                            xnfm[:, TOK * c + 512 * tc:TOK * c + 512 * (tc + 1)],
                            start=(c == 0), stop=(c == 7))
                    mm.then_inc(s_g, 1)
                    for c in range(8):
                        mm = tensor.matmul(
                            p_u[gu % 2][:], wu_sb[par][:, DM * c + 128 * hc:DM * c + 128 * (hc + 1)],
                            xnfm[:, TOK * c + 512 * tc:TOK * c + 512 * (tc + 1)],
                            start=(c == 0), stop=(c == 7))
                    mm.then_inc(s_u, 1)
                for tt in range(PTILES):
                    gd = PTILES * e + tt
                    gp = 2 * (gd % 2)
                    tensor.wait_ge(s_act, NU * (e + 1))
                    if gd >= 2:
                        tensor.wait_ge(s_yw, gd - 1)
                    for hc in range(8):
                        srcA = act_sb.ap()[:, TOK * hc + 128 * tt:TOK * hc + 128 * (tt + 1)]
                        mm0 = tensor.matmul(p_y[gp][:], srcA,
                                            wd_sb[par][:, DM * hc + 0:DM * hc + 512],
                                            start=(hc == 0), stop=(hc == 7),
                                            skip_group_check=True)
                        mm1 = tensor.matmul(p_y[gp + 1][:], srcA,
                                            wd_sb[par][:, DM * hc + 512:DM * hc + 1024],
                                            start=(hc == 0), stop=(hc == 7),
                                            skip_group_check=True)
                    mm1.then_inc(s_dn, 1)

    return nc



def build_l1a():
    nc = bass.Bass("TRN2", target_bir_lowering=False)
    d_x = nc.dram_tensor("x1", [TOK, DM], F32, kind="ExternalInput").ap()
    o_xnfm = nc.dram_tensor("xn1_fm", [DM, TOK], BF16, kind="ExternalOutput").ap()

    xt = [nc.alloc_sbuf_tensor(f"xt{i}", [128, DM], F32) for i in range(2)]
    sq2 = nc.alloc_sbuf_tensor("sq2", [128, DM], F32)
    cols = nc.alloc_sbuf_tensor("cols", [128, 16], F32)
    xn_tm = [nc.alloc_sbuf_tensor(f"xntm{i}", [128, DM], BF16) for i in range(2)]
    xnfm = nc.alloc_sbuf_tensor("xnfm", [128, 8 * TOK], BF16)
    ident = nc.alloc_sbuf_tensor("ident", [128, 128], BF16)
    iot = nc.alloc_sbuf_tensor("iot", [128, 128], F32)
    p_tr = [nc.alloc_psum_tensor(f"p_tr{i}", [128, 128], BF16) for i in range(2)]

    s_xld = [nc.alloc_semaphore(f"s_xld{i}") for i in range(2)]
    s_idg = nc.alloc_semaphore("s_idg")
    s_idv = nc.alloc_semaphore("s_idv")
    s_v2s = nc.alloc_semaphore("s_v2s")
    s_s2v = nc.alloc_semaphore("s_s2v")
    s_xn = nc.alloc_semaphore("s_xn")
    s_tr = nc.alloc_semaphore("s_tr")
    s_ev = nc.alloc_semaphore("s_ev")
    s_st = nc.alloc_semaphore("s_st")
    s_vself = nc.alloc_semaphore("s_vself")
    vctr = [0]

    def vss(vector, inst):
        vctr[0] += 1
        inst.then_inc(s_vself, 1)
        vector.wait_ge(s_vself, vctr[0])

    with nc.Block() as block:
        @block.sync
        def _(sync):
            for i in range(PTILES):
                if i >= 2:
                    sync.wait_ge(s_xn, i - 1)
                sync.dma_start(out=xt[i % 2][:],
                               in_=d_x[128 * i:128 * (i + 1), :]).then_inc(s_xld[i % 2], 16)
            sync.wait_ge(s_ev, 128)
            for c in range(8):
                sync.dma_start(out=o_xnfm[128 * c:128 * (c + 1), :],
                               in_=xnfm[:, TOK * c:TOK * (c + 1)]).then_inc(s_st, 16)

        @block.gpsimd
        def _(gpsimd):
            gpsimd.iota(iot[:], pattern=[[1, 128]], base=0, channel_multiplier=-1,
                        allow_small_or_imprecise_dtypes=True).then_inc(s_idg, 1)
            gpsimd.wait_ge(s_st, 16 * 8)

        @block.vector
        def _(vector):
            vector.wait_ge(s_idg, 1)
            vector.tensor_scalar(ident[:], iot[:], 0.0, None, ALU.is_equal).then_inc(s_idv, 1)
            for i in range(PTILES):
                t = xt[i % 2]
                b = 8 * (i % 2)
                vector.wait_ge(s_xld[i % 2], 16 * (i // 2 + 1))
                vss(vector, vector.reduce_sum(cols[:, b + 0:b + 1], t[:], AX))
                vector.tensor_tensor(sq2[:], t[:], t[:], ALU.mult)
                vss(vector, vector.reduce_sum(cols[:, b + 1:b + 2], sq2[:], AX))
                vss(vector, vector.tensor_scalar(cols[:, b + 2:b + 3], cols[:, b + 0:b + 1], 1.0 / DM, None, ALU.mult))
                vss(vector, vector.tensor_scalar(cols[:, b + 3:b + 4], cols[:, b + 1:b + 2], 1.0 / DM, None, ALU.mult))
                vss(vector, vector.tensor_tensor(cols[:, b + 4:b + 5], cols[:, b + 2:b + 3], cols[:, b + 2:b + 3], ALU.mult))
                vss(vector, vector.tensor_tensor(cols[:, b + 5:b + 6], cols[:, b + 3:b + 4], cols[:, b + 4:b + 5], ALU.subtract))
                vector.tensor_scalar(cols[:, b + 5:b + 6], cols[:, b + 5:b + 6], 1e-5, None, ALU.add).then_inc(s_v2s, 1)
                vector.wait_ge(s_s2v, i + 1)
                vss(vector, vector.reciprocal(cols[:, b + 7:b + 8], cols[:, b + 6:b + 7]))
                if i >= 2:
                    vector.wait_ge(s_tr, 8 * (i - 1))
                vector.tensor_scalar(xn_tm[i % 2][:], t[:], cols[:, b + 2:b + 3], cols[:, b + 7:b + 8],
                                     ALU.subtract, ALU.mult).then_inc(s_xn, 1)

        @block.scalar
        def _(scalar):
            for i in range(PTILES):
                b = 8 * (i % 2)
                scalar.wait_ge(s_v2s, i + 1)
                scalar.activation(cols[:, b + 6:b + 7], cols[:, b + 5:b + 6], AF.Sqrt).then_inc(s_s2v, 1)
                if i >= 1:
                    for c in range(8):
                        n = 8 * (i - 1) + c
                        scalar.wait_ge(s_tr, n + 1)
                        scalar.copy(xnfm[:, TOK * c + 128 * (i - 1): TOK * c + 128 * i],
                                    p_tr[n % 2][:]).then_inc(s_ev, 1)
            i = PTILES
            for c in range(8):
                n = 8 * (i - 1) + c
                scalar.wait_ge(s_tr, n + 1)
                scalar.copy(xnfm[:, TOK * c + 128 * (i - 1): TOK * c + 128 * i],
                            p_tr[n % 2][:]).then_inc(s_ev, 1)

        @block.tensor
        def _(tensor):
            tensor.wait_ge(s_idg, 1)
            tensor.wait_ge(s_idv, 1)
            for i in range(PTILES):
                tensor.wait_ge(s_xn, i + 1)
                for c in range(8):
                    n = 8 * i + c
                    if n >= 2:
                        tensor.wait_ge(s_ev, n - 1)
                    tensor.transpose(p_tr[n % 2][:], xn_tm[i % 2][:, 128 * c:128 * (c + 1)],
                                     ident[:]).then_inc(s_tr, 1)

    return nc




def build_m1():
    nc = bass.Bass("TRN2", target_bir_lowering=False)
    d_xn = nc.dram_tensor("xn1f", [DM, T], BF16, kind="ExternalInput").ap()
    d_win = nc.dram_tensor("w_in_T", [DM, 4 * CH], F32, kind="ExternalInput").ap()
    d_cw = nc.dram_tensor("conv_w", [2 * CH, 4], F32, kind="ExternalInput").ap()
    d_cb = nc.dram_tensor("conv_b", [2 * CH, 1], F32, kind="ExternalInput").ap()
    d_xp = nc.dram_tensor("xproj_T", [2 * CH, 96], F32, kind="ExternalInput").ap()
    o_xc = nc.dram_tensor("xc", [2 * CH, T], F32, kind="ExternalOutput").ap()
    o_sz = nc.dram_tensor("sz", [2 * CH, T], BF16, kind="ExternalOutput").ap()
    o_xdb = nc.dram_tensor("xdb", [2, 96, T], F32, kind="ExternalOutput").ap()

    CK = 1024
    NCHK = T // CK

    xch = [nc.alloc_sbuf_tensor(f"xch{i}", [128, 8 * CK], BF16) for i in range(2)]
    w_in_sb = nc.alloc_sbuf_tensor("w_in_sb", [128, 8 * 4 * CH], BF16)
    conv_sb = nc.alloc_sbuf_tensor("conv_sb", [128, 8 * 4], F32)
    convb_sb = nc.alloc_sbuf_tensor("convb_sb", [128, 8], F32)
    xproj_sb = nc.alloc_sbuf_tensor("xproj_sb", [128, 8 * 96], BF16)
    xi_buf = [nc.alloc_sbuf_tensor(f"xi_buf{i}", [128, CK + 3], F32) for i in range(4)]
    xc_pre = nc.alloc_sbuf_tensor("xc_pre", [128, CK], F32)
    xc_out = [nc.alloc_sbuf_tensor(f"xc_out{i}", [128, CK], F32) for i in range(4)]
    xcb = [nc.alloc_sbuf_tensor(f"xcb{i}", [128, 4 * CK], BF16) for i in range(2)]
    z_out = [nc.alloc_sbuf_tensor(f"z_out{i}", [128, CK], BF16) for i in range(4)]
    xdb_buf = [nc.alloc_sbuf_tensor(f"xdb_buf{i}", [96, CK], F32) for i in range(2)]

    p_ip = [nc.alloc_psum_tensor(f"p_ip{i}", [128, 512], F32) for i in range(2)]
    p_xdb = [nc.alloc_psum_tensor(f"p_xdb{i}", [96, 512], F32) for i in range(2)]

    s_wld = nc.alloc_semaphore("s_wld")
    s_xch = [nc.alloc_semaphore(f"s_xch{i}") for i in range(2)]
    s_mm = nc.alloc_semaphore("s_mm")
    s_ev = nc.alloc_semaphore("s_ev")
    s_cv = nc.alloc_semaphore("s_cv")
    s_sl = nc.alloc_semaphore("s_sl")
    s_b16 = nc.alloc_semaphore("s_b16")
    s_xmm = nc.alloc_semaphore("s_xmm")
    s_xev = nc.alloc_semaphore("s_xev")
    s_xcst = [nc.alloc_semaphore(f"s_xcst{i}") for i in range(4)]
    s_zst = [nc.alloc_semaphore(f"s_zst{i}") for i in range(4)]
    s_xdbst = [nc.alloc_semaphore(f"s_xdbst{i}") for i in range(2)]
    s_chend = nc.alloc_semaphore("s_chend")

    def chunks():
        out = []
        for P in range(2):
            for cc in range(NCHK):
                nat = cc if P == 0 else NCHK - 1 - cc
                out.append((8 * P + cc, P, nat, cc))
        return out

    def cv_wait(gci, ct):
        P, cc = divmod(gci, 8)
        if cc == 0:
            return 36 * P + ct + 1
        return 36 * P + 4 + 4 * (cc - 1) + ct + 1

    with nc.Block() as block:
        @block.sync
        def _(sync):
            chl = chunks()

            def emit_stores(gci, P, nat, cc):
                for ct in range(4):
                    u = 4 * gci + ct
                    sync.wait_ge(s_sl, u + 1)
                    sync.dma_start(out=o_xc[128 * (4 * P + ct):128 * (4 * P + ct + 1),
                                            CK * cc:CK * (cc + 1)],
                                   in_=xc_out[ct][:]).then_inc(s_xcst[ct], 16)
                for zt in range(4):
                    sync.wait_ge(s_ev, 16 * gci + 2 * zt + 10)
                    sync.dma_start(out=o_sz[128 * (4 * P + zt):128 * (4 * P + zt + 1),
                                            CK * cc:CK * (cc + 1)],
                                   in_=z_out[zt][:]).then_inc(s_zst[zt], 16)
                par = gci % 2
                sync.wait_ge(s_xev, gci + 1)
                sync.dma_start(out=o_xdb[P, :, CK * cc:CK * (cc + 1)],
                               in_=xdb_buf[par][:]).then_inc(s_xdbst[par], 16)

            for (gci, P, nat, cc) in chl:
                par = gci % 2
                if gci >= 1:
                    emit_stores(*chl[gci - 1])
                if gci >= 2:
                    sync.wait_ge(s_chend, gci - 1)
                sync.dma_start(out=xch[par].ap().rearrange("p (c n) -> p c n", n=CK),
                               in_=d_xn[:, CK * nat:CK * (nat + 1)].rearrange(
                                   "(c p) n -> p c n", p=128)).then_inc(s_xch[par], 16)
            emit_stores(*chl[-1])

        @block.gpsimd
        def _(gpsimd):
            gpsimd.dma_start(out=w_in_sb.ap().rearrange("p (c n) -> p c n", n=4 * CH),
                             in_=d_win.rearrange("(c p) n -> p c n", p=128)).then_inc(s_wld, 16)
            gpsimd.dma_start(out=conv_sb.ap().rearrange("p (c n) -> p c n", n=4),
                             in_=d_cw.rearrange("(c p) n -> p c n", p=128)).then_inc(s_wld, 16)
            gpsimd.dma_start(out=convb_sb.ap().rearrange("p (c n) -> p c n", n=1),
                             in_=d_cb.rearrange("(c p) n -> p c n", p=128),
                             allow_slow_non_contiguous=True).then_inc(s_wld, 16)
            gpsimd.dma_start(out=xproj_sb.ap().rearrange("p (c n) -> p c n", n=96),
                             in_=d_xp.rearrange("(c p) n -> p c n", p=128)).then_inc(s_wld, 16)
            for q in range(4):
                gpsimd.wait_ge(s_xcst[q], 16 * 16)
                gpsimd.wait_ge(s_zst[q], 16 * 16)
            gpsimd.wait_ge(s_xdbst[0], 16 * 8)
            gpsimd.wait_ge(s_xdbst[1], 16 * 8)

        @block.tensor
        def _(tensor):
            tensor.wait_ge(s_wld, 64)
            for (gci, P, nat, cc) in chunks():
                par = gci % 2
                tensor.wait_ge(s_xch[par], 16 * (gci // 2 + 1))
                for ob in range(8):
                    ochan = 1024 * P + 128 * ob
                    for u in range(2):
                        gu = 16 * gci + 2 * ob + u
                        if gu >= 2:
                            tensor.wait_ge(s_ev, gu - 1)
                        for c in range(8):
                            mm = tensor.matmul(
                                p_ip[gu % 2][:],
                                w_in_sb[:, 4 * CH * c + ochan:4 * CH * c + ochan + 128],
                                xch[par][:, CK * c + 512 * u:CK * c + 512 * (u + 1)],
                                start=(c == 0), stop=(c == 7))
                        mm.then_inc(s_mm, 1)
                for u in range(2):
                    gx = 2 * gci + u
                    if gx >= 2:
                        tensor.wait_ge(s_xev, gci)
                    tensor.wait_ge(s_b16, 4 * (gci + 1))
                    for ct in range(4):
                        mm = tensor.matmul(
                            p_xdb[u][:],
                            xproj_sb[:, 96 * (4 * P + ct):96 * (4 * P + ct + 1)],
                            xcb[gci % 2][:, CK * ct + 512 * u:CK * ct + 512 * (u + 1)],
                            start=(ct == 0), stop=(ct == 3))
                    mm.then_inc(s_xmm, 1)
                tensor.nop().then_inc(s_chend, 1)

        @block.scalar
        def _(scalar):
            for (gci, P, nat, cc) in chunks():
                for ob in range(8):
                    for u in range(2):
                        gu = 16 * gci + 2 * ob + u
                        scalar.wait_ge(s_mm, gu + 1)
                        if ob < 4:
                            ct = ob
                            scalar.wait_ge(s_cv, cv_wait(gci, ct))
                            dst_u = u if P == 0 else 1 - u
                            src = p_ip[gu % 2].ap()[:, :]
                            if P == 1:
                                src = src[:, ::-1]
                            scalar.copy(xi_buf[ct][:, 3 + 512 * dst_u:3 + 512 * (dst_u + 1)],
                                        src).then_inc(s_ev, 1)
                        else:
                            zt = ob - 4
                            if gci >= 1:
                                scalar.wait_ge(s_zst[zt], 16 * gci)
                            dst_u = u if P == 0 else 1 - u
                            src = p_ip[gu % 2].ap()[:, :]
                            if P == 1:
                                src = src[:, ::-1]
                            scalar.activation(z_out[zt][:, 512 * dst_u:512 * (dst_u + 1)],
                                              src, AF.Silu).then_inc(s_ev, 1)
                for ct in range(4):
                    uu = 4 * gci + ct
                    scalar.wait_ge(s_cv, 36 * (gci // 8) + 4 + 4 * (gci % 8) + ct + 1)
                    if gci >= 1:
                        scalar.wait_ge(s_xcst[ct], 16 * gci)
                        scalar.wait_ge(s_b16, 4 * gci)
                    scalar.activation(xc_out[ct][:], xc_pre[:], AF.Silu,
                                      bias=convb_sb[:, (4 * P + ct):(4 * P + ct) + 1]
                                      ).then_inc(s_sl, 1)
                par = gci % 2
                scalar.wait_ge(s_xmm, 2 * (gci + 1))
                if gci >= 2:
                    scalar.wait_ge(s_xdbst[par], 16 * ((gci - 2) // 2 + 1))
                scalar.copy(xdb_buf[par][:, 0:512], p_xdb[0][:])
                scalar.copy(xdb_buf[par][:, 512:1024], p_xdb[1][:]).then_inc(s_xev, 1)

        @block.vector
        def _(vector):
            for P in range(2):
                for ct in range(4):
                    vector.memset(xi_buf[ct][:, 0:3], 0.0).then_inc(s_cv, 1)
                for cc in range(NCHK):
                    gci = 8 * P + cc
                    for ct in range(4):
                        gu_hi = 16 * gci + 2 * ct + 1
                        vector.wait_ge(s_ev, gu_hi + 1)
                        w0 = conv_sb.ap()[:, 4 * (4 * P + ct) + 0:4 * (4 * P + ct) + 1]
                        vector.wait_ge(s_sl, 4 * gci + ct)
                        vector.tensor_scalar(xc_pre[:], xi_buf[ct][:, 0:CK], w0, None, ALU.mult)
                        for k in range(1, 4):
                            wk = conv_sb.ap()[:, 4 * (4 * P + ct) + k:4 * (4 * P + ct) + k + 1]
                            vector.scalar_tensor_tensor(xc_pre[:], xi_buf[ct][:, k:k + CK], wk,
                                                        xc_pre[:], ALU.mult, ALU.add)
                        vector.tensor_copy(xi_buf[ct][:, 0:3], xi_buf[ct][:, CK:CK + 3])
                        vector.nop().then_inc(s_cv, 1)
                    for ct in range(4):
                        uu = 4 * gci + ct
                        vector.wait_ge(s_sl, uu + 1)
                        if gci >= 2:
                            vector.wait_ge(s_xmm, 2 * (gci - 1))
                        vector.tensor_copy(xcb[gci % 2][:, CK * ct:CK * (ct + 1)],
                                           xc_out[ct][:]).then_inc(s_b16, 1)

    return nc




def build_m2():
    nc = bass.Bass("TRN2", target_bir_lowering=False)
    d_xc = nc.dram_tensor("xc", [2 * CH, T], F32, kind="ExternalInput").ap()
    d_sz = nc.dram_tensor("sz", [2 * CH, T], BF16, kind="ExternalInput").ap()
    d_xa = nc.dram_tensor("xdb_a", [2, 96, T], F32, kind="ExternalInput").ap()
    d_xb = [nc.dram_tensor(f"xdb_p{i}", [2, 96, T], F32, kind="ExternalInput").ap()
            for i in range(3)]
    d_dtw = nc.dram_tensor("dt_wT", [DTR, 2 * CH], F32, kind="ExternalInput").ap()
    d_dtb = nc.dram_tensor("dt_b", [2 * CH, 1], F32, kind="ExternalInput").ap()
    d_alog = nc.dram_tensor("A_log", [2 * CH, DST], F32, kind="ExternalInput").ap()
    d_dp = nc.dram_tensor("Dp", [2 * CH, 1], F32, kind="ExternalInput").ap()
    o_yg = nc.dram_tensor("yg", [2 * CH, T], BF16, kind="ExternalOutput").ap()
    h_xdb = nc.dram_tensor("h_xdb", [2, 96, T], F32, kind="Internal").ap()
    h_bc = nc.dram_tensor("h_bc", [2, 32, T], BF16, kind="Internal").ap()

    TC = 1024
    NCK = T // TC

    xdb1 = [nc.alloc_sbuf_tensor(f"xdb1_{i}", [96, 2048], F32) for i in range(2)]
    bc16 = [nc.alloc_sbuf_tensor(f"bc16_{i}", [32, 2048], BF16) for i in range(2)]
    xdb2 = [nc.alloc_sbuf_tensor(f"xdb2_{i}", [96, 2048], F32) for i in range(2)]
    dtw_sb = nc.alloc_sbuf_tensor("dtw_sb", [DTR, 2 * CH], F32)
    dtb_sb = nc.alloc_sbuf_tensor("dtb_sb", [128, 8], F32)
    dp_sb = nc.alloc_sbuf_tensor("dp_sb", [128, 8], F32)
    alog_sb = nc.alloc_sbuf_tensor("alog_sb", [128, 8 * DST], F32)
    A_sb = nc.alloc_sbuf_tensor("A_sb", [128, 8 * DST], F32)
    dtr = [nc.alloc_sbuf_tensor(f"dtr{i}", [DTR, TC], F32) for i in range(2)]
    u_sb = [nc.alloc_sbuf_tensor(f"u_sb{i}", [128, TC], F32) for i in range(4)]
    sz_sb = [nc.alloc_sbuf_tensor(f"sz_sb{i}", [128, TC], BF16) for i in range(4)]
    dl_sb = [nc.alloc_sbuf_tensor(f"dl_sb{i}", [128, TC], F32) for i in range(4)]
    w_sb = [nc.alloc_sbuf_tensor(f"w_sb{i}", [128, TC], F32) for i in range(4)]
    y_sb = [nc.alloc_sbuf_tensor(f"y_sb{i}", [128, TC], F32) for i in range(4)]
    Bb = [nc.alloc_sbuf_tensor(f"Bb{i}", [128, 4 * TC], BF16) for i in range(2)]
    Cb = [nc.alloc_sbuf_tensor(f"Cb{i}", [128, 4 * TC], BF16) for i in range(2)]
    dA = [nc.alloc_sbuf_tensor(f"dA{i}", [128, TC], F32) for i in range(3)]
    dBu = [nc.alloc_sbuf_tensor(f"dBu{i}", [128, TC], F32) for i in range(2)]
    h_b = [nc.alloc_sbuf_tensor(f"h_b{i}", [128, TC], F32) for i in range(2)]
    hC = [nc.alloc_sbuf_tensor(f"hC{i}", [128, TC], F32) for i in range(3)]
    state = [nc.alloc_sbuf_tensor(f"state{i}", [128, DST], F32) for i in range(4)]
    st4 = nc.alloc_sbuf_tensor("st4", [128, 8], F32)
    ygb = [nc.alloc_sbuf_tensor(f"ygb{i}", [128, TC], BF16) for i in range(4)]
    sf1 = nc.alloc_sbuf_tensor("sf1", [128, 512], F32)

    p_dt = [nc.alloc_psum_tensor(f"p_dt{i}", [128, 512], F32) for i in range(2)]

    s_ldw = nc.alloc_semaphore("s_ldw")
    s_x1 = [nc.alloc_semaphore(f"s_x1_{i}") for i in range(2)]
    s_x2 = [nc.alloc_semaphore(f"s_x2_{i}") for i in range(2)]
    s_add = nc.alloc_semaphore("s_add")
    s_b16c = nc.alloc_semaphore("s_b16c")
    s_bcst = [nc.alloc_semaphore(f"s_bcst{i}") for i in range(2)]
    s_xst = [nc.alloc_semaphore(f"s_xst{i}") for i in range(2)]
    s_uld = [nc.alloc_semaphore(f"s_uld{i}") for i in range(4)]
    s_zld = [nc.alloc_semaphore(f"s_zld{i}") for i in range(4)]
    s_dtld = [nc.alloc_semaphore(f"s_dtld{i}") for i in range(2)]
    s_bld = [nc.alloc_semaphore(f"s_bld{i}") for i in range(2)]
    s_cld = [nc.alloc_semaphore(f"s_cld{i}") for i in range(2)]
    s_dmm = nc.alloc_semaphore("s_dmm")
    s_sp = nc.alloc_semaphore("s_sp")
    s_dl = nc.alloc_semaphore("s_dl")
    s_w = nc.alloc_semaphore("s_w")
    s_da = nc.alloc_semaphore("s_da")
    s_hc = nc.alloc_semaphore("s_hc")
    s_y = nc.alloc_semaphore("s_y")
    s_zf = nc.alloc_semaphore("s_zf")
    s_gat = nc.alloc_semaphore("s_gat")
    s_ygst = [nc.alloc_semaphore(f"s_ygst{i}") for i in range(4)]
    s_A = nc.alloc_semaphore("s_A")
    s_vself = nc.alloc_semaphore("s_vself")
    vctr = [0]

    def vss(vector, inst):
        vctr[0] += 1
        inst.then_inc(s_vself, 1)
        vector.wait_ge(s_vself, vctr[0])

    def uidx_of(d, c):
        return NCK * d + c

    NUNIT = 2 * NCK

    with nc.Block() as block:
        @block.sync
        def _(sync):
            for d in range(2):
                for q in range(4):
                    k = 4 * d + q
                    par = k % 2
                    if k >= 2:
                        sync.wait_ge(s_xst[k % 2], 16 * ((k - 2) // 2 + 1))
                    sync.dma_start(out=xdb1[par][:], in_=d_xa[d, :, 2048 * q:2048 * (q + 1)]
                                   ).then_inc(s_x1[par], 16)
                    for m in range(3):
                        if 3 * k + m >= 1:
                            sync.wait_ge(s_add, 3 * k + m)
                        sync.dma_start(out=xdb2[(3 * k + m) % 2][:],
                                       in_=d_xb[m][d, :, 2048 * q:2048 * (q + 1)]
                                       ).then_inc(s_x2[(3 * k + m) % 2], 16)
                    sync.wait_ge(s_add, 3 * (k + 1))
                    sync.dma_start(out=h_xdb[d, :, 2048 * q:2048 * (q + 1)],
                                   in_=xdb1[par][:]).then_inc(s_xst[par], 16)
                    sync.wait_ge(s_b16c, k + 1)
                    sync.dma_start(out=h_bc[d, :, 2048 * q:2048 * (q + 1)],
                                   in_=bc16[par][:]).then_inc(s_bcst[par], 16)
            for d in range(2):
                for c in range(NCK):
                    ui = uidx_of(d, c)
                    if ui >= 1:
                        pd, pc = divmod(ui - 1, NCK)
                        for ct in range(4):
                            pu = 4 * (ui - 1) + ct
                            sync.wait_ge(s_gat, pu + 1)
                            sync.dma_start(out=o_yg[512 * pd + 128 * ct:512 * pd + 128 * (ct + 1),
                                                    TC * pc:TC * (pc + 1)],
                                           in_=ygb[ct][:]).then_inc(s_ygst[ct], 16)
                    par = ui % 2
                    sync.wait_ge(s_xst[0], 16 * (2 * (d + 1)))
                    sync.wait_ge(s_xst[1], 16 * (2 * (d + 1)))
                    if ui >= 2:
                        sync.wait_ge(s_dmm, 2 * (ui - 1))
                    sync.dma_start(out=dtr[par][:], in_=h_xdb[d, 0:DTR, TC * c:TC * (c + 1)]
                                   ).then_inc(s_dtld[par], 16)
                    for ct in range(4):
                        if ui >= 1:
                            sync.wait_ge(s_gat, 4 * (ui - 1) + ct + 1)
                        sync.dma_start(out=u_sb[ct][:],
                                       in_=d_xc[512 * d + 128 * ct:512 * d + 128 * (ct + 1),
                                                TC * c:TC * (c + 1)]).then_inc(s_uld[ct], 16)
                        sync.dma_start(out=sz_sb[ct][:],
                                       in_=d_sz[512 * d + 128 * ct:512 * d + 128 * (ct + 1),
                                                TC * c:TC * (c + 1)]).then_inc(s_zld[ct], 16)
                    for sg in range(4):
                        gg = 4 * ui + sg
                        par2 = gg % 2
                        if gg >= 2:
                            sync.wait_ge(s_hc, 16 * (gg - 1))
                        sync.dma_start(out=Bb[par2][:].rearrange("p (s n) -> p s n", n=TC),
                                       in_=AP(h_bc.tensor,
                                              h_bc.offset + (T * 32 * d + T * 4 * sg + TC * c),
                                              [[0, 128], [T, 4], [1, TC]])).then_inc(s_bld[par2], 16)
                        sync.dma_start(out=Cb[par2][:].rearrange("p (s n) -> p s n", n=TC),
                                       in_=AP(h_bc.tensor,
                                              h_bc.offset + (T * 32 * d + T * (DST + 4 * sg) + TC * c),
                                              [[0, 128], [T, 4], [1, TC]])).then_inc(s_cld[par2], 16)
            for ct in range(4):
                pu = 4 * (NUNIT - 1) + ct
                sync.wait_ge(s_gat, pu + 1)
                sync.dma_start(out=o_yg[512 + 128 * ct:512 + 128 * (ct + 1),
                                        TC * (NCK - 1):TC * NCK],
                               in_=ygb[ct][:]).then_inc(s_ygst[ct], 16)

        @block.gpsimd
        def _(gpsimd):
            gpsimd.dma_start(out=dtw_sb[:], in_=d_dtw).then_inc(s_ldw, 16)
            gpsimd.dma_start(out=dtb_sb.ap().rearrange("p (c n) -> p c n", n=1),
                             in_=d_dtb.rearrange("(c p) n -> p c n", p=128),
                             allow_slow_non_contiguous=True).then_inc(s_ldw, 16)
            gpsimd.dma_start(out=dp_sb.ap().rearrange("p (c n) -> p c n", n=1),
                             in_=d_dp.rearrange("(c p) n -> p c n", p=128),
                             allow_slow_non_contiguous=True).then_inc(s_ldw, 16)
            gpsimd.dma_start(out=alog_sb.ap().rearrange("p (c n) -> p c n", n=DST),
                             in_=d_alog.rearrange("(c p) n -> p c n", p=128)).then_inc(s_ldw, 16)
            for d in range(2):
                for c in range(NCK):
                    ui = uidx_of(d, c)
                    for s in range(DST):
                        gb = 16 * ui + s
                        for ct in range(4):
                            gsc = 4 * gb + ct
                            gpsimd.wait_ge(s_hc, gsc + 1)
                            if s == 0:
                                if ui >= 1:
                                    gpsimd.wait_ge(s_gat, 4 * (ui - 1) + ct + 1)
                                gpsimd.tensor_copy(y_sb[ct][:], hC[gsc % 3][:]).then_inc(s_y, 1)
                            else:
                                gpsimd.tensor_tensor(y_sb[ct][:], y_sb[ct][:], hC[gsc % 3][:],
                                                     ALU.add).then_inc(s_y, 1)
            for q in range(4):
                gpsimd.wait_ge(s_ygst[q], 16 * NUNIT)
            gpsimd.wait_ge(s_bcst[0], 64)
            gpsimd.wait_ge(s_bcst[1], 64)

        @block.scalar
        def _(scalar):
            scalar.wait_ge(s_ldw, 64)
            scalar.activation(A_sb[:], alog_sb[:], AF.Exp).then_inc(s_A, 1)
            scalar.wait_ge(s_A, 2)
            for d in range(2):
                for c in range(NCK):
                    ui = uidx_of(d, c)
                    for ct in range(4):
                        for u2 in range(2):
                            gdt = 8 * ui + 2 * ct + u2
                            scalar.wait_ge(s_dmm, gdt + 1)
                            bias = dtb_sb.ap()[:, 4 * d + ct:4 * d + ct + 1]
                            scalar.activation(sf1[:], p_dt[gdt % 2][:], AF.Abs, bias=bias)
                            scalar.activation(sf1[:], sf1[:], AF.Exp, scale=-1.0)
                            scalar.activation(sf1[:], sf1[:], AF.Ln, bias=1.0)
                            scalar.activation(dl_sb[ct][:, 512 * u2:512 * (u2 + 1)],
                                              p_dt[gdt % 2][:], AF.Relu,
                                              bias=bias).then_inc(s_sp, 1)
                            scalar.wait_ge(s_dl, gdt + 1)
                    for s in range(DST):
                        gb = 16 * ui + s
                        for ct in range(4):
                            gsc = 4 * gb + ct
                            scalar.wait_ge(s_dl, 8 * ui + 2 * ct + 2)
                            if gsc >= 3:
                                scalar.wait_ge(s_hc, gsc - 2)
                            scalar.activation(dA[gsc % 3][:], dl_sb[ct][:], AF.Exp,
                                              scale=A_sb[:, DST * (4 * d + ct) + s:
                                                         DST * (4 * d + ct) + s + 1]
                                              ).then_inc(s_da, 1)

        @block.vector
        def _(vector):
            vector.wait_ge(s_A, 1)
            vector.tensor_scalar(A_sb[:], A_sb[:], -1.0, None, ALU.mult).then_inc(s_A, 1)
            for k in range(8):
                par = k % 2
                vector.wait_ge(s_x1[par], 16 * (k // 2 + 1))
                for m in range(3):
                    g2 = 3 * k + m
                    vector.wait_ge(s_x2[g2 % 2], 16 * (g2 // 2 + 1))
                    vector.tensor_tensor(xdb1[par][:], xdb1[par][:], xdb2[g2 % 2][:],
                                         ALU.add).then_inc(s_add, 1)
                if k >= 2:
                    vector.wait_ge(s_bcst[par], 16 * ((k - 2) // 2 + 1))
                vector.tensor_copy(bc16[par][:], xdb1[par][64:96, :]).then_inc(s_b16c, 1)
            for d in range(2):
                for c in range(NCK):
                    ui = uidx_of(d, c)
                    for ct in range(4):
                        for u2 in range(2):
                            gdt = 8 * ui + 2 * ct + u2
                            vector.wait_ge(s_sp, gdt + 1)
                            vector.tensor_tensor(dl_sb[ct][:, 512 * u2:512 * (u2 + 1)],
                                                 dl_sb[ct][:, 512 * u2:512 * (u2 + 1)],
                                                 sf1[:], ALU.add).then_inc(s_dl, 1)
                        vector.wait_ge(s_uld[ct], 16 * (ui + 1))
                        vector.tensor_tensor(w_sb[ct][:], dl_sb[ct][:], u_sb[ct][:],
                                             ALU.mult).then_inc(s_w, 1)
                    if c == 0:
                        for ct in range(4):
                            vector.memset(state[ct][:], 0.0)
                    for s in range(DST):
                        gb = 16 * ui + s
                        for ct in range(4):
                            gsc = 4 * gb + ct
                            gg = 4 * ui + s // 4
                            so = s % 4
                            vector.wait_ge(s_bld[gg % 2], 16 * (gg // 2 + 1))
                            vector.tensor_tensor(dBu[gsc % 2][:], w_sb[ct][:],
                                                 Bb[gg % 2][:, TC * so:TC * (so + 1)],
                                                 ALU.mult)
                            vector.wait_ge(s_da, gsc + 1)
                            vector.tensor_tensor_scan(h_b[gsc % 2][:], dA[gsc % 3][:],
                                                      dBu[gsc % 2][:],
                                                      state[ct][:, s:s + 1], ALU.mult, ALU.add)
                            st_cp = vector.tensor_copy(state[ct][:, s:s + 1],
                                                       h_b[gsc % 2][:, TC - 1:TC])
                            if ct == 3:
                                vss(vector, st_cp)
                            vector.wait_ge(s_cld[gg % 2], 16 * (gg // 2 + 1))
                            if gsc >= 3:
                                vector.wait_ge(s_y, gsc - 2)
                            vector.tensor_tensor(hC[gsc % 3][:], h_b[gsc % 2][:],
                                                 Cb[gg % 2][:, TC * so:TC * (so + 1)],
                                                 ALU.mult).then_inc(s_hc, 1)
                    for ct in range(4):
                        uu = 4 * ui + ct
                        vector.wait_ge(s_y, 64 * ui + 60 + ct + 1)
                        vector.wait_ge(s_zld[ct], 16 * (ui + 1))
                        dcol = dp_sb.ap()[:, 4 * d + ct:4 * d + ct + 1]
                        vector.scalar_tensor_tensor(y_sb[ct][:], u_sb[ct][:], dcol, y_sb[ct][:],
                                                    ALU.mult, ALU.add)
                        if ui >= 1:
                            vector.wait_ge(s_ygst[ct], 16 * ui)
                        vector.tensor_tensor(ygb[ct][:], y_sb[ct][:], sz_sb[ct][:],
                                             ALU.mult).then_inc(s_gat, 1)

        @block.tensor
        def _(tensor):
            tensor.wait_ge(s_ldw, 16)
            for d in range(2):
                for c in range(NCK):
                    ui = uidx_of(d, c)
                    par = ui % 2
                    tensor.wait_ge(s_dtld[par], 16 * (ui // 2 + 1))
                    for ct in range(4):
                        for u2 in range(2):
                            gdt = 8 * ui + 2 * ct + u2
                            if gdt >= 2:
                                tensor.wait_ge(s_sp, gdt - 1)
                            tensor.matmul(p_dt[gdt % 2][:],
                                          dtw_sb[:, 512 * d + 128 * ct:512 * d + 128 * (ct + 1)],
                                          dtr[par][:, 512 * u2:512 * (u2 + 1)],
                                          start=True, stop=True).then_inc(s_dmm, 1)

    return nc




def build_l1ffn():
    nc = bass.Bass("TRN2", target_bir_lowering=False)
    d_wup = nc.dram_tensor("w_upT", [DM, FF], F32, kind="ExternalInput").ap()
    d_wdn = nc.dram_tensor("w_downT", [FF, DM], F32, kind="ExternalInput").ap()
    d_xn = nc.dram_tensor("xn1_fm", [DM, TOK], BF16, kind="ExternalInput").ap()
    o_f = nc.dram_tensor("ffn", [TOK, DM], F32, kind="ExternalOutput").ap()
    h_hbm = nc.dram_tensor("h_hbm", [FF, TOK], BF16, kind="Internal").ap()
    h_row = nc.dram_tensor("hrow", [1, TOK], F32, kind="Internal").ap()
    h_s1 = nc.dram_tensor("h_s1", [1, 2], F32, kind="Internal").ap()

    arena = nc.alloc_sbuf_tensor("arena", [128, 32768], BF16)
    wld = [nc.alloc_sbuf_tensor(f"wld{i}", [128, 2048], F32) for i in range(2)]
    tern = nc.alloc_sbuf_tensor("tern", [128, 2048], BF16)
    acc8 = nc.alloc_sbuf_tensor("acc8", [128, 40], F32)
    onef = nc.alloc_sbuf_tensor("onef", [128, 1], F32)
    onesb = nc.alloc_sbuf_tensor("onesb", [128, 1], BF16)
    tau_bc = nc.alloc_sbuf_tensor("tau_bc", [128, 2], F32)
    xn_sb = nc.alloc_sbuf_tensor("xn_sb", [128, 8 * TOK], BF16)
    sqb = nc.alloc_sbuf_tensor("sqb", [128, 2048], BF16)
    rrow = nc.alloc_sbuf_tensor("rrow", [1, TOK], F32)
    rbc = nc.alloc_sbuf_tensor("rbc", [128, TOK], BF16)
    h_t = [nc.alloc_sbuf_tensor(f"h_t{i}", [128, 512], BF16) for i in range(2)]
    hcol = nc.alloc_sbuf_tensor("hcol", [128, 32 * 128], BF16)
    outb = [nc.alloc_sbuf_tensor(f"outb{i}", [128, DM], F32) for i in range(2)]

    pA = [nc.alloc_psum_tensor(f"pA{i}", [128, 512], F32) for i in range(2)]
    pB = [nc.alloc_psum_tensor(f"pB{i}", [128, 512], F32) for i in range(2)]
    pS = [nc.alloc_psum_tensor(f"pS{i}", [1, 512], F32) for i in range(4)]

    s_wl = [nc.alloc_semaphore(f"s_wl{i}") for i in range(2)]
    s_abs = nc.alloc_semaphore("s_abs")
    s_ones = nc.alloc_semaphore("s_ones")
    s_smm = nc.alloc_semaphore("s_smm")
    s_sev = nc.alloc_semaphore("s_sev")
    s_bc1 = nc.alloc_semaphore("s_bc1")
    s_bc2 = nc.alloc_semaphore("s_bc2")
    s_tern = nc.alloc_semaphore("s_tern")
    s_ldxn = nc.alloc_semaphore("s_ldxn")
    s_sq = nc.alloc_semaphore("s_sq")
    s_qmm = nc.alloc_semaphore("s_qmm")
    s_r1d = nc.alloc_semaphore("s_r1d")
    s_rrow = nc.alloc_semaphore("s_rrow")
    s_rb1 = nc.alloc_semaphore("s_rb1")
    s_rb2 = nc.alloc_semaphore("s_rb2")
    s_xno = nc.alloc_semaphore("s_xno")
    s_up = nc.alloc_semaphore("s_up")
    s_gel = nc.alloc_semaphore("s_gel")
    s_h2 = nc.alloc_semaphore("s_h2")
    s_h2m = nc.alloc_semaphore("s_h2m")
    s_hst = [nc.alloc_semaphore(f"s_hst{i}") for i in range(2)]
    s_hld = nc.alloc_semaphore("s_hld")
    s_hsc = nc.alloc_semaphore("s_hsc")
    s_dmm = nc.alloc_semaphore("s_dmm")
    s_fout = nc.alloc_semaphore("s_fout")
    s_ost = [nc.alloc_semaphore(f"s_ost{i}") for i in range(2)]
    s_vself = nc.alloc_semaphore("s_vself")
    s_sself = nc.alloc_semaphore("s_sself")
    vctr = [0]
    sctr = [0]

    def vss(vector, inst):
        vctr[0] += 1
        inst.then_inc(s_vself, 1)
        vector.wait_ge(s_vself, vctr[0])

    def sss(scalar, inst):
        sctr[0] += 1
        inst.then_inc(s_sself, 1)
        scalar.wait_ge(s_sself, sctr[0])

    def wchunks():
        out = []
        for i8 in range(8):
            for hh in range(2):
                out.append(("up", i8, hh))
        for i16 in range(16):
            out.append(("dn", i16, 0))
        return out

    WCH = wchunks()

    def wload(sync, k, tag, a, b):
        par = k % 2
        if tag == "up":
            sync.dma_start(out=wld[par][:],
                           in_=d_wup[128 * a:128 * (a + 1), 2048 * b:2048 * (b + 1)]
                           ).then_inc(s_wl[par], 16)
        else:
            sync.dma_start(out=wld[par].ap().rearrange("p (c n) -> p c n", n=DM),
                           in_=d_wdn[256 * a:256 * (a + 1), :].rearrange(
                               "(c p) n -> p c n", p=128)).then_inc(s_wl[par], 16)

    with nc.Block() as block:
        @block.sync
        def _(sync):
            for k, (tag, a, b) in enumerate(WCH):
                if k >= 2:
                    sync.wait_ge(s_abs, k - 1)
                wload(sync, k, tag, a, b)
            for c in range(8):
                sync.dma_start(out=xn_sb[:, TOK * c:TOK * (c + 1)],
                               in_=d_xn[128 * c:128 * (c + 1), :]).then_inc(s_ldxn, 16)
            sync.wait_ge(s_sev, 2)
            sync.dma_start(out=h_s1, in_=acc8[0:1, 34:36]).then_inc(s_bc1, 16)
            sync.wait_ge(s_bc1, 16)
            sync.dma_start(out=tau_bc[:], in_=AP(h_s1.tensor, h_s1.offset,
                                                 [[0, 128], [1, 2]])).then_inc(s_bc2, 16)
            for k, (tag, a, b) in enumerate(WCH[:16]):
                sync.wait_ge(s_tern, max(0, k - 1))
                wload(sync, 32 + k, tag, a, b)
            sync.wait_ge(s_rrow, 101)
            sync.dma_start(out=h_row, in_=rrow[:]).then_inc(s_rb1, 16)
            for k in range(128):
                par = k % 2
                sync.wait_ge(s_gel, k + 1)
                fc, tc = divmod(k, 4)
                sync.dma_start(out=h_hbm[128 * fc:128 * (fc + 1), 512 * tc:512 * (tc + 1)],
                               in_=h_t[par][:]).then_inc(s_hst[par], 16)
            for k, (tag, a, b) in enumerate(WCH):
                if k >= 16:
                    sync.wait_ge(s_tern, k - 1)
                    wload(sync, 32 + k, tag, a, b)
            sync.wait_ge(s_rrow, 202)
            sync.dma_start(out=h_row, in_=rrow[:]).then_inc(s_rb1, 16)
            sync.wait_ge(s_hst[0], 16 * 64)
            sync.wait_ge(s_hst[1], 16 * 64)
            for tt in range(PTILES + 2):
                if tt < PTILES:
                    if tt >= 1:
                        sync.wait_ge(s_dmm, tt)
                    sync.dma_start(out=hcol.ap().rearrange("p (c n) -> p c n", n=128),
                                   in_=h_hbm[:, 128 * tt:128 * (tt + 1)].rearrange(
                                       "(c p) n -> p c n", p=128)).then_inc(s_hld, 16)
                if tt >= 2:
                    jj = tt - 2
                    sync.wait_ge(s_fout, jj + 1)
                    sync.dma_start(out=o_f[128 * jj:128 * (jj + 1), :],
                                   in_=outb[jj % 2][:]).then_inc(s_ost[jj % 2], 16)

        @block.gpsimd
        def _(gpsimd):
            gpsimd.memset(onef[:], 1.0)
            gpsimd.memset(onesb[:], 1.0).then_inc(s_ones, 1)
            gpsimd.wait_ge(s_rb1, 16)
            gpsimd.dma_start(out=rbc[:], in_=AP(h_row.tensor, h_row.offset,
                                                [[0, 128], [1, TOK]])).then_inc(s_rb2, 16)
            gpsimd.wait_ge(s_rb1, 32)
            gpsimd.wait_ge(s_xno, 8)
            gpsimd.dma_start(out=rbc[:], in_=AP(h_row.tensor, h_row.offset,
                                                [[0, 128], [1, TOK]])).then_inc(s_rb2, 16)
            gpsimd.wait_ge(s_hst[0], 16 * 64)
            gpsimd.wait_ge(s_hst[1], 16 * 64)
            gpsimd.wait_ge(s_ost[0], 16 * (PTILES // 2))
            gpsimd.wait_ge(s_ost[1], 16 * (PTILES // 2))

        @block.vector
        def _(vector):
            for k in range(32):
                vector.wait_ge(s_wl[k % 2], 16 * (k // 2 + 1))
                vss(vector, vector.tensor_reduce(acc8[:, k:k + 1], wld[k % 2][:], AX,
                                                 ALU.add, apply_absolute_value=True))
                vector.nop().then_inc(s_abs, 1)
            vss(vector, vector.reduce_sum(acc8[:, 32:33], acc8[:, 0:16], AX))
            vss(vector, vector.reduce_sum(acc8[:, 33:34], acc8[:, 16:32], AX))
            vector.nop().then_inc(s_abs, 1)
            def tern_one(k, tag, a, b):
                vector.wait_ge(s_wl[k % 2], 16 * (16 + k // 2 + 1))
                vector.wait_ge(s_bc2, 16)
                if tag == "dn" and a == 0 and b == 0:
                    vector.wait_ge(s_up, 128)
                tcol = tau_bc.ap()[:, 0:1] if tag == "up" else tau_bc.ap()[:, 1:2]
                if tag == "up":
                    dst = arena.ap()[:, 4096 * a + 2048 * b:4096 * a + 2048 * (b + 1)]
                else:
                    dst = arena.ap()[:, 2048 * a:2048 * (a + 1)]
                vss(vector, vector.tensor_scalar(tern[:], wld[k % 2][:], tcol, None, ALU.is_gt))
                vss(vector, vector.tensor_scalar(wld[k % 2][:], wld[k % 2][:], -1.0, None, ALU.mult))
                vss(vector, vector.tensor_scalar(dst, wld[k % 2][:], tcol, None, ALU.is_gt))
                vss(vector, vector.tensor_tensor(dst, tern[:], dst, ALU.subtract))
                vector.nop().then_inc(s_tern, 1)
            for k, (tag, a, b) in enumerate(WCH[:16]):
                tern_one(k, tag, a, b)
            vector.wait_ge(s_rrow, 100)
            vss(vector, vector.tensor_scalar(rrow[:], rrow[:], 1e-8, None, ALU.add))
            vss(vector, vector.reciprocal(rrow[:], rrow[:]))
            vector.nop().then_inc(s_rrow, 1)
            vector.wait_ge(s_rb2, 16)
            vector.wait_ge(s_qmm, 32)
            for c in range(8):
                vector.tensor_tensor(xn_sb[:, TOK * c:TOK * (c + 1)],
                                     xn_sb[:, TOK * c:TOK * (c + 1)], rbc[:],
                                     ALU.mult).then_inc(s_xno, 1)
            for k, (tag, a, b) in enumerate(WCH):
                if k >= 16:
                    tern_one(k, tag, a, b)
            vector.wait_ge(s_rrow, 201)
            vss(vector, vector.tensor_scalar(rrow[:], rrow[:], 1e-8, None, ALU.add))
            vss(vector, vector.reciprocal(rrow[:], rrow[:]))
            vector.nop().then_inc(s_rrow, 1)
            for tt in range(PTILES):
                vector.wait_ge(s_rb2, 32)
                vector.wait_ge(s_hld, 16 * (tt + 1))
                rs = AP(rbc.ap().tensor, rbc.ap().offset + 128 * tt,
                        [rbc.ap().ap[0], [0, 32], [1, 128]])
                h3 = hcol.ap().rearrange("p (c n) -> p c n", n=128)
                vector.tensor_tensor(h3, h3, rs, ALU.mult).then_inc(s_hsc, 1)
                vector.wait_ge(s_dmm, tt + 1)
                par = tt % 2
                if tt >= 2:
                    vector.wait_ge(s_ost[par], 16 * ((tt - 2) // 2 + 1))
                vector.tensor_copy(outb[par][:, 0:512], pA[tt % 2][:])
                vector.tensor_copy(outb[par][:, 512:1024], pB[tt % 2][:]).then_inc(s_fout, 1)

        @block.scalar
        def _(scalar):
            scalar.wait_ge(s_smm, 2)
            sss(scalar, scalar.copy(acc8[0:1, 34:35], pS[0].ap()[0:1, 0:1]))
            sss(scalar, scalar.copy(acc8[0:1, 35:36], pS[1].ap()[0:1, 0:1]))
            sss(scalar, scalar.mul(acc8[0:1, 34:35], acc8[0:1, 34:35], 0.5 / (DM * FF)))
            scalar.mul(acc8[0:1, 35:36], acc8[0:1, 35:36], 0.5 / (DM * FF)).then_inc(s_sev, 2)
            scalar.wait_ge(s_ldxn, 16 * 8)
            for c in range(8):
                if c >= 1:
                    scalar.wait_ge(s_qmm, 4 * c)
                scalar.activation(sqb[:], xn_sb[:, TOK * c:TOK * (c + 1)],
                                  AF.Square).then_inc(s_sq, 1)
            scalar.wait_ge(s_qmm, 32)
            for q in range(4):
                sss(scalar, scalar.activation(rrow[:, 512 * q:512 * (q + 1)],
                                              pS[q].ap()[0:1, :], AF.Sqrt, scale=1.0 / DM))
            scalar.nop().then_inc(s_rrow, 100)
            scalar.nop().then_inc(s_r1d, 1)
            for k in range(128):
                par = k % 2
                scalar.wait_ge(s_up, k + 1)
                if k >= 2:
                    scalar.wait_ge(s_hst[par], 16 * ((k - 2) // 2 + 1))
                scalar.activation(h_t[par][:], pA[k % 2][:], AF.Gelu).then_inc(s_gel, 1)
                if k >= 2:
                    scalar.wait_ge(s_h2m, k - 1)
                scalar.activation(sqb[:, 512 * par:512 * (par + 1)], h_t[par][:],
                                  AF.Square).then_inc(s_h2, 1)
            scalar.wait_ge(s_h2m, 128)
            scalar.wait_ge(s_rb1, 16)
            for q in range(4):
                sss(scalar, scalar.activation(rrow[:, 512 * q:512 * (q + 1)],
                                              pS[q].ap()[0:1, :], AF.Sqrt, scale=1.0 / FF))
            scalar.nop().then_inc(s_rrow, 100)

        @block.tensor
        def _(tensor):
            tensor.wait_ge(s_abs, 33)
            tensor.wait_ge(s_ones, 1)
            tensor.matmul(pS[0].ap()[0:1, 0:1], onef[:], acc8[:, 32:33], start=True, stop=True)
            tensor.matmul(pS[1].ap()[0:1, 0:1], onef[:], acc8[:, 33:34],
                          start=True, stop=True).then_inc(s_smm, 2)
            for c in range(8):
                tensor.wait_ge(s_sq, c + 1)
                if c == 0:
                    tensor.wait_ge(s_sev, 2)
                for q in range(4):
                    mmq = tensor.matmul(pS[q].ap()[0:1, :], onesb[:], sqb[:, 512 * q:512 * (q + 1)],
                                        start=(c == 0), stop=(c == 7), skip_group_check=True)
                mmq.then_inc(s_qmm, 4)
            for k in range(128):
                fc, tc = divmod(k, 4)
                if fc == 0 and tc == 0:
                    tensor.wait_ge(s_tern, 16)
                    tensor.wait_ge(s_xno, 8)
                if k >= 2:
                    tensor.wait_ge(s_gel, k - 1)
                for c in range(8):
                    mm = tensor.matmul(pA[k % 2][:],
                                       arena[:, 4096 * c + 128 * fc:4096 * c + 128 * (fc + 1)],
                                       xn_sb[:, TOK * c + 512 * tc:TOK * c + 512 * (tc + 1)],
                                       start=(c == 0), stop=(c == 7))
                mm.then_inc(s_up, 1)
                tensor.wait_ge(s_h2, k + 1)
                if k < 4:
                    tensor.wait_ge(s_r1d, 1)
                tensor.matmul(pS[k % 4].ap()[0:1, :], onesb[:],
                              sqb[:, 512 * (k % 2):512 * (k % 2 + 1)],
                              start=(k // 4 == 0), stop=(k // 4 == 31),
                              skip_group_check=True).then_inc(s_h2m, 1)
            tensor.wait_ge(s_tern, 32)
            for tt in range(PTILES):
                tensor.wait_ge(s_hsc, tt + 1)
                if tt >= 2:
                    tensor.wait_ge(s_fout, tt - 1)
                for fc in range(32):
                    lhs = hcol.ap()[:, 128 * fc:128 * (fc + 1)]
                    tensor.matmul(pA[tt % 2][:], lhs, arena[:, 1024 * fc + 0:1024 * fc + 512],
                                  start=(fc == 0), stop=(fc == 31), skip_group_check=True)
                    mmd = tensor.matmul(pB[tt % 2][:], lhs, arena[:, 1024 * fc + 512:1024 * (fc + 1)],
                                        start=(fc == 0), stop=(fc == 31), skip_group_check=True)
                mmd.then_inc(s_dmm, 1)

    return nc




def build_l1fin():
    nc = bass.Bass("TRN2", target_bir_lowering=False)
    d_yf = nc.dram_tensor("yf", [DIN, TOK], BF16, kind="ExternalInput").ap()
    d_yb = nc.dram_tensor("yb", [DIN, TOK], BF16, kind="ExternalInput").ap()
    d_owf = nc.dram_tensor("outwT_f", [DIN, DM], F32, kind="ExternalInput").ap()
    d_owb = nc.dram_tensor("outwT_b", [DIN, DM], F32, kind="ExternalInput").ap()
    d_x1 = nc.dram_tensor("x1", [TOK, DM], F32, kind="ExternalInput").ap()
    d_ffn = nc.dram_tensor("ffn", [TOK, DM], F32, kind="ExternalInput").ap()
    o_out = nc.dram_tensor("out", [TOK, DM], F32, kind="ExternalOutput").ap()

    owsb = [nc.alloc_sbuf_tensor(f"owsb{i}", [128, 16 * DM], BF16) for i in range(2)]
    yfc = [nc.alloc_sbuf_tensor(f"yfc{i}", [128, 16 * 128], BF16) for i in range(2)]
    ybc = [nc.alloc_sbuf_tensor(f"ybc{i}", [128, 16 * 128], BF16) for i in range(2)]
    x1t = [nc.alloc_sbuf_tensor(f"x1t{i}", [128, DM], F32) for i in range(2)]
    fft = [nc.alloc_sbuf_tensor(f"fft{i}", [128, DM], F32) for i in range(2)]
    t1b = nc.alloc_sbuf_tensor("t1b", [128, DM], F32)
    outb = [nc.alloc_sbuf_tensor(f"outb{i}", [128, DM], F32) for i in range(2)]

    pA = [nc.alloc_psum_tensor(f"pA{i}", [128, 512], F32) for i in range(2)]
    pB = [nc.alloc_psum_tensor(f"pB{i}", [128, 512], F32) for i in range(2)]

    s_owl = nc.alloc_semaphore("s_owl")
    s_yfl = [nc.alloc_semaphore(f"s_yfl{i}") for i in range(2)]
    s_ybl = [nc.alloc_semaphore(f"s_ybl{i}") for i in range(2)]
    s_x1l = [nc.alloc_semaphore(f"s_x1l{i}") for i in range(2)]
    s_ffl = [nc.alloc_semaphore(f"s_ffl{i}") for i in range(2)]
    s_mm = nc.alloc_semaphore("s_mm")
    s_fo = nc.alloc_semaphore("s_fo")
    s_ost = [nc.alloc_semaphore(f"s_ost{i}") for i in range(2)]
    s_vself = nc.alloc_semaphore("s_vself")
    vctr = [0]

    def vss(vector, inst):
        vctr[0] += 1
        inst.then_inc(s_vself, 1)
        vector.wait_ge(s_vself, vctr[0])

    with nc.Block() as block:
        @block.sync
        def _(sync):
            for tt in range(PTILES + 2):
                if tt < PTILES:
                    par = tt % 2
                    if tt >= 2:
                        sync.wait_ge(s_mm, tt - 1)
                    sync.dma_start(out=yfc[par].ap().rearrange("p (c n) -> p c n", n=128),
                                   in_=d_yf[:, 128 * tt:128 * (tt + 1)].rearrange(
                                       "(c p) n -> p c n", p=128)).then_inc(s_yfl[par], 16)
                    sync.dma_start(out=ybc[par].ap().rearrange("p (c n) -> p c n", n=128),
                                   in_=d_yb[:, 128 * tt:128 * (tt + 1)].rearrange(
                                       "(c p) n -> p c n", p=128)).then_inc(s_ybl[par], 16)
                    if tt >= 2:
                        sync.wait_ge(s_fo, tt - 1)
                    sync.dma_start(out=x1t[par][:], in_=d_x1[128 * tt:128 * (tt + 1), :]
                                   ).then_inc(s_x1l[par], 16)
                    sync.dma_start(out=fft[par][:], in_=d_ffn[128 * tt:128 * (tt + 1), :]
                                   ).then_inc(s_ffl[par], 16)
                if tt >= 2:
                    jj = tt - 2
                    sync.wait_ge(s_fo, jj + 1)
                    sync.dma_start(out=o_out[128 * jj:128 * (jj + 1), :],
                                   in_=outb[jj % 2][:]).then_inc(s_ost[jj % 2], 16)

        @block.gpsimd
        def _(gpsimd):
            gpsimd.dma_start(out=owsb[0].ap().rearrange("p (c n) -> p c n", n=DM),
                             in_=d_owf.rearrange("(c p) n -> p c n", p=128)).then_inc(s_owl, 16)
            gpsimd.dma_start(out=owsb[1].ap().rearrange("p (c n) -> p c n", n=DM),
                             in_=d_owb.rearrange("(c p) n -> p c n", p=128)).then_inc(s_owl, 16)
            gpsimd.wait_ge(s_ost[0], 16 * (PTILES // 2))
            gpsimd.wait_ge(s_ost[1], 16 * (PTILES // 2))

        @block.tensor
        def _(tensor):
            tensor.wait_ge(s_owl, 32)
            for tt in range(PTILES):
                par = tt % 2
                tensor.wait_ge(s_yfl[par], 16 * (tt // 2 + 1))
                tensor.wait_ge(s_ybl[par], 16 * (tt // 2 + 1))
                if tt >= 2:
                    tensor.wait_ge(s_fo, tt - 1)
                for dd in range(2):
                    ysrc = yfc[par] if dd == 0 else ybc[par]
                    for c in range(16):
                        lhs = ysrc.ap()[:, 128 * c:128 * (c + 1)]
                        first = (dd == 0 and c == 0)
                        last = (dd == 1 and c == 15)
                        tensor.matmul(pA[par][:], lhs, owsb[dd][:, DM * c + 0:DM * c + 512],
                                      start=first, stop=last, skip_group_check=True)
                        mml = tensor.matmul(pB[par][:], lhs, owsb[dd][:, DM * c + 512:DM * (c + 1)],
                                            start=first, stop=last, skip_group_check=True)
                mml.then_inc(s_mm, 1)

        @block.vector
        def _(vector):
            for tt in range(PTILES):
                par = tt % 2
                vector.wait_ge(s_mm, tt + 1)
                vector.wait_ge(s_x1l[par], 16 * (tt // 2 + 1))
                vector.wait_ge(s_ffl[par], 16 * (tt // 2 + 1))
                if tt >= 2:
                    vector.wait_ge(s_ost[par], 16 * ((tt - 2) // 2 + 1))
                vss(vector, vector.tensor_tensor(t1b[:], x1t[par][:], fft[par][:], ALU.add))
                vss(vector, vector.tensor_tensor(outb[par][:, 0:512], t1b[:, 0:512], pA[par][:], ALU.add))
                vss(vector, vector.tensor_tensor(outb[par][:, 512:1024], t1b[:, 512:1024],
                                                 pB[par][:], ALU.add))
                vector.nop().then_inc(s_fo, 1)

        @block.scalar
        def _(scalar):
            scalar.nop()

    return nc



def _rows(j):
    return np.concatenate([np.arange(128 * t, 128 * (t + 1)) for t in zig_tiles(j)])


def kernel(**inputs):
    C = np.ascontiguousarray
    x = C(inputs["L0_attn_x"])
    rows_of = [_rows(j) for j in range(4)]

    nc = build_prep()
    maps = []
    for cid in range(8):
        g, j = divmod(cid, 4)
        maps.append({
            "x": C(x[g][rows_of[j]]),
            "wq": C(inputs["L0_attn_w_q"]), "wk": C(inputs["L0_attn_w_k"]),
            "wv1": C(inputs["L0_attn_w_v1"]), "wv2": C(inputs["L0_attn_w_v2"]),
        })
    r1 = run_bass_kernel_spmd(nc, maps, core_ids=list(range(8))).results

    Kf = [np.zeros((KD, T), dtype=BF) for _ in range(2)]
    Vf = [np.zeros((T, DM), dtype=BF) for _ in range(2)]
    for cid in range(8):
        g, j = divmod(cid, 4)
        Kf[g][:, rows_of[j]] = np.asarray(r1[cid]["k_fm"])
        Vf[g][rows_of[j]] = np.asarray(r1[cid]["v_tm"])

    nc = build_l0att()
    maps = []
    for cid in range(8):
        g, j = divmod(cid, 4)
        maps.append({
            "q_fm": np.asarray(r1[cid]["q_fm"]),
            "Kfull": C(Kf[g]), "Vfull": C(Vf[g]),
            "x": C(x[g][rows_of[j]]),
            "dpos": np.array([[128.0 * t for t in zig_tiles(j)]], np.float32),
        })
    r2 = run_bass_kernel_spmd(nc, maps, core_ids=list(range(8))).results

    nc = build_l0moe()
    maps = []
    for cid in range(8):
        g, j = divmod(cid, 4)
        maps.append({
            "xn_fm": np.asarray(r1[cid]["xn_fm"]),
            "xatt": np.asarray(r2[cid]["xatt"]),
            "router_w": C(inputs["L0_ffn_router_w"]),
            "router_b": C(np.asarray(inputs["L0_ffn_router_b"])[None, :]),
            "we_gate": C(inputs["L0_ffn_we_gate"]),
            "we_up": C(inputs["L0_ffn_we_up"]),
            "we_down": C(inputs["L0_ffn_we_down"]),
        })
    r3 = run_bass_kernel_spmd(nc, maps, core_ids=list(range(8))).results

    nc = build_l1a()
    maps = [{"x1": np.asarray(r3[cid]["x1"])} for cid in range(8)]
    r4 = run_bass_kernel_spmd(nc, maps, core_ids=list(range(8))).results

    xn1f = [np.zeros((DM, T), dtype=BF) for _ in range(2)]
    for cid in range(8):
        g, j = divmod(cid, 4)
        xn1f[g][:, rows_of[j]] = np.asarray(r4[cid]["xn1_fm"])

    fw = np.asarray(inputs["L1_attn_f_in_proj_w"])
    bw = np.asarray(inputs["L1_attn_b_in_proj_w"])
    nc = build_m1()
    maps = []
    for cid in range(8):
        g, j = divmod(cid, 4)
        sl = slice(512 * j, 512 * (j + 1))
        w_in_T = np.concatenate([
            fw[512 * j:512 * (j + 1)].T, fw[2048 + 512 * j:2048 + 512 * (j + 1)].T,
            bw[512 * j:512 * (j + 1)].T, bw[2048 + 512 * j:2048 + 512 * (j + 1)].T], axis=1)
        maps.append({
            "xn1f": C(xn1f[g]),
            "w_in_T": C(w_in_T.astype(np.float32)),
            "conv_w": C(np.concatenate([np.asarray(inputs["L1_attn_f_conv_w"])[sl],
                                        np.asarray(inputs["L1_attn_b_conv_w"])[sl]]).astype(np.float32)),
            "conv_b": C(np.concatenate([np.asarray(inputs["L1_attn_f_conv_b"])[sl],
                                        np.asarray(inputs["L1_attn_b_conv_b"])[sl]])[:, None].astype(np.float32)),
            "xproj_T": C(np.concatenate([np.asarray(inputs["L1_attn_f_x_proj_w"]).T[sl],
                                         np.asarray(inputs["L1_attn_b_x_proj_w"]).T[sl]]).astype(np.float32)),
        })
    r5 = run_bass_kernel_spmd(nc, maps, core_ids=list(range(8))).results

    nc = build_m2()
    maps = []
    for cid in range(8):
        g, j = divmod(cid, 4)
        sl = slice(512 * j, 512 * (j + 1))
        partners = [4 * g + jj for jj in range(4) if jj != j]
        m = {
            "xc": np.asarray(r5[cid]["xc"]), "sz": np.asarray(r5[cid]["sz"]),
            "xdb_a": np.asarray(r5[cid]["xdb"]),
            "dt_wT": C(np.concatenate([np.asarray(inputs["L1_attn_f_dt_proj_w"]).T[:, sl],
                                       np.asarray(inputs["L1_attn_b_dt_proj_w"]).T[:, sl]],
                                      axis=1).astype(np.float32)),
            "dt_b": C(np.concatenate([np.asarray(inputs["L1_attn_f_dt_proj_b"])[sl],
                                      np.asarray(inputs["L1_attn_b_dt_proj_b"])[sl]])[:, None].astype(np.float32)),
            "A_log": C(np.concatenate([np.asarray(inputs["L1_attn_f_A_log"])[sl],
                                       np.asarray(inputs["L1_attn_b_A_log"])[sl]]).astype(np.float32)),
            "Dp": C(np.concatenate([np.asarray(inputs["L1_attn_f_D"])[sl],
                                    np.asarray(inputs["L1_attn_b_D"])[sl]])[:, None].astype(np.float32)),
        }
        for mi, pc in enumerate(partners):
            m[f"xdb_p{mi}"] = np.asarray(r5[pc]["xdb"])
        maps.append(m)
    r6 = run_bass_kernel_spmd(nc, maps, core_ids=list(range(8))).results

    Yf = [np.zeros((DIN, T), dtype=BF) for _ in range(2)]
    Yb = [np.zeros((DIN, T), dtype=BF) for _ in range(2)]
    for cid in range(8):
        g, j = divmod(cid, 4)
        yg = np.asarray(r6[cid]["yg"])
        Yf[g][512 * j:512 * (j + 1)] = yg[0:512]
        Yb[g][512 * j:512 * (j + 1)] = yg[512:1024][:, ::-1]

    nc = build_l1ffn()
    maps = []
    for cid in range(8):
        g, j = divmod(cid, 4)
        maps.append({
            "w_upT": C(np.asarray(inputs["L1_ffn_w_up"]).T.astype(np.float32)),
            "w_downT": C(np.asarray(inputs["L1_ffn_w_down"]).T.astype(np.float32)),
            "xn1_fm": np.asarray(r4[cid]["xn1_fm"]),
        })
    r7 = run_bass_kernel_spmd(nc, maps, core_ids=list(range(8))).results

    nc = build_l1fin()
    maps = []
    for cid in range(8):
        g, j = divmod(cid, 4)
        maps.append({
            "yf": C(Yf[g][:, rows_of[j]]),
            "yb": C(Yb[g][:, rows_of[j]]),
            "outwT_f": C(np.asarray(inputs["L1_attn_f_out_proj_w"]).T.astype(np.float32)),
            "outwT_b": C(np.asarray(inputs["L1_attn_b_out_proj_w"]).T.astype(np.float32)),
            "x1": np.asarray(r3[cid]["x1"]),
            "ffn": np.asarray(r7[cid]["ffn"]),
        })
    r8 = run_bass_kernel_spmd(nc, maps, core_ids=list(range(8))).results

    out = np.zeros((2, T, DM), np.float32)
    for cid in range(8):
        g, j = divmod(cid, 4)
        out[g][rows_of[j]] = np.asarray(r8[cid]["out"])
    return out
